# version 02
# baseline (speedup 1.0000x reference)
import numpy as np
import ml_dtypes

import concourse.bass as bass
import concourse.mybir as mybir
from concourse.bass_utils import run_bass_kernel_spmd

F32 = mybir.dt.float32
BF16 = mybir.dt.bfloat16
I32 = mybir.dt.int32
AF = mybir.ActivationFunctionType
ALU = mybir.AluOpType
AX = mybir.AxisListType

ENGINES = ("sync", "vector", "scalar", "tensor", "gpsimd")
NCORES = 8

D = 1024
DI = 2048
NS = 16
DTR = 64
IF = 3072
QMAX = 32767.0
SEM_MAX = 48000
NSEM = 3


class Prog:

    def __init__(self, nc):
        self.nc = nc
        self.ops = []
        self.last_writer = {}
        self.readers = {}
        self.deps = []
        self.barrier_deps = set()
        self._last_compute = {}
        self._last_dmas = {}

    def barrier(self):
        deps = set()
        for e, i in self._last_compute.items():
            deps.add(i)
        for e, lst in self._last_dmas.items():
            deps.update(lst)
        self.barrier_deps = deps

    def _name(self, b):
        if isinstance(b, str):
            return b
        return b.tensor.name if isinstance(b, bass.AP) else b.name

    def _add(self, engine, fn, reads, writes, inc):
        i = len(self.ops)
        dep = set()
        reads = [self._name(b) for b in reads]
        writes = [self._name(b) for b in writes]
        for b in reads:
            w = self.last_writer.get(b)
            if w is not None:
                dep.add(w)
        for b in writes:
            w = self.last_writer.get(b)
            if w is not None:
                dep.add(w)
            dep.update(self.readers.get(b, ()))
        dep |= self.barrier_deps
        dep.discard(i)
        if inc == 16:
            self._last_dmas.setdefault(engine, [])
            self._last_dmas[engine].append(i)
            if len(self._last_dmas[engine]) > 8:
                self._last_dmas[engine].pop(0)
        else:
            self._last_compute[engine] = i
        self.ops.append(dict(engine=engine, fn=fn, inc=inc))
        self.deps.append(dep)
        for b in reads:
            self.readers.setdefault(b, []).append(i)
        for b in writes:
            self.last_writer[b] = i
            self.readers[b] = []
        return i

    def op(self, engine, fn, reads=(), writes=()):
        return self._add(engine, fn, reads, writes, 1)

    def dma(self, engine, fn, reads=(), writes=()):
        return self._add(engine, fn, reads, writes, 16)

    def emit(self, block, semlists):
        n = len(self.ops)
        slot_of = [0] * n
        val_of = [0] * n
        nslot = {e: len(semlists[e]) for e in ENGINES}
        slot_ct = {e: [0] * nslot[e] for e in ENGINES}
        dma_rr = {e: 0 for e in ENGINES}
        engct = {e: 0 for e in ENGINES}
        for i, o in enumerate(self.ops):
            e = o["engine"]
            if o["inc"] == 16:
                s = dma_rr[e] % nslot[e]
                dma_rr[e] += 1
            else:
                s = (engct[e] // SEM_MAX) % nslot[e]
            engct[e] += o["inc"]
            slot_ct[e][s] += o["inc"]
            slot_of[i] = s
            val_of[i] = slot_ct[e][s]
        slot_ops = {}
        for i, o in enumerate(self.ops):
            slot_ops.setdefault((o["engine"], slot_of[i]), []).append(i)

        import bisect

        def cum_before(key, i):
            lst = slot_ops.get(key)
            if not lst:
                return 0
            k = bisect.bisect_left(lst, i)
            return val_of[lst[k - 1]] if k else 0

        for ename in ENGINES:
            idxs = [i for i, o in enumerate(self.ops) if o["engine"] == ename]
            if not idxs:
                continue

            def make(ename, idxs):
                def section(eng):
                    water = {}
                    for i in idxs:
                        o = self.ops[i]
                        need = {}
                        for j in self.deps[i]:
                            je = self.ops[j]["engine"]
                            key = (je, slot_of[j])
                            if self.ops[j]["inc"] == 16:
                                v = cum_before(key, i)
                            else:
                                v = val_of[j]
                            if need.get(key, 0) < v:
                                need[key] = v
                        for (je, sl), v in sorted(need.items()):
                            if water.get((je, sl), 0) >= v:
                                continue
                            eng.wait_ge(semlists[je][sl], v)
                            water[(je, sl)] = v
                        o["fn"](eng).then_inc(
                            semlists[ename][slot_of[i]], o["inc"])
                return section

            getattr(block, ename)(make(ename, idxs))
        totals = {e: [0] * nslot[e] for e in ENGINES}
        for i, o in enumerate(self.ops):
            e = o["engine"]
            totals[e][slot_of[i]] = max(totals[e][slot_of[i]], val_of[i])
        return totals


class Ctx:
    def __init__(self, nc):
        self.nc = nc
        self.stack = [[]]
        self.uid = 0

    def push(self):
        self.stack.append([])

    def pop(self):
        for g in reversed(self.stack.pop()):
            g.__exit__(None, None, None)

    def _alloc(self, kind, name, shape, dt):
        self.uid += 1
        nm = f"{name}_{self.uid}"
        g = (self.nc.sbuf_tensor(nm, shape, dt) if kind == "s"
             else self.nc.psum_tensor(nm, shape, dt))
        t = g.__enter__()
        self.stack[-1].append(g)
        return t

    def sbuf(self, name, shape, dt=F32):
        return self._alloc("s", name, shape, dt)

    def psum(self, name, shape, dt=F32):
        return self._alloc("p", name, shape, dt)



def kernel(L0_attn_x, L0_attn_W_in, L0_attn_conv_w, L0_attn_conv_b,
           L0_attn_W_x, L0_attn_W_dt, L0_attn_b_dt, L0_attn_A_log,
           L0_attn_D, L0_attn_W_out, L0_ffn_w_gate, L0_ffn_w_up,
           L0_ffn_w_down):
    x = np.ascontiguousarray(np.asarray(L0_attn_x, dtype=np.float32))
    B, T, Dm = x.shape

    def cols(v):
        return np.ascontiguousarray(
            np.asarray(v, np.float32).reshape(DI // 128, 128).T)

    conv_w = np.asarray(L0_attn_conv_w, np.float32).reshape(16, 128, 4)
    conv_w_t = np.ascontiguousarray(conv_w.transpose(1, 0, 2).reshape(128, 64))
    a_log_t = np.ascontiguousarray(
        np.asarray(L0_attn_A_log, np.float32).reshape(16, 128, NS)
        .transpose(1, 0, 2).reshape(128, 16 * NS))

    common = {
        "W_in": np.ascontiguousarray(np.asarray(L0_attn_W_in, np.float32)),
        "W_x": np.ascontiguousarray(np.asarray(L0_attn_W_x, np.float32)),
        "W_dt": np.ascontiguousarray(np.asarray(L0_attn_W_dt, np.float32)),
        "W_out": np.ascontiguousarray(np.asarray(L0_attn_W_out, np.float32)),
        "w_gate": np.ascontiguousarray(np.asarray(L0_ffn_w_gate, np.float32)),
        "w_up": np.ascontiguousarray(np.asarray(L0_ffn_w_up, np.float32)),
        "w_down": np.ascontiguousarray(np.asarray(L0_ffn_w_down, np.float32)),
        "conv_wt": conv_w_t,
        "conv_b": cols(L0_attn_conv_b),
        "b_dt": cols(L0_attn_b_dt),
        "Dvec": cols(L0_attn_D),
        "A_log_t": a_log_t,
        "ident": np.eye(128, dtype=ml_dtypes.bfloat16),
        "ones_row": np.ones((1, 128), dtype=np.float32),
        "sel16": _sel16(),
    }

    if FUSED:
        nc = build_program(T, ncores=B)
        in_maps = []
        for c in range(B):
            m = dict(common)
            m["x_in"] = np.ascontiguousarray(x[c])
            in_maps.append(m)
        res = run_bass_kernel_spmd(nc, in_maps, core_ids=list(range(B)))
        out = np.stack([np.asarray(r["x_out"]) for r in res.results], axis=0)
        return out.astype(np.float32)
    return _kernel_staged(x, common, T, B)


_PROG_CACHE = {}


def _get_prog(T, B, stage):
    key = (T, B, stage)
    if key not in _PROG_CACHE:
        _PROG_CACHE[key] = build_program(T, ncores=B, stage=stage)
    return _PROG_CACHE[key]


def _kernel_staged(x, common, T, B):
    cores = list(range(B))

    def run(stage, per_core):
        nc = _get_prog(T, B, stage)
        return run_bass_kernel_spmd(nc, per_core, core_ids=cores).results

    base = {k: common[k] for k in ("ident", "ones_row")}

    def gather_mx(res):
        return np.ascontiguousarray(
            np.concatenate([np.asarray(r["mxout"]) for r in res], axis=0))

    def ffn(x_host):
        r = run("max", [dict(base, x_in=x_host[c]) for c in cores])
        g1 = gather_mx(r)
        r = run("A", [dict(base, x_in=x_host[c], g1_in=g1,
                           w_up=common["w_up"], w_gate=common["w_gate"])
                      for c in cores])
        g23 = gather_mx(r)
        sp_up = [np.asarray(r[c]["sp_up"]) for c in cores]
        sp_gate = [np.asarray(r[c]["sp_gate"]) for c in cores]
        r = run("B", [dict(base, sp_gate=sp_gate[c], g23_in=g23)
                      for c in cores])
        g4 = gather_mx(r)
        sp_P = [np.asarray(r[c]["sp_P"]) for c in cores]
        r = run("C", [dict(base, sp_up=sp_up[c], sp_P=sp_P[c], g23_in=g23,
                           g4_in=g4) for c in cores])
        g5 = gather_mx(r)
        sp_o = [np.asarray(r[c]["sp_o"]) for c in cores]
        r = run("D", [dict(base, x_in=x_host[c], sp_o=sp_o[c],
                           w_down=common["w_down"], g23_in=g23, g4_in=g4,
                           g5_in=g5) for c in cores])
        return [np.asarray(r[c]["x_out"]) for c in cores]

    x_host = [np.ascontiguousarray(x[c]) for c in range(B)]
    x2 = ffn(x_host)
    mkeys = ("W_in", "W_x", "W_dt", "W_out", "conv_wt", "conv_b", "b_dt",
             "Dvec", "A_log_t", "sel16")
    r = run("mamba", [dict(base, x_in=x2[c],
                           **{k: common[k] for k in mkeys}) for c in cores])
    x3 = [np.asarray(r[c]["x_out"]) for c in cores]
    out = ffn(x3)
    return np.stack(out, axis=0).astype(np.float32)


FUSED = False


def _sel16():
    sel = np.zeros((16, 16 * 128), dtype=ml_dtypes.bfloat16)
    for n in range(16):
        sel[n, n * 128:(n + 1) * 128] = 1
    return sel



STAGE_IO = {
    "max":   ({"x_in"}, {"mxout"}),
    "A":     ({"x_in", "g1_in", "w_up", "w_gate"},
              {"sp_up", "sp_gate", "mxout"}),
    "B":     ({"sp_gate", "g23_in"}, {"sp_P", "mxout"}),
    "C":     ({"sp_up", "sp_P", "g23_in", "g4_in"}, {"sp_o", "mxout"}),
    "D":     ({"x_in", "sp_o", "w_down", "g23_in", "g4_in", "g5_in"},
              {"x_out"}),
    "mamba": ({"x_in", "W_in", "W_x", "W_dt", "W_out", "conv_wt", "conv_b",
               "b_dt", "Dvec", "A_log_t", "sel16"}, {"x_out"}),
}


def build_program(T, mixer="mamba", stop_after=None, ncores=8, trunc=None,
                  stage=None):
    nc = bass.Bass("TRN2", target_bir_lowering=False)
    ctx = Ctx(nc)
    p = Prog(nc)
    NT = T // 128
    NCH = T // 512

    dram = {}
    for nm, shape, dt, kind in [
        ("x_in", [T, D], F32, "ExternalInput"),
        ("x_out", [T, D], F32, "ExternalOutput"),
        ("W_in", [D, 2 * DI], F32, "ExternalInput"),
        ("W_x", [DI, DTR + 2 * NS], F32, "ExternalInput"),
        ("W_dt", [DTR, DI], F32, "ExternalInput"),
        ("W_out", [DI, D], F32, "ExternalInput"),
        ("w_gate", [IF, D], F32, "ExternalInput"),
        ("w_up", [IF, D], F32, "ExternalInput"),
        ("w_down", [D, IF], F32, "ExternalInput"),
        ("conv_wt", [128, 64], F32, "ExternalInput"),
        ("conv_b", [128, 16], F32, "ExternalInput"),
        ("b_dt", [128, 16], F32, "ExternalInput"),
        ("Dvec", [128, 16], F32, "ExternalInput"),
        ("A_log_t", [128, 16 * NS], F32, "ExternalInput"),
        ("ident", [128, 128], BF16, "ExternalInput"),
        ("ones_row", [1, 128], F32, "ExternalInput"),
        ("sel16", [16, 16 * 128], BF16, "ExternalInput"),
        ("xres", [T, D], F32, "Internal"),
        ("sp_up", [IF, T], F32, "Internal"),
        ("sp_gate", [IF, T], F32, "Internal"),
        ("sp_P", [IF, T], F32, "Internal"),
        ("sp_o", [IF, T], F32, "Internal"),
        ("sp_z", [DI, T], F32, "Internal"),
        ("wupT", [D, IF], BF16, "Internal"),
        ("wgateT", [D, IF], BF16, "Internal"),
        ("wdownT", [IF, D], BF16, "Internal"),
        ("winb", [D, 2 * DI], BF16, "Internal"),
        ("omixd", [DI, T], BF16, "Internal"),
        ("woutb", [DI, D], BF16, "Internal"),
        ("cc_in", [1, 2], F32, "Internal"),
        ("cc_out", [ncores, 2], F32, "Internal"),
        ("mxout", [1, 2], F32, "Internal"),
        ("g1_in", [ncores, 2], F32, "Internal"),
        ("g23_in", [ncores, 2], F32, "Internal"),
        ("g4_in", [ncores, 2], F32, "Internal"),
        ("g5_in", [ncores, 2], F32, "Internal"),
    ]:
        if stage is not None:
            ins_, outs_ = STAGE_IO[stage]
            base = {"ident", "ones_row"}
            if nm in ins_ or nm in base:
                kind = "ExternalInput"
            elif nm in outs_:
                kind = "ExternalOutput"
            else:
                kind = "Internal"
        dram[nm] = nc.dram_tensor(nm, shape, dt, kind=kind)

    ident_b = ctx.sbuf("identb", [128, 128], BF16)
    ones_f = ctx.sbuf("onesf", [1, 128], F32)
    p.dma("sync", lambda e: e.dma_start(out=ident_b[:], in_=dram["ident"].ap()),
          writes=[ident_b])
    p.dma("sync", lambda e: e.dma_start(out=ones_f[:], in_=dram["ones_row"].ap()),
          writes=[ones_f])
    stat_m = ctx.sbuf("stat_m", [128, NT])
    stat_r = ctx.sbuf("stat_r", [128, NT])
    eps_b = ctx.sbuf("eps_b", [128, 1])
    p.op("vector", lambda e: e.memset(eps_b[:], 1e-5), writes=[eps_b])

    H = dict(ctx=ctx, p=p, nc=nc, dram=dram, ident_b=ident_b, ones_f=ones_f,
             stat_m=stat_m, stat_r=stat_r, eps_b=eps_b, T=T, NT=NT, NCH=NCH,
             ncores=ncores, stage=stage)

    if stage is not None:
        if stage == "max":
            ffn_call(H, 0, x_src=dram["x_in"], x_dst=None, stages={"max"})
        elif stage in ("A", "B", "C", "D"):
            ctx.push()
            prep_ffn_weights(H, which={
                "A": ("wupT", "wgateT"), "B": (), "C": (),
                "D": ("wdownT",)}[stage])
            ctx.pop()
            p.barrier()
            ctx.push()
            ffn_call(H, 0, x_src=dram["x_in"], x_dst=dram["x_out"],
                     stages={stage})
            ctx.pop()
        elif stage == "mamba":
            ctx.push()
            mamba(H, x_src=dram["x_in"], x_dst=dram["x_out"])
            ctx.pop()
        return _finish(nc, ctx, p, trunc)

    ctx.push()
    prep_ffn_weights(H)
    ctx.pop()
    p.barrier()

    ctx.push()
    ffn_call(H, 0, x_src=dram["x_in"], x_dst=dram["xres"])
    ctx.pop()
    p.barrier()

    if stop_after != "ffn1":
        ctx.push()
        if mixer == "mamba":
            mamba(H, x_src=dram["xres"], x_dst=dram["xres"])
        else:
            mixer_copy(H, x_src=dram["xres"], x_dst=dram["xres"])
        ctx.pop()
        p.barrier()

        ctx.push()
        ffn_call(H, 1, x_src=dram["xres"], x_dst=dram["x_out"])
        ctx.pop()
    else:
        p.barrier()
        mixer_copy(H, x_src=dram["xres"], x_dst=dram["x_out"])

    return _finish(nc, ctx, p, trunc)


def _finish(nc, ctx, p, trunc=None):
    if trunc is not None:
        p.ops = p.ops[:trunc]
        p.deps = p.deps[:trunc]
    NSLOTS = {"sync": 6, "scalar": 6, "gpsimd": 4, "vector": 3, "tensor": 3}
    with nc.Block() as block:
        semctx = []
        semlists = {}
        for e in ENGINES:
            semlists[e] = []
            for k in range(NSLOTS[e]):
                g = nc.semaphore(f"s_{e}{k}")
                semlists[e].append(g.__enter__())
                semctx.append(g)
        totals = p.emit(block, semlists)

        @block.gpsimd
        def _(eng):
            for e in ENGINES:
                for k in range(NSLOTS[e]):
                    if totals[e][k]:
                        eng.wait_ge(semlists[e][k], totals[e][k])

        for g in reversed(semctx):
            g.__exit__(None, None, None)

    while len(ctx.stack) > 1:
        ctx.pop()
    for g in reversed(ctx.stack[0]):
        g.__exit__(None, None, None)
    ctx.stack[0] = []
    return nc



def round_half_up(H, y, i32, vf, gg, pre_fn, pre_reads):
    p = H["p"]
    p.op("vector", lambda e: pre_fn(e, y), reads=pre_reads, writes=[y])
    p.op("vector", lambda e: e.tensor_copy(i32[:], y[:]), reads=[y], writes=[i32])
    p.op("vector", lambda e: e.tensor_copy(vf[:], i32[:]), reads=[i32], writes=[vf])
    p.op("vector", lambda e: e.tensor_tensor(gg[:], vf[:], y[:], ALU.is_gt),
         reads=[vf, y], writes=[gg])
    p.op("vector", lambda e: e.tensor_tensor(vf[:], vf[:], gg[:], ALU.subtract),
         reads=[vf, gg], writes=[vf])


def scalar_bcast(H, tag, src_ap, src_buf, pbc, dst):
    p = H["p"]
    p.op("tensor", lambda e: e.matmul(pbc[:, 0:1], H["ones_f"][:], src_ap,
                                      start=True, stop=True),
         reads=[src_buf, H["ones_f"]], writes=[pbc])
    p.op("scalar", lambda e: e.activation(dst[:], pbc[:, 0:1], AF.Copy,
                                          bias=0.0, scale=1.0),
         reads=[pbc], writes=[dst])


def reduce_acc(H, tag, acc, nvals):
    ctx, p = H["ctx"], H["p"]
    row = ctx.sbuf(tag + "_row", [1, 256])
    mx = ctx.sbuf(tag + "_mx", [1, 2])
    for v in range(nvals):
        p.dma("sync", lambda e, v=v: e.dma_start(
            out=row[0:1, v * 128:(v + 1) * 128], in_=acc[:, v:v + 1]),
            reads=[acc], writes=[row])
    p.op("vector", lambda e: e.memset(mx[:], 0.0), writes=[mx])
    for v in range(nvals):
        p.op("vector", lambda e, v=v: e.tensor_reduce(
            mx[0:1, v:v + 1], row[0:1, v * 128:(v + 1) * 128], AX.X, ALU.max),
            reads=[row], writes=[mx])
    return mx


def export_mx(H, mx):
    p = H["p"]
    p.dma("sync", lambda e: e.dma_start(out=H["dram"]["mxout"].ap(), in_=mx[:]),
          reads=[mx], writes=["mxout"])


def consume_gmax(H, tag, name):
    ctx, p, dram = H["ctx"], H["p"], H["dram"]
    nr = H["ncores"]
    gt = ctx.sbuf(tag + "_g", [1, 16])
    p.dma("sync", lambda e: e.dma_start(
        out=gt[0:1, 0:2 * nr], in_=dram[name].ap()),
        reads=[name], writes=[gt])
    gm = ctx.sbuf(tag + "_gm", [1, 2])
    p.op("vector", lambda e: e.memset(gm[:], 0.0), writes=[gm])
    for v in range(2):
        p.op("vector", lambda e, v=v: e.tensor_reduce(
            gm[0:1, v:v + 1],
            bass.AP(gt, v, [[16, 1], [2, nr]]),
            AX.X, ALU.max),
            reads=[gt], writes=[gm])
    return gm


def global_max(H, tag, acc, nvals, pbc):
    ctx, p, dram = H["ctx"], H["p"], H["dram"]
    mx = reduce_acc(H, tag, acc, nvals)
    p.dma("sync", lambda e: e.dma_start(out=dram["cc_in"].ap(), in_=mx[:]),
          reads=[mx], writes=["cc_in"])
    p.dma("gpsimd", lambda e: e.collective_compute(
        "AllGather", ALU.bypass, [list(range(H["ncores"]))],
        [dram["cc_in"].ap()], [dram["cc_out"].ap()]),
        reads=["cc_in"], writes=["cc_out"])
    gt = ctx.sbuf(tag + "_g", [1, 16])
    nr0 = H["ncores"]
    p.dma("sync", lambda e: e.dma_start(
        out=gt[0:1, 0:2 * nr0], in_=dram["cc_out"].ap()[0:nr0, :]),
        reads=["cc_out"], writes=[gt])
    gm = ctx.sbuf(tag + "_gm", [1, 2])
    p.op("vector", lambda e: e.memset(gm[:], 0.0), writes=[gm])
    nr = H["ncores"]
    for v in range(nvals):
        p.op("vector", lambda e, v=v: e.tensor_reduce(
            gm[0:1, v:v + 1],
            bass.AP(gt, v, [[16, 1], [2, nr]]),
            AX.X, ALU.max),
            reads=[gt], writes=[gm])
    return gm


def quant_scales(H, tag, gm, col, pbc):
    ctx, p = H["ctx"], H["p"]
    sv = ctx.sbuf(tag + "_sv", [1, 1])
    rv = ctx.sbuf(tag + "_rv", [1, 1])
    p.op("vector", lambda e: e.tensor_scalar(
        sv[:], gm[0:1, col:col + 1], 1.0 / QMAX, 1e-12, ALU.mult, ALU.max),
        reads=[gm], writes=[sv])
    p.op("vector", lambda e: e.reciprocal(rv[:], sv[:]), reads=[sv], writes=[rv])
    sbc = ctx.sbuf(tag + "_sb", [128, 1])
    rbc = ctx.sbuf(tag + "_rb", [128, 1])
    scalar_bcast(H, tag + "s", sv[0:1, 0:1], sv, pbc, sbc)
    scalar_bcast(H, tag + "r", rv[0:1, 0:1], rv, pbc, rbc)
    return sbc, rbc, sv



def prep_ffn_weights(H, which=("wupT", "wgateT", "wdownT")):
    ctx, p, dram = H["ctx"], H["p"], H["dram"]
    ident_b = H["ident_b"]
    IMAX = IF
    wt = ctx.sbuf("pw_w", [128, IMAX])
    bm = ctx.sbuf("pw_bm", [128, IMAX // 32])
    rr = ctx.sbuf("pw_r", [128, IMAX // 32])
    y = ctx.sbuf("pw_y", [128, IMAX])
    i32 = ctx.sbuf("pw_i", [128, IMAX], I32)
    gg = ctx.sbuf("pw_g", [128, IMAX])
    q = ctx.sbuf("pw_q", [128, IMAX])
    wb = ctx.sbuf("pw_b", [128, IMAX], BF16)
    tp = [ctx.psum("pw_tp", [128, 128], BF16) for _ in range(2)]
    tps = [ctx.sbuf("pw_tps", [128, 128], BF16) for _ in range(2)]

    for (srcn, dstn, O, I) in (("w_up", "wupT", IF, D),
                               ("w_gate", "wgateT", IF, D),
                               ("w_down", "wdownT", D, IF)):
        if dstn not in which:
            continue
        src, dst = dram[srcn], dram[dstn]
        nb = I // 32
        for it in range(O // 128):
            p.dma("sync", lambda e, it=it, src=src, I=I: e.dma_start(
                out=wt[:, 0:I], in_=src.ap()[it * 128:(it + 1) * 128, :]),
                writes=[wt])
            p.op("vector", lambda e, I=I, nb=nb: e.tensor_reduce(
                bm[:, 0:nb], bass.AP(wt, 0, [[IMAX, 128], [32, nb], [1, 32]]),
                AX.X, ALU.max, apply_absolute_value=True),
                reads=[wt], writes=[bm])
            p.op("vector", lambda e, nb=nb: e.tensor_scalar(
                bm[:, 0:nb], bm[:, 0:nb], 1.0 / 7.0, 1e-12, ALU.mult, ALU.max),
                reads=[bm], writes=[bm])
            p.op("vector", lambda e, nb=nb: e.reciprocal(rr[:, 0:nb], bm[:, 0:nb]),
                 reads=[bm], writes=[rr])
            p.op("vector", lambda e, I=I, nb=nb: e.tensor_tensor(
                y[:, 0:I], wt[:, 0:I],
                bass.AP(rr, 0, [[IMAX // 32, 128], [1, nb], [0, 32]]),
                ALU.mult), reads=[wt, rr], writes=[y])
            p.op("vector", lambda e, I=I: e.tensor_scalar(
                y[:, 0:I], y[:, 0:I], 0.5, None, ALU.add),
                reads=[y], writes=[y])
            p.op("vector", lambda e, I=I: e.tensor_copy(i32[:, 0:I], y[:, 0:I]),
                 reads=[y], writes=[i32])
            p.op("vector", lambda e, I=I: e.tensor_copy(q[:, 0:I], i32[:, 0:I]),
                 reads=[i32], writes=[q])
            p.op("vector", lambda e, I=I: e.tensor_tensor(
                gg[:, 0:I], q[:, 0:I], y[:, 0:I], ALU.is_gt),
                reads=[q, y], writes=[gg])
            p.op("vector", lambda e, I=I: e.tensor_tensor(
                q[:, 0:I], q[:, 0:I], gg[:, 0:I], ALU.subtract),
                reads=[q, gg], writes=[q])
            p.op("vector", lambda e, I=I, nb=nb: e.tensor_tensor(
                wb[:, 0:I], q[:, 0:I],
                bass.AP(bm, 0, [[IMAX // 32, 128], [1, nb], [0, 32]]),
                ALU.mult), reads=[q, bm], writes=[wb])
            for jb in range(I // 128):
                tpx = tp[jb % 2]
                tsx = tps[jb % 2]
                p.op("tensor", lambda e, jb=jb, tpx=tpx: e.transpose(
                    tpx[:], wb[:, jb * 128:(jb + 1) * 128], ident_b[:]),
                    reads=[wb, ident_b], writes=[tpx])
                p.op("scalar", lambda e, tpx=tpx, tsx=tsx: e.activation(
                    tsx[:], tpx[:], AF.Copy, bias=0.0, scale=1.0),
                    reads=[tpx], writes=[tsx])
                p.dma("scalar", lambda e, tsx=tsx, dst=dst, it=it, jb=jb:
                      e.dma_start(out=dst.ap()[jb * 128:(jb + 1) * 128,
                                               it * 128:(it + 1) * 128],
                                  in_=tsx[:]),
                      reads=[tsx], writes=[f"{dstn}_{it}_{jb}"])



def ffn_call(H, call_id, x_src, x_dst, stages=("max", "A", "B", "C", "D")):
    ctx, p, dram = H["ctx"], H["p"], H["dram"]
    T, NT, NCH = H["T"], H["NT"], H["NCH"]
    stat_m, stat_r = H["stat_m"], H["stat_r"]
    ident_b = H["ident_b"]
    fused = H.get("stage") is None
    cid = f"f{call_id}"
    NIT = IF // 128
    sp_up, sp_gate = dram["sp_up"], dram["sp_gate"]
    sp_P, sp_o = dram["sp_P"], dram["sp_o"]

    def xsrc_r(tb):
        return f"xdst_{x_src.name}_{tb}"

    mmp = [ctx.psum(cid + "_mm", [128, 512]) for _ in range(2)]
    dps = [ctx.psum(cid + "_dp", [128, 512]) for _ in range(2)]
    tpp = [ctx.psum(cid + "_tp", [128, 128], BF16) for _ in range(2)]
    pbc = ctx.psum(cid + "_bc", [128, 8])

    xt = [ctx.sbuf(cid + "_mxx", [128, D]) for _ in range(2)]
    bn6 = ctx.sbuf(cid + "_bn6", [128, 12])
    bn2 = ctx.sbuf(cid + "_bn2", [128, 2])
    rmx = ctx.sbuf(cid + "_rmx", [128, 1])
    rmn = ctx.sbuf(cid + "_rmn", [128, 1])
    acc1 = ctx.sbuf(cid + "_acc1", [128, 2])
    vtmp = ctx.sbuf(cid + "_vt", [128, 1])
    if "max" in stages:
        p.op("vector", lambda e: e.memset(acc1[:], 0.0), writes=[acc1])
    for tb in (range(NT) if "max" in stages else ()):
        t = xt[tb % 2]
        p.dma("sync", lambda e, tb=tb, t=t: e.dma_start(
            out=t[:], in_=x_src.ap()[tb * 128:(tb + 1) * 128, :]),
            reads=[xsrc_r(tb)], writes=[t])
        p.op("vector", lambda e, t=t: e.bn_stats(bn6[:, 0:6], t[:, 0:512]),
             reads=[t], writes=[bn6])
        p.op("vector", lambda e, t=t: e.bn_stats(bn6[:, 6:12], t[:, 512:1024]),
             reads=[t], writes=[bn6])
        p.op("vector", lambda e: e.bn_aggr(bn2[:], bn6[:]),
             reads=[bn6], writes=[bn2])
        p.op("vector", lambda e, tb=tb: e.tensor_copy(
            stat_m[:, tb:tb + 1], bn2[:, 0:1]), reads=[bn2], writes=[stat_m])
        p.op("scalar", lambda e: e.activation(vtmp[:], bn2[:, 1:2], AF.Ln,
                                              bias=H["eps_b"][:], scale=1.0),
             reads=[bn2, H["eps_b"]], writes=[vtmp])
        p.op("scalar", lambda e, tb=tb: e.activation(
            stat_r[:, tb:tb + 1], vtmp[:], AF.Exp, bias=0.0, scale=-0.5),
            reads=[vtmp], writes=[stat_r])
        p.op("vector", lambda e, t=t: e.tensor_reduce(
            rmx[:], t[:], AX.X, ALU.max), reads=[t], writes=[rmx])
        p.op("vector", lambda e, t=t: e.tensor_reduce(
            rmn[:], t[:], AX.X, ALU.max, negate=True),
            reads=[t], writes=[rmn])
        p.op("vector", lambda e, tb=tb: e.tensor_tensor(
            rmx[:], rmx[:], stat_m[:, tb:tb + 1], ALU.subtract),
            reads=[rmx, stat_m], writes=[rmx])
        p.op("vector", lambda e, tb=tb: e.tensor_tensor(
            rmn[:], rmn[:], stat_m[:, tb:tb + 1], ALU.add),
            reads=[rmn, stat_m], writes=[rmn])
        p.op("vector", lambda e: e.tensor_tensor(
            rmx[:], rmx[:], rmn[:], ALU.max), reads=[rmx, rmn], writes=[rmx])
        p.op("vector", lambda e, tb=tb: e.tensor_tensor(
            rmx[:], rmx[:], stat_r[:, tb:tb + 1], ALU.mult),
            reads=[rmx, stat_r], writes=[rmx])
        p.op("vector", lambda e: e.tensor_tensor(
            acc1[:, 0:1], acc1[:, 0:1], rmx[:], ALU.max),
            reads=[acc1, rmx], writes=[acc1])

    if "max" in stages and not fused:
        export_mx(H, reduce_acc(H, cid + "g1", acc1, 1))
    if "A" in stages:
        if fused:
            gm1 = global_max(H, cid + "g1", acc1, 1, pbc)
        else:
            gm1 = consume_gmax(H, cid + "g1", "g1_in")
        s1, r1, _ = quant_scales(H, cid + "q1", gm1, 0, pbc)

    amax2 = ctx.sbuf(cid + "_am2", [128, 2])
    rtmp = ctx.sbuf(cid + "_rt", [128, 1])
    ctx.push()
    doA = "A" in stages
    wup_s = [ctx.sbuf(cid + "_wu", [128, IF], BF16) for _ in range(8)]
    wgate_s = [ctx.sbuf(cid + "_wg", [128, IF], BF16) for _ in range(8)]
    for kb in (range(8) if doA else ()):
        p.dma("sync", lambda e, kb=kb: e.dma_start(
            out=wup_s[kb][:], in_=dram["wupT"].ap()[kb * 128:(kb + 1) * 128, :]),
            reads=[f"wupT_{mi}_{kb}" for mi in range(NIT)], writes=[wup_s[kb]])
        p.dma("sync", lambda e, kb=kb: e.dma_start(
            out=wgate_s[kb][:],
            in_=dram["wgateT"].ap()[kb * 128:(kb + 1) * 128, :]),
            reads=[f"wgateT_{mi}_{kb}" for mi in range(NIT)],
            writes=[wgate_s[kb]])

    xqT = [ctx.sbuf(cid + "_xqT", [128, 512], BF16) for _ in range(8)]
    ya = ctx.sbuf(cid + "_Ay", [128, D])
    ia = ctx.sbuf(cid + "_Ai", [128, D], I32)
    ga = ctx.sbuf(cid + "_Ag", [128, D])
    ixf = ctx.sbuf(cid + "_ixf", [128, D])
    ixb = ctx.sbuf(cid + "_ixb", [128, D], BF16)
    kvec = ctx.sbuf(cid + "_kv", [128, 1])
    bvec = ctx.sbuf(cid + "_bv", [128, 1])
    ug = [ctx.sbuf(cid + "_ug", [128, 512]) for _ in range(4)]
    if doA:
        p.op("vector", lambda e: e.memset(amax2[:], 0.0), writes=[amax2])

    for ch in (range(NCH) if doA else ()):
        for tb4 in range(4):
            tb = ch * 4 + tb4
            t = xt[tb % 2]
            p.dma("sync", lambda e, tb=tb, t=t: e.dma_start(
                out=t[:], in_=x_src.ap()[tb * 128:(tb + 1) * 128, :]),
                reads=[xsrc_r(tb)], writes=[t])
            p.op("vector", lambda e, t=t: e.bn_stats(bn6[:, 0:6], t[:, 0:512]),
                 reads=[t], writes=[bn6])
            p.op("vector", lambda e, t=t: e.bn_stats(bn6[:, 6:12],
                                                     t[:, 512:1024]),
                 reads=[t], writes=[bn6])
            p.op("vector", lambda e: e.bn_aggr(bn2[:], bn6[:]),
                 reads=[bn6], writes=[bn2])
            p.op("scalar", lambda e: e.activation(
                vtmp[:], bn2[:, 1:2], AF.Ln, bias=H["eps_b"][:], scale=1.0),
                reads=[bn2, H["eps_b"]], writes=[vtmp])
            p.op("scalar", lambda e: e.activation(
                rmx[:], vtmp[:], AF.Exp, bias=0.0, scale=-0.5),
                reads=[vtmp], writes=[rmx])
            p.op("vector", lambda e: e.tensor_tensor(
                kvec[:], rmx[:], r1[:], ALU.mult),
                reads=[rmx, r1], writes=[kvec])
            p.op("vector", lambda e: e.tensor_tensor(
                bvec[:], bn2[:, 0:1], kvec[:], ALU.mult),
                reads=[bn2, kvec], writes=[bvec])
            p.op("vector", lambda e: e.tensor_scalar(
                bvec[:], bvec[:], -1.0, 0.5, ALU.mult, ALU.add),
                reads=[bvec], writes=[bvec])

            def pre(e, yy, t=t):
                return e.tensor_scalar(yy[:], t[:], kvec[:], bvec[:],
                                       ALU.mult, ALU.add)

            round_half_up(H, ya, ia, ixf, ga, pre, [t, kvec, bvec])
            p.op("vector", lambda e: e.tensor_copy(ixb[:], ixf[:]),
                 reads=[ixf], writes=[ixb])
            for db in range(8):
                tpx = tpp[db % 2]
                p.op("tensor", lambda e, db=db, tpx=tpx: e.transpose(
                    tpx[:], ixb[:, db * 128:(db + 1) * 128], ident_b[:]),
                    reads=[ixb, ident_b], writes=[tpx])
                p.op("scalar", lambda e, db=db, tb4=tb4, tpx=tpx: e.activation(
                    xqT[db][:, tb4 * 128:(tb4 + 1) * 128], tpx[:], AF.Copy,
                    bias=0.0, scale=1.0),
                    reads=[tpx], writes=[xqT[db]])
        for wi, (ws, spd) in enumerate(((wup_s, sp_up), (wgate_s, sp_gate))):
            for mi in range(NIT):
                pmm = mmp[mi % 2]
                for kb in range(8):
                    p.op("tensor", lambda e, wi=wi, mi=mi, kb=kb, pmm=pmm,
                         ws=ws: e.matmul(
                             pmm[:], ws[kb][:, mi * 128:(mi + 1) * 128],
                             xqT[kb][:], start=(kb == 0), stop=(kb == 7)),
                         reads=[ws[kb], xqT[kb]], writes=[pmm])
                ut = ug[(wi * 2 + mi) % 4]
                p.op("scalar", lambda e, pmm=pmm, ut=ut: e.activation(
                    ut[:], pmm[:], AF.Copy, bias=0.0, scale=s1[:]),
                    reads=[pmm, s1], writes=[ut])
                p.op("vector", lambda e, ut=ut: e.tensor_reduce(
                    rtmp[:], ut[:], AX.X, ALU.max, apply_absolute_value=True),
                    reads=[ut], writes=[rtmp])
                p.op("vector", lambda e, wi=wi: e.tensor_tensor(
                    amax2[:, wi:wi + 1], amax2[:, wi:wi + 1], rtmp[:],
                    ALU.max), reads=[amax2, rtmp], writes=[amax2])
                p.dma("sync", lambda e, spd=spd, mi=mi, ch=ch, ut=ut:
                      e.dma_start(out=spd.ap()[mi * 128:(mi + 1) * 128,
                                               ch * 512:(ch + 1) * 512],
                                  in_=ut[:]),
                      reads=[ut], writes=[f"{spd.name}_{ch}_{mi}"])

    ctx.pop()
    if doA and not fused:
        export_mx(H, reduce_acc(H, cid + "g23", amax2, 2))
    doB = "B" in stages
    doC = "C" in stages
    doD = "D" in stages
    if fused:
        gm23 = global_max(H, cid + "g23", amax2, 2, pbc)
    elif doB or doC or doD:
        gm23 = consume_gmax(H, cid + "g23", "g23_in")
    if fused or doC or doD:
        s2, r2, s2v = quant_scales(H, cid + "q2", gm23, 0, pbc)
    if fused or doB or doC or doD:
        s3, r3, s3v = quant_scales(H, cid + "q3", gm23, 1, pbc)

    gtl = [ctx.sbuf(cid + "_Bg", [128, 512]) for _ in range(2)]
    yb = ctx.sbuf(cid + "_By", [128, 512])
    ib = ctx.sbuf(cid + "_Bi", [128, 512], I32)
    gb = ctx.sbuf(cid + "_Bgg", [128, 512])
    i3f = ctx.sbuf(cid + "_i3f", [128, 512])
    sig = ctx.sbuf(cid + "_sig", [128, 512])
    jf = ctx.sbuf(cid + "_jf", [128, 512])
    ppt = ctx.sbuf(cid + "_pp", [128, 512])
    amax4 = ctx.sbuf(cid + "_am4", [128, 2])
    if doB:
        p.op("vector", lambda e: e.memset(amax4[:], 0.0), writes=[amax4])

    for ch in (range(NCH) if doB else ()):
        for mi in range(NIT):
            gt = gtl[mi % 2]
            p.dma("sync", lambda e, mi=mi, ch=ch, gt=gt: e.dma_start(
                out=gt[:], in_=sp_gate.ap()[mi * 128:(mi + 1) * 128,
                                            ch * 512:(ch + 1) * 512]),
                reads=[f"sp_gate_{ch}_{mi}"], writes=[gt])

            def preb(e, yy, gt=gt):
                return e.tensor_scalar(yy[:], gt[:], r3[:], 0.5,
                                       ALU.mult, ALU.add)

            round_half_up(H, yb, ib, i3f, gb, preb, [gt, r3])
            p.op("scalar", lambda e: e.activation(
                sig[:], i3f[:], AF.Sigmoid, bias=0.0, scale=s3[:]),
                reads=[i3f, s3], writes=[sig])

            def prej(e, yy):
                return e.tensor_scalar(yy[:], sig[:], 65536.0, 0.5,
                                       ALU.mult, ALU.add)

            round_half_up(H, yb, ib, jf, gb, prej, [sig])
            p.op("vector", lambda e: e.tensor_scalar(
                jf[:], jf[:], 65535.0, None, ALU.min), reads=[jf], writes=[jf])
            p.op("vector", lambda e: e.tensor_tensor(
                ppt[:], i3f[:], jf[:], ALU.mult), reads=[i3f, jf], writes=[ppt])
            p.op("vector", lambda e: e.tensor_reduce(
                rtmp[:], ppt[:], AX.X, ALU.max, apply_absolute_value=True),
                reads=[ppt], writes=[rtmp])
            p.op("vector", lambda e: e.tensor_tensor(
                amax4[:, 0:1], amax4[:, 0:1], rtmp[:], ALU.max),
                reads=[amax4, rtmp], writes=[amax4])
            p.dma("sync", lambda e, mi=mi, ch=ch: e.dma_start(
                out=sp_P.ap()[mi * 128:(mi + 1) * 128,
                              ch * 512:(ch + 1) * 512], in_=ppt[:]),
                reads=[ppt], writes=[f"sp_P_{ch}_{mi}"])

    if doB and not fused:
        export_mx(H, reduce_acc(H, cid + "g4", amax4, 1))
    if fused:
        gm4 = global_max(H, cid + "g4", amax4, 1, pbc)
    elif doC or doD:
        gm4 = consume_gmax(H, cid + "g4", "g4_in")
    if fused or doC or doD:
        s3f = ctx.sbuf(cid + "_s3f", [1, 1])
        s4v = ctx.sbuf(cid + "_s4v", [1, 1])
        c4v = ctx.sbuf(cid + "_c4v", [1, 1])
        p.op("vector", lambda e: e.tensor_scalar(
            s3f[:], s3v[:], 1.0 / 65536.0, None, ALU.mult),
            reads=[s3v], writes=[s3f])
        p.op("vector", lambda e: e.tensor_tensor(
            s4v[:], gm4[0:1, 0:1], s3f[:], ALU.mult),
            reads=[gm4, s3f], writes=[s4v])
        p.op("vector", lambda e: e.tensor_scalar(
            s4v[:], s4v[:], 1.0 / QMAX, 1e-12, ALU.mult, ALU.max),
            reads=[s4v], writes=[s4v])
        p.op("vector", lambda e: e.reciprocal(c4v[:], s4v[:]),
             reads=[s4v], writes=[c4v])
        p.op("vector", lambda e: e.tensor_tensor(
            c4v[:], c4v[:], s3f[:], ALU.mult), reads=[c4v, s3f], writes=[c4v])
        c4 = ctx.sbuf(cid + "_c4b", [128, 1])
        scalar_bcast(H, cid + "c4", c4v[0:1, 0:1], c4v, pbc, c4)

    upl = [ctx.sbuf(cid + "_Cu", [128, 512]) for _ in range(2)]
    ptl = [ctx.sbuf(cid + "_Cp", [128, 512]) for _ in range(2)]
    i4f = ctx.sbuf(cid + "_i4f", [128, 512])
    iuf = ctx.sbuf(cid + "_iuf", [128, 512])
    oo = ctx.sbuf(cid + "_oo", [128, 512])
    amax5 = ctx.sbuf(cid + "_am5", [128, 2])
    if doC:
        p.op("vector", lambda e: e.memset(amax5[:], 0.0), writes=[amax5])

    for ch in (range(NCH) if doC else ()):
        for mi in range(NIT):
            ut = upl[mi % 2]
            pt = ptl[mi % 2]
            p.dma("sync", lambda e, mi=mi, ch=ch, ut=ut: e.dma_start(
                out=ut[:], in_=sp_up.ap()[mi * 128:(mi + 1) * 128,
                                          ch * 512:(ch + 1) * 512]),
                reads=[f"sp_up_{ch}_{mi}"], writes=[ut])
            p.dma("scalar", lambda e, mi=mi, ch=ch, pt=pt: e.dma_start(
                out=pt[:], in_=sp_P.ap()[mi * 128:(mi + 1) * 128,
                                         ch * 512:(ch + 1) * 512]),
                reads=[f"sp_P_{ch}_{mi}"], writes=[pt])

            def prec4(e, yy, pt=pt):
                return e.tensor_scalar(yy[:], pt[:], c4[:], 0.5,
                                       ALU.mult, ALU.add)

            round_half_up(H, yb, ib, i4f, gb, prec4, [pt, c4])

            def prec2(e, yy, ut=ut):
                return e.tensor_scalar(yy[:], ut[:], r2[:], 0.5,
                                       ALU.mult, ALU.add)

            round_half_up(H, yb, ib, iuf, gb, prec2, [ut, r2])
            p.op("vector", lambda e: e.tensor_tensor(
                oo[:], i4f[:], iuf[:], ALU.mult), reads=[i4f, iuf], writes=[oo])
            p.op("vector", lambda e: e.tensor_reduce(
                rtmp[:], oo[:], AX.X, ALU.max, apply_absolute_value=True),
                reads=[oo], writes=[rtmp])
            p.op("vector", lambda e: e.tensor_tensor(
                amax5[:, 0:1], amax5[:, 0:1], rtmp[:], ALU.max),
                reads=[amax5, rtmp], writes=[amax5])
            p.dma("sync", lambda e, mi=mi, ch=ch: e.dma_start(
                out=sp_o.ap()[mi * 128:(mi + 1) * 128,
                              ch * 512:(ch + 1) * 512], in_=oo[:]),
                reads=[oo], writes=[f"sp_o_{ch}_{mi}"])

    if doC and not fused:
        export_mx(H, reduce_acc(H, cid + "g5", amax5, 1))
    if fused:
        gm5 = global_max(H, cid + "g5", amax5, 1, pbc)
    elif doD:
        gm5 = consume_gmax(H, cid + "g5", "g5_in")
    if fused or doD:
        s42 = ctx.sbuf(cid + "_s42", [1, 1])
        s5v = ctx.sbuf(cid + "_s5v", [1, 1])
        c5v = ctx.sbuf(cid + "_c5v", [1, 1])
        sc5v = ctx.sbuf(cid + "_sc5v", [1, 1])
        p.op("vector", lambda e: e.tensor_tensor(
            s42[:], s4v[:], s2v[:], ALU.mult), reads=[s4v, s2v], writes=[s42])
        p.op("vector", lambda e: e.tensor_tensor(
            s5v[:], gm5[0:1, 0:1], s42[:], ALU.mult),
            reads=[gm5, s42], writes=[s5v])
        p.op("vector", lambda e: e.tensor_scalar(
            s5v[:], s5v[:], 1.0 / QMAX, 1e-12, ALU.mult, ALU.max),
            reads=[s5v], writes=[s5v])
        p.op("vector", lambda e: e.reciprocal(c5v[:], s5v[:]),
             reads=[s5v], writes=[c5v])
        p.op("vector", lambda e: e.tensor_tensor(
            c5v[:], c5v[:], s42[:], ALU.mult), reads=[c5v, s42], writes=[c5v])
        p.op("vector", lambda e: e.tensor_scalar(
            sc5v[:], s5v[:], 0.5, None, ALU.mult), reads=[s5v], writes=[sc5v])
        c5 = ctx.sbuf(cid + "_c5b", [128, 1])
        sc5 = ctx.sbuf(cid + "_sc5b", [128, 1])
        scalar_bcast(H, cid + "c5", c5v[0:1, 0:1], c5v, pbc, c5)
        scalar_bcast(H, cid + "sc5", sc5v[0:1, 0:1], sc5v, pbc, sc5)

    ctx.push()
    wd_s = [ctx.sbuf(cid + "_wd", [128, D], BF16) for _ in range(NIT)]
    for mi in (range(NIT) if doD else ()):
        p.dma("sync", lambda e, mi=mi: e.dma_start(
            out=wd_s[mi][:], in_=dram["wdownT"].ap()[mi * 128:(mi + 1) * 128, :]),
            reads=[f"wdownT_{it}_{mi}" for it in range(D // 128)],
            writes=[wd_s[mi]])
    otl = [ctx.sbuf(cid + "_Do", [128, 512]) for _ in range(2)]
    i5b = [ctx.sbuf(cid + "_i5b", [128, 512], BF16) for _ in range(NIT)]
    i5f = ctx.sbuf(cid + "_i5f", [128, 512])
    xr = [ctx.sbuf(cid + "_Dx", [128, D]) for _ in range(2)]
    xo = [ctx.sbuf(cid + "_Dxo", [128, D]) for _ in range(2)]

    for ch in (range(NCH) if doD else ()):
        for mi in range(NIT):
            ot = otl[mi % 2]
            p.dma("sync", lambda e, mi=mi, ch=ch, ot=ot: e.dma_start(
                out=ot[:], in_=sp_o.ap()[mi * 128:(mi + 1) * 128,
                                         ch * 512:(ch + 1) * 512]),
                reads=[f"sp_o_{ch}_{mi}"], writes=[ot])

            def prec5(e, yy, ot=ot):
                return e.tensor_scalar(yy[:], ot[:], c5[:], 0.5,
                                       ALU.mult, ALU.add)

            round_half_up(H, yb, ib, i5f, gb, prec5, [ot, c5])
            p.op("vector", lambda e, mi=mi: e.tensor_copy(i5b[mi][:], i5f[:]),
                 reads=[i5f], writes=[i5b[mi]])
        for tb4 in range(4):
            tb = ch * 4 + tb4
            xrt = xr[tb % 2]
            xot = xo[tb % 2]
            p.dma("sync", lambda e, tb=tb, xrt=xrt: e.dma_start(
                out=xrt[:], in_=x_src.ap()[tb * 128:(tb + 1) * 128, :]),
                reads=[xsrc_r(tb)], writes=[xrt])
            for dh in range(2):
                pd = dps[dh]
                for mi in range(NIT):
                    p.op("tensor", lambda e, mi=mi, tb4=tb4, dh=dh, pd=pd:
                         e.matmul(pd[:],
                                  i5b[mi][:, tb4 * 128:(tb4 + 1) * 128],
                                  wd_s[mi][:, dh * 512:(dh + 1) * 512],
                                  start=(mi == 0), stop=(mi == NIT - 1)),
                         reads=[i5b[mi], wd_s[mi]], writes=[pd])
                p.op("vector", lambda e, dh=dh, xrt=xrt, xot=xot, pd=pd:
                     e.scalar_tensor_tensor(
                         xot[:, dh * 512:(dh + 1) * 512], pd[:], sc5[:],
                         xrt[:, dh * 512:(dh + 1) * 512], ALU.mult, ALU.add),
                     reads=[pd, sc5, xrt], writes=[xot])
            p.dma("sync", lambda e, tb=tb, xot=xot: e.dma_start(
                out=x_dst.ap()[tb * 128:(tb + 1) * 128, :], in_=xot[:]),
                reads=[xot], writes=[f"xdst_{x_dst.name}_{tb}"])
    ctx.pop()



def mixer_copy(H, x_src, x_dst):
    ctx, p = H["ctx"], H["p"]
    T = H["T"]
    t = ctx.sbuf("mbcp", [128, D])
    for tb in range(T // 128):
        p.dma("sync", lambda e, tb=tb: e.dma_start(
            out=t[:], in_=x_src.ap()[tb * 128:(tb + 1) * 128, :]),
            reads=[f"xdst_{x_src.name}_{tb}"], writes=[t])
        p.dma("sync", lambda e, tb=tb: e.dma_start(
            out=x_dst.ap()[tb * 128:(tb + 1) * 128, :], in_=t[:]),
            reads=[t], writes=[f"xdst_{x_dst.name}_{tb}"])


def mamba(H, x_src, x_dst):
    ctx, p, dram = H["ctx"], H["p"], H["nc"] and H["dram"]
    p = H["p"]
    ctx = H["ctx"]
    T, NT = H["T"], H["NT"]
    NSC = T // 512
    ident_b = H["ident_b"]

    def xsrc_r(tb):
        return f"xdst_{x_src.name}_{tb}"

    ps = [ctx.psum("mb_ps", [128, 512]) for _ in range(6)]
    p96 = ctx.psum("mb_p96", [96, 512])
    tpp0 = ctx.psum("mb_tp", [128, 256], BF16)
    tpp = [tpp0[:, 0:128], tpp0[:, 128:256]]

    winb_d = dram["winb"]
    woutb_d = dram["woutb"]
    wx_b = ctx.sbuf("mb_wxb", [128, 16 * 96], BF16)
    wdt_b = ctx.sbuf("mb_wdtb", [64, DI], BF16)
    ctx.push()
    wtmp = ctx.sbuf("mb_wtmp", [128, 2 * DI])
    wtmpb = ctx.sbuf("mb_wtmpb", [128, 2 * DI], BF16)
    for kb in range(8):
        p.dma("sync", lambda e, kb=kb: e.dma_start(
            out=wtmp[:], in_=dram["W_in"].ap()[kb * 128:(kb + 1) * 128, :]),
            writes=[wtmp])
        p.op("vector", lambda e: e.tensor_copy(wtmpb[:], wtmp[:]),
             reads=[wtmp], writes=[wtmpb])
        p.dma("sync", lambda e, kb=kb: e.dma_start(
            out=winb_d.ap()[kb * 128:(kb + 1) * 128, :], in_=wtmpb[:]),
            reads=[wtmpb], writes=[f"winb_{kb}"])
    for kt in range(16):
        p.dma("sync", lambda e, kt=kt: e.dma_start(
            out=wtmp[:, 0:D], in_=dram["W_out"].ap()[kt * 128:(kt + 1) * 128, :]),
            writes=[wtmp])
        p.op("vector", lambda e: e.tensor_copy(wtmpb[:, 0:D], wtmp[:, 0:D]),
             reads=[wtmp], writes=[wtmpb])
        p.dma("sync", lambda e, kt=kt: e.dma_start(
            out=woutb_d.ap()[kt * 128:(kt + 1) * 128, :], in_=wtmpb[:, 0:D]),
            reads=[wtmpb], writes=[f"woutb_{kt}"])

    for kt in range(16):
        p.dma("sync", lambda e, kt=kt: e.dma_start(
            out=wtmp[:, 0:96], in_=dram["W_x"].ap()[kt * 128:(kt + 1) * 128, :]),
            writes=[wtmp])
        p.op("vector", lambda e, kt=kt: e.tensor_copy(
            wx_b[:, kt * 96:(kt + 1) * 96], wtmp[:, 0:96]),
            reads=[wtmp], writes=[wx_b])
    p.dma("sync", lambda e: e.dma_start(
        out=wtmp[0:64, 0:DI], in_=dram["W_dt"].ap()), writes=[wtmp])
    p.op("vector", lambda e: e.tensor_copy(wdt_b[:], wtmp[0:64, 0:DI]),
         reads=[wtmp], writes=[wdt_b])
    ctx.pop()
    p.barrier()
    win_r = [ctx.sbuf("mb_winr", [128, 2 * DI], BF16) for _ in range(8)]
    for kb in range(8):
        p.dma("sync", lambda e, kb=kb: e.dma_start(
            out=win_r[kb][:], in_=winb_d.ap()[kb * 128:(kb + 1) * 128, :]),
            reads=[f"winb_{kb}"], writes=[win_r[kb]])
    wout_s = [ctx.sbuf("mb_wouts", [128, D], BF16) for _ in range(2)]


    convw = ctx.sbuf("mb_convw", [128, 64])
    convb2 = ctx.sbuf("mb_convb2", [128, 16])
    bdt = ctx.sbuf("mb_bdt", [128, 16])
    dvec = ctx.sbuf("mb_dvec", [128, 16])
    a_t = ctx.sbuf("mb_at", [128, 256])
    p.dma("sync", lambda e: e.dma_start(out=convw[:], in_=dram["conv_wt"].ap()),
          writes=[convw])
    p.dma("sync", lambda e: e.dma_start(out=convb2[:], in_=dram["conv_b"].ap()),
          writes=[convb2])
    p.dma("sync", lambda e: e.dma_start(out=bdt[:], in_=dram["b_dt"].ap()),
          writes=[bdt])
    p.dma("sync", lambda e: e.dma_start(out=dvec[:], in_=dram["Dvec"].ap()),
          writes=[dvec])
    p.dma("sync", lambda e: e.dma_start(out=a_t[:], in_=dram["A_log_t"].ap()),
          writes=[a_t])
    p.op("scalar", lambda e: e.activation(a_t[:], a_t[:], AF.Exp,
                                          bias=0.0, scale=1.0),
         reads=[a_t], writes=[a_t])
    p.op("vector", lambda e: e.tensor_scalar(
        a_t[:], a_t[:], -1.0, None, ALU.mult), reads=[a_t], writes=[a_t])
    sel16 = ctx.sbuf("mb_sel", [16, 16 * 128], BF16)
    p.dma("sync", lambda e: e.dma_start(out=sel16[:], in_=dram["sel16"].ap()),
          writes=[sel16])

    h_state = ctx.sbuf("mb_h", [128, 256])
    xi_tail = ctx.sbuf("mb_tail", [128, 48])
    p.op("vector", lambda e: e.memset(h_state[:], 0.0), writes=[h_state])
    p.op("vector", lambda e: e.memset(xi_tail[:], 0.0), writes=[xi_tail])

    xc2 = [ctx.sbuf("mb_x2", [128, D]) for _ in range(2)]
    lnT = [ctx.sbuf("mb_lnT", [128, 512], BF16) for _ in range(8)]
    lnb = ctx.sbuf("mb_lnb", [128, D], BF16)
    mloc = ctx.sbuf("mb_m", [128, 1])
    rloc = ctx.sbuf("mb_r", [128, 1])
    bn6 = ctx.sbuf("mb_bn6", [128, 12])
    bn2 = ctx.sbuf("mb_bn2", [128, 2])
    vt = ctx.sbuf("mb_vt", [128, 1])
    xi = [ctx.sbuf("mb_xi", [128, 516]) for _ in range(2)]
    xc_f = [ctx.sbuf("mb_xc", [128, 512]) for _ in range(16)]
    xcb = [ctx.sbuf("mb_xcb", [128, 512], BF16) for _ in range(2)]
    del_f = [ctx.sbuf("mb_del", [128, 512]) for _ in range(16)]
    cvt = ctx.sbuf("mb_cv", [128, 512])
    cv2 = ctx.sbuf("mb_cv2", [128, 512])
    ztile = [ctx.sbuf("mb_zt", [128, 512]) for _ in range(2)]
    xdbl = ctx.sbuf("mb_xdbl", [96, 512])
    dtb = ctx.sbuf("mb_dtb", [64, 512], BF16)
    bc32 = ctx.sbuf("mb_bc32", [32, 512])
    bhf32 = ctx.sbuf("mb_bhf32", [32, 512])
    bch32 = ctx.sbuf("mb_bch", [32, 512], BF16)
    bcl32 = ctx.sbuf("mb_bcl", [32, 512], BF16)
    bCh = ctx.sbuf("mb_bCh", [16, 512], BF16)
    bCl = ctx.sbuf("mb_bCl", [16, 512], BF16)
    u_t = ctx.sbuf("mb_u", [128, 512])
    h_t = ctx.sbuf("mb_ht", [128, 512])
    dbu = ctx.sbuf("mb_dbu", [128, 512])
    pb_t = ctx.sbuf("mb_pb", [128, 512], BF16)
    w_t = ctx.sbuf("mb_wt2", [128, 512])
    sz_t = ctx.sbuf("mb_sz", [128, 512])
    omix = [ctx.sbuf("mb_om", [128, 512], BF16) for _ in range(2)]
    xo_t = [ctx.sbuf("mb_xo", [128, D])]

    for sc in range(NSC):
        t0 = sc * 512
        for tb4 in range(4):
            tb = sc * 4 + tb4
            xt2 = xc2[tb4 % 2]
            p.dma("sync", lambda e, tb=tb, xt2=xt2: e.dma_start(
                out=xt2[:], in_=x_src.ap()[tb * 128:(tb + 1) * 128, :]),
                reads=[xsrc_r(tb)], writes=[xt2])
            p.op("vector", lambda e, xt2=xt2: e.bn_stats(bn6[:, 0:6],
                                                         xt2[:, 0:512]),
                 reads=[xt2], writes=[bn6])
            p.op("vector", lambda e, xt2=xt2: e.bn_stats(bn6[:, 6:12],
                                                         xt2[:, 512:1024]),
                 reads=[xt2], writes=[bn6])
            p.op("vector", lambda e: e.bn_aggr(bn2[:], bn6[:]),
                 reads=[bn6], writes=[bn2])
            p.op("scalar", lambda e: e.activation(vt[:], bn2[:, 1:2], AF.Ln,
                                                  bias=H["eps_b"][:], scale=1.0),
                 reads=[bn2, H["eps_b"]], writes=[vt])
            p.op("scalar", lambda e: e.activation(rloc[:], vt[:], AF.Exp,
                                                  bias=0.0, scale=-0.5),
                 reads=[vt], writes=[rloc])
            p.op("vector", lambda e: e.tensor_tensor(
                mloc[:], bn2[:, 0:1], rloc[:], ALU.mult),
                reads=[bn2, rloc], writes=[mloc])
            p.op("vector", lambda e: e.tensor_scalar(
                mloc[:], mloc[:], -1.0, None, ALU.mult),
                reads=[mloc], writes=[mloc])
            p.op("vector", lambda e, xt2=xt2: e.tensor_scalar(
                lnb[:], xt2[:], rloc[:], mloc[:], ALU.mult, ALU.add),
                reads=[xt2, rloc, mloc], writes=[lnb])
            for db in range(8):
                tpx = tpp[db % 2]
                p.op("tensor", lambda e, db=db, tpx=tpx: e.transpose(
                    tpx[:], lnb[:, db * 128:(db + 1) * 128], ident_b[:]),
                    reads=[lnb, ident_b], writes=[tpx])
                p.op("scalar", lambda e, db=db, tb4=tb4, tpx=tpx: e.activation(
                    lnT[db][:, tb4 * 128:(tb4 + 1) * 128], tpx[:], AF.Copy,
                    bias=0.0, scale=1.0),
                    reads=[tpx], writes=[lnT[db]])
        for mi in range(32):
            pmm = ps[mi % 2]
            for kb in range(8):
                wsrc = win_r[kb]
                p.op("tensor", lambda e, mi=mi, kb=kb, pmm=pmm, wsrc=wsrc:
                     e.matmul(pmm[:], wsrc[:, mi * 128:(mi + 1) * 128],
                              lnT[kb][:], start=(kb == 0), stop=(kb == 7)),
                     reads=[wsrc, lnT[kb]], writes=[pmm])
            if mi < 16:
                dtile = mi
                xit = xi[dtile % 2]
                p.op("scalar", lambda e, pmm=pmm, xit=xit: e.activation(
                    xit[:, 3:515], pmm[:], AF.Copy, bias=0.0, scale=1.0),
                    reads=[pmm], writes=[xit])
                p.op("vector", lambda e, dtile=dtile, xit=xit: e.tensor_copy(
                    xit[:, 0:3], xi_tail[:, dtile * 3:dtile * 3 + 3]),
                    reads=[xi_tail], writes=[xit])
                p.op("vector", lambda e, dtile=dtile, xit=xit: e.tensor_copy(
                    xi_tail[:, dtile * 3:dtile * 3 + 3], xit[:, 512:515]),
                    reads=[xit], writes=[xi_tail])
                p.op("vector", lambda e, dtile=dtile, xit=xit: e.tensor_scalar(
                    cvt[:], xit[:, 0:512], convw[:, dtile * 4:dtile * 4 + 1],
                    None, ALU.mult),
                    reads=[xit, convw], writes=[cvt])
                for k in range(1, 4):
                    p.op("vector", lambda e, dtile=dtile, k=k, xit=xit:
                         e.scalar_tensor_tensor(
                             cvt[:], xit[:, k:k + 512],
                             convw[:, dtile * 4 + k:dtile * 4 + k + 1],
                             cvt[:], ALU.mult, ALU.add),
                         reads=[xit, convw, cvt], writes=[cvt])
                p.op("vector", lambda e, dtile=dtile: e.tensor_scalar(
                    cvt[:], cvt[:], convb2[:, dtile:dtile + 1], None, ALU.add),
                    reads=[cvt, convb2], writes=[cvt])
                p.op("scalar", lambda e: e.activation(cv2[:], cvt[:], AF.Tanh,
                                                      bias=0.0, scale=0.5),
                     reads=[cvt], writes=[cv2])
                p.op("vector", lambda e: e.tensor_scalar(
                    cv2[:], cv2[:], 0.5, 0.5, ALU.mult, ALU.add),
                    reads=[cv2], writes=[cv2])
                p.op("vector", lambda e, dtile=dtile: e.tensor_tensor(
                    xc_f[dtile][:], cvt[:], cv2[:], ALU.mult),
                    reads=[cvt, cv2], writes=[xc_f[dtile]])
                xcr = xcb[dtile % 2]
                p.op("vector", lambda e, dtile=dtile, xcr=xcr: e.tensor_copy(
                    xcr[:], xc_f[dtile][:]),
                    reads=[xc_f[dtile]], writes=[xcr])
                p.op("tensor", lambda e, dtile=dtile, xcr=xcr: e.matmul(
                    p96[:], wx_b[:, dtile * 96:(dtile + 1) * 96], xcr[:],
                    start=(dtile == 0), stop=(dtile == 15)),
                    reads=[wx_b, xcr], writes=[p96])
            else:
                dtile = mi - 16
                zt = ztile[dtile % 2]
                p.op("scalar", lambda e, pmm=pmm, zt=zt: e.activation(
                    zt[:], pmm[:], AF.Copy, bias=0.0, scale=1.0),
                    reads=[pmm], writes=[zt])
                p.dma("sync", lambda e, dtile=dtile, sc=sc, zt=zt: e.dma_start(
                    out=dram["sp_z"].ap()[dtile * 128:(dtile + 1) * 128,
                                          t0:t0 + 512], in_=zt[:]),
                    reads=[zt], writes=[f"sp_z_{sc}_{dtile}"])
        p.op("scalar", lambda e: e.activation(xdbl[:], p96[:], AF.Copy,
                                              bias=0.0, scale=1.0),
             reads=[p96], writes=[xdbl])
        p.op("vector", lambda e: e.tensor_copy(dtb[:], xdbl[0:64, :]),
             reads=[xdbl], writes=[dtb])
        for mt in range(16):
            pmd = ps[2 + (mt % 2)]
            p.op("tensor", lambda e, mt=mt, pmd=pmd: e.matmul(
                pmd[:], wdt_b[:, mt * 128:(mt + 1) * 128], dtb[:],
                start=True, stop=True),
                reads=[wdt_b, dtb], writes=[pmd])
            p.op("vector", lambda e, mt=mt, pmd=pmd: e.tensor_scalar(
                cvt[:], pmd[:], bdt[:, mt:mt + 1], None, ALU.add),
                reads=[pmd, bdt], writes=[cvt])
            p.op("scalar", lambda e: e.activation(cv2[:], cvt[:], AF.Abs,
                                                  bias=0.0, scale=1.0),
                 reads=[cvt], writes=[cv2])
            p.op("scalar", lambda e: e.activation(cv2[:], cv2[:], AF.Exp,
                                                  bias=0.0, scale=-1.0),
                 reads=[cv2], writes=[cv2])
            p.op("scalar", lambda e: e.activation(cv2[:], cv2[:], AF.Ln,
                                                  bias=1.0, scale=1.0),
                 reads=[cv2], writes=[cv2])
            p.op("vector", lambda e: e.tensor_scalar(
                cvt[:], cvt[:], 0.0, None, ALU.max), reads=[cvt], writes=[cvt])
            p.op("vector", lambda e, mt=mt: e.tensor_tensor(
                del_f[mt][:], cvt[:], cv2[:], ALU.add),
                reads=[cvt, cv2], writes=[del_f[mt]])
        p.op("scalar", lambda e: e.activation(bc32[:], p96[64:96, :],
                                              AF.Copy, bias=0.0, scale=1.0),
             reads=[p96], writes=[bc32])
        p.op("vector", lambda e: e.tensor_copy(bch32[:], bc32[:]),
             reads=[bc32], writes=[bch32])
        p.op("vector", lambda e: e.tensor_copy(bhf32[:], bch32[:]),
             reads=[bch32], writes=[bhf32])
        p.op("vector", lambda e: e.tensor_tensor(
            bhf32[:], bc32[:], bhf32[:], ALU.subtract), reads=[bc32, bhf32],
            writes=[bhf32])
        p.op("vector", lambda e: e.tensor_copy(bcl32[:], bhf32[:]),
             reads=[bhf32], writes=[bcl32])
        p.dma("scalar", lambda e: e.dma_start(out=bCh[:], in_=bch32[16:32, :]),
              reads=[bch32], writes=[bCh])
        p.dma("scalar", lambda e: e.dma_start(out=bCl[:], in_=bcl32[16:32, :]),
              reads=[bcl32], writes=[bCl])
        for dtile in range(16):
            py = ps[dtile % 2]
            p.op("vector", lambda e, dtile=dtile: e.tensor_tensor(
                u_t[:], del_f[dtile][:], xc_f[dtile][:], ALU.mult),
                reads=[del_f[dtile], xc_f[dtile]], writes=[u_t])
            for n in range(16):
                col = dtile * 16 + n
                pB, pC, pA = ps[4], ps[5], ps[2 + (n % 2)]
                p.op("tensor", lambda e, n=n, pB=pB: e.matmul(
                    pB[:], sel16[:, n * 128:(n + 1) * 128], bch32[0:16, :],
                    start=True, stop=False),
                    reads=[sel16, bch32], writes=[pB])
                p.op("tensor", lambda e, n=n, pB=pB: e.matmul(
                    pB[:], sel16[:, n * 128:(n + 1) * 128], bcl32[0:16, :],
                    start=False, stop=True),
                    reads=[sel16, bcl32], writes=[pB])
                p.op("tensor", lambda e, n=n, pC=pC: e.matmul(
                    pC[:], sel16[:, n * 128:(n + 1) * 128], bCh[:],
                    start=True, stop=False),
                    reads=[sel16, bCh], writes=[pC])
                p.op("tensor", lambda e, n=n, pC=pC: e.matmul(
                    pC[:], sel16[:, n * 128:(n + 1) * 128], bCl[:],
                    start=False, stop=True),
                    reads=[sel16, bCl], writes=[pC])
                p.op("scalar", lambda e, dtile=dtile, col=col, pA=pA:
                     e.activation(pA[:], del_f[dtile][:], AF.Exp, bias=0.0,
                                  scale=a_t[:, col:col + 1]),
                     reads=[del_f[dtile], a_t], writes=[pA])
                p.op("vector", lambda e, pB=pB: e.tensor_tensor(
                    dbu[:], u_t[:], pB[:], ALU.mult),
                    reads=[u_t, pB], writes=[dbu])
                p.op("vector", lambda e, col=col, pA=pA:
                     e.scalar_tensor_tensor(
                         dbu[:, 0:1], pA[:, 0:1], h_state[:, col:col + 1],
                         dbu[:, 0:1], ALU.mult, ALU.add),
                     reads=[pA, h_state, dbu], writes=[dbu])
                p.op("vector", lambda e, pA=pA: e.tensor_tensor_scan(
                    h_t[:], pA[:], dbu[:], 0.0, ALU.mult, ALU.add),
                    reads=[pA, dbu], writes=[h_t])
                p.op("vector", lambda e, col=col: e.tensor_copy(
                    h_state[:, col:col + 1], h_t[:, 511:512]),
                    reads=[h_t], writes=[h_state])
                p.op("vector", lambda e, pC=pC: e.tensor_tensor(
                    pb_t[:], h_t[:], pC[:], ALU.mult),
                    reads=[h_t, pC], writes=[pb_t])
                p.op("tensor", lambda e, n=n, py=py: e.matmul(
                    py[:], ident_b[:], pb_t[:],
                    start=(n == 0), stop=(n == 15)),
                    reads=[ident_b, pb_t], writes=[py])
            p.op("vector", lambda e, dtile=dtile, py=py:
                 e.scalar_tensor_tensor(
                     w_t[:], xc_f[dtile][:], dvec[:, dtile:dtile + 1], py[:],
                     ALU.mult, ALU.add),
                 reads=[xc_f[dtile], dvec, py], writes=[w_t])
            zt = ztile[dtile % 2]
            p.dma("sync", lambda e, dtile=dtile, sc=sc, zt=zt: e.dma_start(
                out=zt[:], in_=dram["sp_z"].ap()[dtile * 128:(dtile + 1) * 128,
                                                 t0:t0 + 512]),
                reads=[f"sp_z_{sc}_{dtile}"], writes=[zt])
            p.op("scalar", lambda e, zt=zt: e.activation(
                sz_t[:], zt[:], AF.Tanh, bias=0.0, scale=0.5),
                reads=[zt], writes=[sz_t])
            p.op("vector", lambda e: e.tensor_scalar(
                sz_t[:], sz_t[:], 0.5, 0.5, ALU.mult, ALU.add),
                reads=[sz_t], writes=[sz_t])
            p.op("vector", lambda e, zt=zt: e.tensor_tensor(
                sz_t[:], sz_t[:], zt[:], ALU.mult),
                reads=[sz_t, zt], writes=[sz_t])
            omt = omix[dtile % 2]
            p.op("vector", lambda e, omt=omt: e.tensor_tensor(
                omt[:], w_t[:], sz_t[:], ALU.mult),
                reads=[w_t, sz_t], writes=[omt])
            p.dma("sync", lambda e, dtile=dtile, sc=sc, omt=omt: e.dma_start(
                out=dram["omixd"].ap()[dtile * 128:(dtile + 1) * 128,
                                       t0:t0 + 512], in_=omt[:]),
                reads=[omt], writes=[f"omixd_{sc}_{dtile}"])
        for half in range(2):
            for kt in range(16):
                wo = wout_s[kt % 2]
                omt = omix[kt % 2]
                p.dma("scalar", lambda e, kt=kt, wo=wo: e.dma_start(
                    out=wo[:], in_=woutb_d.ap()[kt * 128:(kt + 1) * 128, :]),
                    reads=[f"woutb_{kt}"], writes=[wo])
                p.dma("sync", lambda e, kt=kt, sc=sc, omt=omt: e.dma_start(
                    out=omt[:], in_=dram["omixd"].ap()[kt * 128:(kt + 1) * 128,
                                                       t0:t0 + 512]),
                    reads=[f"omixd_{sc}_{kt}"], writes=[omt])
                for t2 in range(2):
                    tb4 = half * 2 + t2
                    for dh in range(2):
                        po = ps[2 + t2 * 2 + dh]
                        p.op("tensor", lambda e, kt=kt, tb4=tb4, dh=dh, po=po,
                             wo=wo, omt=omt: e.matmul(
                                 po[:], omt[:, tb4 * 128:(tb4 + 1) * 128],
                                 wo[:, dh * 512:(dh + 1) * 512],
                                 start=(kt == 0), stop=(kt == 15)),
                             reads=[omt, wo], writes=[po])
            for t2 in range(2):
                tb4 = half * 2 + t2
                tb = sc * 4 + tb4
                xot = xo_t[0]
                x2r = xc2[t2]
                p.dma("sync", lambda e, tb=tb, x2r=x2r: e.dma_start(
                    out=x2r[:], in_=x_src.ap()[tb * 128:(tb + 1) * 128, :]),
                    reads=[xsrc_r(tb)], writes=[x2r])
                for dh in range(2):
                    po = ps[2 + t2 * 2 + dh]
                    p.op("vector", lambda e, dh=dh, x2r=x2r, xot=xot, po=po:
                         e.scalar_tensor_tensor(
                             xot[:, dh * 512:(dh + 1) * 512], po[:], 1.0,
                             x2r[:, dh * 512:(dh + 1) * 512],
                             ALU.mult, ALU.add),
                         reads=[po, x2r], writes=[xot])
                p.dma("sync", lambda e, tb=tb, xot=xot: e.dma_start(
                    out=x_dst.ap()[tb * 128:(tb + 1) * 128, :], in_=xot[:]),
                    reads=[xot], writes=[f"xdst_{x_dst.name}_{tb}"])


# version 03
# speedup vs baseline: 1.1091x; 1.1091x over previous
import numpy as np
import ml_dtypes

import concourse.bass as bass
import concourse.mybir as mybir
from concourse.bass_utils import run_bass_kernel_spmd

F32 = mybir.dt.float32
BF16 = mybir.dt.bfloat16
I32 = mybir.dt.int32
AF = mybir.ActivationFunctionType
ALU = mybir.AluOpType
AX = mybir.AxisListType

ENGINES = ("sync", "vector", "scalar", "tensor", "gpsimd")
NCORES = 8

D = 1024
DI = 2048
NS = 16
DTR = 64
IF = 3072
QMAX = 32767.0
SEM_MAX = 48000
NSEM = 3


class Prog:

    def __init__(self, nc):
        self.nc = nc
        self.ops = []
        self.last_writer = {}
        self.readers = {}
        self.deps = []
        self.barrier_deps = set()
        self._last_compute = {}
        self._last_dmas = {}

    def barrier(self):
        deps = set()
        for e, i in self._last_compute.items():
            deps.add(i)
        for e, lst in self._last_dmas.items():
            deps.update(lst)
        self.barrier_deps = deps

    def _name(self, b):
        if isinstance(b, str):
            return b
        return b.tensor.name if isinstance(b, bass.AP) else b.name

    def _add(self, engine, fn, reads, writes, inc):
        i = len(self.ops)
        dep = set()
        reads = [self._name(b) for b in reads]
        writes = [self._name(b) for b in writes]
        for b in reads:
            w = self.last_writer.get(b)
            if w is not None:
                dep.add(w)
        for b in writes:
            w = self.last_writer.get(b)
            if w is not None:
                dep.add(w)
            dep.update(self.readers.get(b, ()))
        dep |= self.barrier_deps
        dep.discard(i)
        if inc == 16:
            self._last_dmas.setdefault(engine, [])
            self._last_dmas[engine].append(i)
            if len(self._last_dmas[engine]) > 8:
                self._last_dmas[engine].pop(0)
        else:
            self._last_compute[engine] = i
        self.ops.append(dict(engine=engine, fn=fn, inc=inc))
        self.deps.append(dep)
        for b in reads:
            self.readers.setdefault(b, []).append(i)
        for b in writes:
            self.last_writer[b] = i
            self.readers[b] = []
        return i

    def op(self, engine, fn, reads=(), writes=()):
        return self._add(engine, fn, reads, writes, 1)

    def dma(self, engine, fn, reads=(), writes=()):
        return self._add(engine, fn, reads, writes, 16)

    def emit(self, block, semlists):
        n = len(self.ops)
        slot_of = [0] * n
        val_of = [0] * n
        nslot = {e: len(semlists[e]) for e in ENGINES}
        slot_ct = {e: [0] * nslot[e] for e in ENGINES}
        dma_rr = {e: 0 for e in ENGINES}
        engct = {e: 0 for e in ENGINES}
        for i, o in enumerate(self.ops):
            e = o["engine"]
            if o["inc"] == 16:
                s = dma_rr[e] % nslot[e]
                dma_rr[e] += 1
            else:
                s = (engct[e] // SEM_MAX) % nslot[e]
            engct[e] += o["inc"]
            slot_ct[e][s] += o["inc"]
            slot_of[i] = s
            val_of[i] = slot_ct[e][s]
        slot_ops = {}
        for i, o in enumerate(self.ops):
            slot_ops.setdefault((o["engine"], slot_of[i]), []).append(i)

        import bisect

        def cum_before(key, i):
            lst = slot_ops.get(key)
            if not lst:
                return 0
            k = bisect.bisect_left(lst, i)
            return val_of[lst[k - 1]] if k else 0

        for ename in ENGINES:
            idxs = [i for i, o in enumerate(self.ops) if o["engine"] == ename]
            if not idxs:
                continue

            def make(ename, idxs):
                def section(eng):
                    water = {}
                    for i in idxs:
                        o = self.ops[i]
                        need = {}
                        for j in self.deps[i]:
                            je = self.ops[j]["engine"]
                            key = (je, slot_of[j])
                            if self.ops[j]["inc"] == 16:
                                v = cum_before(key, i)
                            else:
                                v = val_of[j]
                            if need.get(key, 0) < v:
                                need[key] = v
                        for (je, sl), v in sorted(need.items()):
                            if water.get((je, sl), 0) >= v:
                                continue
                            eng.wait_ge(semlists[je][sl], v)
                            water[(je, sl)] = v
                        o["fn"](eng).then_inc(
                            semlists[ename][slot_of[i]], o["inc"])
                return section

            getattr(block, ename)(make(ename, idxs))
        totals = {e: [0] * nslot[e] for e in ENGINES}
        for i, o in enumerate(self.ops):
            e = o["engine"]
            totals[e][slot_of[i]] = max(totals[e][slot_of[i]], val_of[i])
        return totals


class Ctx:
    def __init__(self, nc):
        self.nc = nc
        self.stack = [[]]
        self.uid = 0

    def push(self):
        self.stack.append([])

    def pop(self):
        for g in reversed(self.stack.pop()):
            g.__exit__(None, None, None)

    def _alloc(self, kind, name, shape, dt):
        self.uid += 1
        nm = f"{name}_{self.uid}"
        g = (self.nc.sbuf_tensor(nm, shape, dt) if kind == "s"
             else self.nc.psum_tensor(nm, shape, dt))
        t = g.__enter__()
        self.stack[-1].append(g)
        return t

    def sbuf(self, name, shape, dt=F32):
        return self._alloc("s", name, shape, dt)

    def psum(self, name, shape, dt=F32):
        return self._alloc("p", name, shape, dt)



def kernel(L0_attn_x, L0_attn_W_in, L0_attn_conv_w, L0_attn_conv_b,
           L0_attn_W_x, L0_attn_W_dt, L0_attn_b_dt, L0_attn_A_log,
           L0_attn_D, L0_attn_W_out, L0_ffn_w_gate, L0_ffn_w_up,
           L0_ffn_w_down):
    x = np.ascontiguousarray(np.asarray(L0_attn_x, dtype=np.float32))
    B, T, Dm = x.shape

    def cols(v):
        return np.ascontiguousarray(
            np.asarray(v, np.float32).reshape(DI // 128, 128).T)

    conv_w = np.asarray(L0_attn_conv_w, np.float32).reshape(16, 128, 4)
    conv_w_t = np.ascontiguousarray(conv_w.transpose(1, 0, 2).reshape(128, 64))
    a_log_t = np.ascontiguousarray(
        np.asarray(L0_attn_A_log, np.float32).reshape(16, 128, NS)
        .transpose(1, 0, 2).reshape(128, 16 * NS))

    common = {
        "W_in": np.ascontiguousarray(np.asarray(L0_attn_W_in, np.float32)),
        "W_x": np.ascontiguousarray(np.asarray(L0_attn_W_x, np.float32)),
        "W_dt": np.ascontiguousarray(np.asarray(L0_attn_W_dt, np.float32)),
        "W_out": np.ascontiguousarray(np.asarray(L0_attn_W_out, np.float32)),
        "w_gate": np.ascontiguousarray(np.asarray(L0_ffn_w_gate, np.float32)),
        "w_up": np.ascontiguousarray(np.asarray(L0_ffn_w_up, np.float32)),
        "w_down": np.ascontiguousarray(np.asarray(L0_ffn_w_down, np.float32)),
        "conv_wt": conv_w_t,
        "conv_b": cols(L0_attn_conv_b),
        "b_dt": cols(L0_attn_b_dt),
        "Dvec": cols(L0_attn_D),
        "A_log_t": a_log_t,
        "ident": np.eye(128, dtype=ml_dtypes.bfloat16),
        "ones_row": np.ones((1, 128), dtype=np.float32),
        "sel16": _sel16(),
    }

    if FUSED:
        nc = build_program(T, ncores=B)
        in_maps = []
        for c in range(B):
            m = dict(common)
            m["x_in"] = np.ascontiguousarray(x[c])
            in_maps.append(m)
        res = run_bass_kernel_spmd(nc, in_maps, core_ids=list(range(B)))
        out = np.stack([np.asarray(r["x_out"]) for r in res.results], axis=0)
        return out.astype(np.float32)
    return _kernel_staged(x, common, T, B)


_PROG_CACHE = {}


def _get_prog(T, B, stage):
    key = (T, B, stage)
    if key not in _PROG_CACHE:
        _PROG_CACHE[key] = build_program(T, ncores=B, stage=stage)
    return _PROG_CACHE[key]


def _kernel_staged(x, common, T, B):
    cores = list(range(B))

    def run(stage, per_core):
        nc = _get_prog(T, B, stage)
        return run_bass_kernel_spmd(nc, per_core, core_ids=cores).results

    base = {k: common[k] for k in ("ident", "ones_row")}

    def gather_mx(res):
        return np.ascontiguousarray(
            np.concatenate([np.asarray(r["mxout"]) for r in res], axis=0))

    def ffn(x_host):
        r = run("max", [dict(base, x_in=x_host[c]) for c in cores])
        g1 = gather_mx(r)
        r = run("A", [dict(base, x_in=x_host[c], g1_in=g1,
                           w_up=common["w_up"], w_gate=common["w_gate"])
                      for c in cores])
        g23 = gather_mx(r)
        sp_up = [np.asarray(r[c]["sp_up"]) for c in cores]
        sp_gate = [np.asarray(r[c]["sp_gate"]) for c in cores]
        r = run("B", [dict(base, sp_gate=sp_gate[c], g23_in=g23)
                      for c in cores])
        g4 = gather_mx(r)
        sp_P = [np.asarray(r[c]["sp_P"]) for c in cores]
        r = run("C", [dict(base, sp_up=sp_up[c], sp_P=sp_P[c], g23_in=g23,
                           g4_in=g4) for c in cores])
        g5 = gather_mx(r)
        sp_o = [np.asarray(r[c]["sp_o"]) for c in cores]
        r = run("D", [dict(base, x_in=x_host[c], sp_o=sp_o[c],
                           w_down=common["w_down"], g23_in=g23, g4_in=g4,
                           g5_in=g5) for c in cores])
        return [np.asarray(r[c]["x_out"]) for c in cores]

    x_host = [np.ascontiguousarray(x[c]) for c in range(B)]
    x2 = ffn(x_host)
    mkeys = ("W_in", "W_x", "W_dt", "W_out", "conv_wt", "conv_b", "b_dt",
             "Dvec", "A_log_t", "sel16")
    r = run("mamba", [dict(base, x_in=x2[c],
                           **{k: common[k] for k in mkeys}) for c in cores])
    x3 = [np.asarray(r[c]["x_out"]) for c in cores]
    out = ffn(x3)
    return np.stack(out, axis=0).astype(np.float32)


FUSED = False


def _sel16():
    sel = np.zeros((16, 16 * 128), dtype=ml_dtypes.bfloat16)
    for n in range(16):
        sel[n, n * 128:(n + 1) * 128] = 1
    return sel



STAGE_IO = {
    "max":   ({"x_in"}, {"mxout"}),
    "A":     ({"x_in", "g1_in", "w_up", "w_gate"},
              {"sp_up", "sp_gate", "mxout"}),
    "B":     ({"sp_gate", "g23_in"}, {"sp_P", "mxout"}),
    "C":     ({"sp_up", "sp_P", "g23_in", "g4_in"}, {"sp_o", "mxout"}),
    "D":     ({"x_in", "sp_o", "w_down", "g23_in", "g4_in", "g5_in"},
              {"x_out"}),
    "mamba": ({"x_in", "W_in", "W_x", "W_dt", "W_out", "conv_wt", "conv_b",
               "b_dt", "Dvec", "A_log_t", "sel16"}, {"x_out"}),
}


def build_program(T, mixer="mamba", stop_after=None, ncores=8, trunc=None,
                  stage=None):
    nc = bass.Bass("TRN2", target_bir_lowering=False)
    ctx = Ctx(nc)
    p = Prog(nc)
    NT = T // 128
    NCH = T // 512

    dram = {}
    for nm, shape, dt, kind in [
        ("x_in", [T, D], F32, "ExternalInput"),
        ("x_out", [T, D], F32, "ExternalOutput"),
        ("W_in", [D, 2 * DI], F32, "ExternalInput"),
        ("W_x", [DI, DTR + 2 * NS], F32, "ExternalInput"),
        ("W_dt", [DTR, DI], F32, "ExternalInput"),
        ("W_out", [DI, D], F32, "ExternalInput"),
        ("w_gate", [IF, D], F32, "ExternalInput"),
        ("w_up", [IF, D], F32, "ExternalInput"),
        ("w_down", [D, IF], F32, "ExternalInput"),
        ("conv_wt", [128, 64], F32, "ExternalInput"),
        ("conv_b", [128, 16], F32, "ExternalInput"),
        ("b_dt", [128, 16], F32, "ExternalInput"),
        ("Dvec", [128, 16], F32, "ExternalInput"),
        ("A_log_t", [128, 16 * NS], F32, "ExternalInput"),
        ("ident", [128, 128], BF16, "ExternalInput"),
        ("ones_row", [1, 128], F32, "ExternalInput"),
        ("sel16", [16, 16 * 128], BF16, "ExternalInput"),
        ("xres", [T, D], F32, "Internal"),
        ("sp_up", [IF, T], F32, "Internal"),
        ("sp_gate", [IF, T], F32, "Internal"),
        ("sp_P", [IF, T], F32, "Internal"),
        ("sp_o", [IF, T], F32, "Internal"),
        ("sp_z", [DI, T], F32, "Internal"),
        ("wupT", [D, IF], BF16, "Internal"),
        ("wgateT", [D, IF], BF16, "Internal"),
        ("wdownT", [IF, D], BF16, "Internal"),
        ("winb", [D, 2 * DI], BF16, "Internal"),
        ("omixd", [DI, T], BF16, "Internal"),
        ("woutb", [DI, D], BF16, "Internal"),
        ("cc_in", [1, 2], F32, "Internal"),
        ("cc_out", [ncores, 2], F32, "Internal"),
        ("mxout", [1, 2], F32, "Internal"),
        ("g1_in", [ncores, 2], F32, "Internal"),
        ("g23_in", [ncores, 2], F32, "Internal"),
        ("g4_in", [ncores, 2], F32, "Internal"),
        ("g5_in", [ncores, 2], F32, "Internal"),
    ]:
        if stage is not None:
            ins_, outs_ = STAGE_IO[stage]
            base = {"ident", "ones_row"}
            if nm in ins_ or nm in base:
                kind = "ExternalInput"
            elif nm in outs_:
                kind = "ExternalOutput"
            else:
                kind = "Internal"
        dram[nm] = nc.dram_tensor(nm, shape, dt, kind=kind)

    ident_b = ctx.sbuf("identb", [128, 128], BF16)
    ones_f = ctx.sbuf("onesf", [1, 128], F32)
    p.dma("sync", lambda e: e.dma_start(out=ident_b[:], in_=dram["ident"].ap()),
          writes=[ident_b])
    p.dma("sync", lambda e: e.dma_start(out=ones_f[:], in_=dram["ones_row"].ap()),
          writes=[ones_f])
    stat_m = ctx.sbuf("stat_m", [128, NT])
    stat_r = ctx.sbuf("stat_r", [128, NT])
    eps_b = ctx.sbuf("eps_b", [128, 1])
    p.op("vector", lambda e: e.memset(eps_b[:], 1e-5), writes=[eps_b])

    H = dict(ctx=ctx, p=p, nc=nc, dram=dram, ident_b=ident_b, ones_f=ones_f,
             stat_m=stat_m, stat_r=stat_r, eps_b=eps_b, T=T, NT=NT, NCH=NCH,
             ncores=ncores, stage=stage)

    if stage is not None:
        if stage == "max":
            ffn_call(H, 0, x_src=dram["x_in"], x_dst=None, stages={"max"})
        elif stage in ("A", "B", "C", "D"):
            ctx.push()
            prep_ffn_weights(H, which={
                "A": ("wupT", "wgateT"), "B": (), "C": (),
                "D": ("wdownT",)}[stage])
            ctx.pop()
            p.barrier()
            ctx.push()
            ffn_call(H, 0, x_src=dram["x_in"], x_dst=dram["x_out"],
                     stages={stage})
            ctx.pop()
        elif stage == "mamba":
            ctx.push()
            mamba(H, x_src=dram["x_in"], x_dst=dram["x_out"])
            ctx.pop()
        return _finish(nc, ctx, p, trunc)

    ctx.push()
    prep_ffn_weights(H)
    ctx.pop()
    p.barrier()

    ctx.push()
    ffn_call(H, 0, x_src=dram["x_in"], x_dst=dram["xres"])
    ctx.pop()
    p.barrier()

    if stop_after != "ffn1":
        ctx.push()
        if mixer == "mamba":
            mamba(H, x_src=dram["xres"], x_dst=dram["xres"])
        else:
            mixer_copy(H, x_src=dram["xres"], x_dst=dram["xres"])
        ctx.pop()
        p.barrier()

        ctx.push()
        ffn_call(H, 1, x_src=dram["xres"], x_dst=dram["x_out"])
        ctx.pop()
    else:
        p.barrier()
        mixer_copy(H, x_src=dram["xres"], x_dst=dram["x_out"])

    return _finish(nc, ctx, p, trunc)


def _finish(nc, ctx, p, trunc=None):
    if trunc is not None:
        p.ops = p.ops[:trunc]
        p.deps = p.deps[:trunc]
    NSLOTS = {"sync": 6, "scalar": 6, "gpsimd": 4, "vector": 3, "tensor": 3}
    with nc.Block() as block:
        semctx = []
        semlists = {}
        for e in ENGINES:
            semlists[e] = []
            for k in range(NSLOTS[e]):
                g = nc.semaphore(f"s_{e}{k}")
                semlists[e].append(g.__enter__())
                semctx.append(g)
        totals = p.emit(block, semlists)

        @block.gpsimd
        def _(eng):
            for e in ENGINES:
                for k in range(NSLOTS[e]):
                    if totals[e][k]:
                        eng.wait_ge(semlists[e][k], totals[e][k])

        for g in reversed(semctx):
            g.__exit__(None, None, None)

    while len(ctx.stack) > 1:
        ctx.pop()
    for g in reversed(ctx.stack[0]):
        g.__exit__(None, None, None)
    ctx.stack[0] = []
    return nc



def round_half_up(H, y, i32, vf, gg, pre_fn, pre_reads):
    p = H["p"]
    p.op("vector", lambda e: pre_fn(e, y), reads=pre_reads, writes=[y])
    p.op("vector", lambda e: e.tensor_copy(i32[:], y[:]), reads=[y], writes=[i32])
    p.op("vector", lambda e: e.tensor_copy(vf[:], i32[:]), reads=[i32], writes=[vf])
    p.op("vector", lambda e: e.tensor_tensor(gg[:], vf[:], y[:], ALU.is_gt),
         reads=[vf, y], writes=[gg])
    p.op("vector", lambda e: e.tensor_tensor(vf[:], vf[:], gg[:], ALU.subtract),
         reads=[vf, gg], writes=[vf])


def scalar_bcast(H, tag, src_ap, src_buf, pbc, dst):
    p = H["p"]
    p.op("tensor", lambda e: e.matmul(pbc[:, 0:1], H["ones_f"][:], src_ap,
                                      start=True, stop=True),
         reads=[src_buf, H["ones_f"]], writes=[pbc])
    p.op("scalar", lambda e: e.activation(dst[:], pbc[:, 0:1], AF.Copy,
                                          bias=0.0, scale=1.0),
         reads=[pbc], writes=[dst])


def reduce_acc(H, tag, acc, nvals):
    ctx, p = H["ctx"], H["p"]
    row = ctx.sbuf(tag + "_row", [1, 256])
    mx = ctx.sbuf(tag + "_mx", [1, 2])
    for v in range(nvals):
        p.dma("sync", lambda e, v=v: e.dma_start(
            out=row[0:1, v * 128:(v + 1) * 128], in_=acc[:, v:v + 1]),
            reads=[acc], writes=[row])
    p.op("vector", lambda e: e.memset(mx[:], 0.0), writes=[mx])
    for v in range(nvals):
        p.op("vector", lambda e, v=v: e.tensor_reduce(
            mx[0:1, v:v + 1], row[0:1, v * 128:(v + 1) * 128], AX.X, ALU.max),
            reads=[row], writes=[mx])
    return mx


def export_mx(H, mx):
    p = H["p"]
    p.dma("sync", lambda e: e.dma_start(out=H["dram"]["mxout"].ap(), in_=mx[:]),
          reads=[mx], writes=["mxout"])


def consume_gmax(H, tag, name):
    ctx, p, dram = H["ctx"], H["p"], H["dram"]
    nr = H["ncores"]
    gt = ctx.sbuf(tag + "_g", [1, 16])
    p.dma("sync", lambda e: e.dma_start(
        out=gt[0:1, 0:2 * nr], in_=dram[name].ap()),
        reads=[name], writes=[gt])
    gm = ctx.sbuf(tag + "_gm", [1, 2])
    p.op("vector", lambda e: e.memset(gm[:], 0.0), writes=[gm])
    for v in range(2):
        p.op("vector", lambda e, v=v: e.tensor_reduce(
            gm[0:1, v:v + 1],
            bass.AP(gt, v, [[16, 1], [2, nr]]),
            AX.X, ALU.max),
            reads=[gt], writes=[gm])
    return gm


def global_max(H, tag, acc, nvals, pbc):
    ctx, p, dram = H["ctx"], H["p"], H["dram"]
    mx = reduce_acc(H, tag, acc, nvals)
    p.dma("sync", lambda e: e.dma_start(out=dram["cc_in"].ap(), in_=mx[:]),
          reads=[mx], writes=["cc_in"])
    p.dma("gpsimd", lambda e: e.collective_compute(
        "AllGather", ALU.bypass, [list(range(H["ncores"]))],
        [dram["cc_in"].ap()], [dram["cc_out"].ap()]),
        reads=["cc_in"], writes=["cc_out"])
    gt = ctx.sbuf(tag + "_g", [1, 16])
    nr0 = H["ncores"]
    p.dma("sync", lambda e: e.dma_start(
        out=gt[0:1, 0:2 * nr0], in_=dram["cc_out"].ap()[0:nr0, :]),
        reads=["cc_out"], writes=[gt])
    gm = ctx.sbuf(tag + "_gm", [1, 2])
    p.op("vector", lambda e: e.memset(gm[:], 0.0), writes=[gm])
    nr = H["ncores"]
    for v in range(nvals):
        p.op("vector", lambda e, v=v: e.tensor_reduce(
            gm[0:1, v:v + 1],
            bass.AP(gt, v, [[16, 1], [2, nr]]),
            AX.X, ALU.max),
            reads=[gt], writes=[gm])
    return gm


def quant_scales(H, tag, gm, col, pbc):
    ctx, p = H["ctx"], H["p"]
    sv = ctx.sbuf(tag + "_sv", [1, 1])
    rv = ctx.sbuf(tag + "_rv", [1, 1])
    p.op("vector", lambda e: e.tensor_scalar(
        sv[:], gm[0:1, col:col + 1], 1.0 / QMAX, 1e-12, ALU.mult, ALU.max),
        reads=[gm], writes=[sv])
    p.op("vector", lambda e: e.reciprocal(rv[:], sv[:]), reads=[sv], writes=[rv])
    sbc = ctx.sbuf(tag + "_sb", [128, 1])
    rbc = ctx.sbuf(tag + "_rb", [128, 1])
    scalar_bcast(H, tag + "s", sv[0:1, 0:1], sv, pbc, sbc)
    scalar_bcast(H, tag + "r", rv[0:1, 0:1], rv, pbc, rbc)
    return sbc, rbc, sv



def prep_ffn_weights(H, which=("wupT", "wgateT", "wdownT")):
    ctx, p, dram = H["ctx"], H["p"], H["dram"]
    ident_b = H["ident_b"]
    IMAX = IF
    wt = ctx.sbuf("pw_w", [128, IMAX])
    bm = ctx.sbuf("pw_bm", [128, IMAX // 32])
    rr = ctx.sbuf("pw_r", [128, IMAX // 32])
    y = ctx.sbuf("pw_y", [128, IMAX])
    i32 = ctx.sbuf("pw_i", [128, IMAX], I32)
    gg = ctx.sbuf("pw_g", [128, IMAX])
    q = ctx.sbuf("pw_q", [128, IMAX])
    wb = ctx.sbuf("pw_b", [128, IMAX], BF16)
    tp = [ctx.psum("pw_tp", [128, 128], BF16) for _ in range(2)]
    tps = [ctx.sbuf("pw_tps", [128, 128], BF16) for _ in range(2)]

    for (srcn, dstn, O, I) in (("w_up", "wupT", IF, D),
                               ("w_gate", "wgateT", IF, D),
                               ("w_down", "wdownT", D, IF)):
        if dstn not in which:
            continue
        src, dst = dram[srcn], dram[dstn]
        nb = I // 32
        for it in range(O // 128):
            p.dma("sync", lambda e, it=it, src=src, I=I: e.dma_start(
                out=wt[:, 0:I], in_=src.ap()[it * 128:(it + 1) * 128, :]),
                writes=[wt])
            p.op("vector", lambda e, I=I, nb=nb: e.tensor_reduce(
                bm[:, 0:nb], bass.AP(wt, 0, [[IMAX, 128], [32, nb], [1, 32]]),
                AX.X, ALU.max, apply_absolute_value=True),
                reads=[wt], writes=[bm])
            p.op("vector", lambda e, nb=nb: e.tensor_scalar(
                bm[:, 0:nb], bm[:, 0:nb], 1.0 / 7.0, 1e-12, ALU.mult, ALU.max),
                reads=[bm], writes=[bm])
            p.op("vector", lambda e, nb=nb: e.reciprocal(rr[:, 0:nb], bm[:, 0:nb]),
                 reads=[bm], writes=[rr])
            p.op("vector", lambda e, I=I, nb=nb: e.tensor_tensor(
                y[:, 0:I], wt[:, 0:I],
                bass.AP(rr, 0, [[IMAX // 32, 128], [1, nb], [0, 32]]),
                ALU.mult), reads=[wt, rr], writes=[y])
            p.op("vector", lambda e, I=I: e.tensor_scalar(
                y[:, 0:I], y[:, 0:I], 0.5, None, ALU.add),
                reads=[y], writes=[y])
            p.op("vector", lambda e, I=I: e.tensor_copy(i32[:, 0:I], y[:, 0:I]),
                 reads=[y], writes=[i32])
            p.op("vector", lambda e, I=I: e.tensor_copy(q[:, 0:I], i32[:, 0:I]),
                 reads=[i32], writes=[q])
            p.op("vector", lambda e, I=I: e.tensor_tensor(
                gg[:, 0:I], q[:, 0:I], y[:, 0:I], ALU.is_gt),
                reads=[q, y], writes=[gg])
            p.op("vector", lambda e, I=I: e.tensor_tensor(
                q[:, 0:I], q[:, 0:I], gg[:, 0:I], ALU.subtract),
                reads=[q, gg], writes=[q])
            p.op("vector", lambda e, I=I, nb=nb: e.tensor_tensor(
                wb[:, 0:I], q[:, 0:I],
                bass.AP(bm, 0, [[IMAX // 32, 128], [1, nb], [0, 32]]),
                ALU.mult), reads=[q, bm], writes=[wb])
            for jb in range(I // 128):
                tpx = tp[jb % 2]
                tsx = tps[jb % 2]
                p.op("tensor", lambda e, jb=jb, tpx=tpx: e.transpose(
                    tpx[:], wb[:, jb * 128:(jb + 1) * 128], ident_b[:]),
                    reads=[wb, ident_b], writes=[tpx])
                p.op("scalar", lambda e, tpx=tpx, tsx=tsx: e.activation(
                    tsx[:], tpx[:], AF.Copy, bias=0.0, scale=1.0),
                    reads=[tpx], writes=[tsx])
                p.dma("scalar", lambda e, tsx=tsx, dst=dst, it=it, jb=jb:
                      e.dma_start(out=dst.ap()[jb * 128:(jb + 1) * 128,
                                               it * 128:(it + 1) * 128],
                                  in_=tsx[:]),
                      reads=[tsx], writes=[f"{dstn}_{it}_{jb}"])



def ffn_call(H, call_id, x_src, x_dst, stages=("max", "A", "B", "C", "D")):
    ctx, p, dram = H["ctx"], H["p"], H["dram"]
    T, NT, NCH = H["T"], H["NT"], H["NCH"]
    stat_m, stat_r = H["stat_m"], H["stat_r"]
    ident_b = H["ident_b"]
    fused = H.get("stage") is None
    cid = f"f{call_id}"
    NIT = IF // 128
    sp_up, sp_gate = dram["sp_up"], dram["sp_gate"]
    sp_P, sp_o = dram["sp_P"], dram["sp_o"]

    def xsrc_r(tb):
        return f"xdst_{x_src.name}_{tb}"

    mmp = [ctx.psum(cid + "_mm", [128, 512]) for _ in range(2)]
    dps = [ctx.psum(cid + "_dp", [128, 512]) for _ in range(2)]
    tpp = [ctx.psum(cid + "_tp", [128, 128], BF16) for _ in range(2)]
    pbc = ctx.psum(cid + "_bc", [128, 8])

    xt = [ctx.sbuf(cid + "_mxx", [128, D]) for _ in range(2)]
    bn6 = ctx.sbuf(cid + "_bn6", [128, 12])
    bn2 = ctx.sbuf(cid + "_bn2", [128, 2])
    rmx = ctx.sbuf(cid + "_rmx", [128, 1])
    rmn = ctx.sbuf(cid + "_rmn", [128, 1])
    acc1 = ctx.sbuf(cid + "_acc1", [128, 2])
    vtmp = ctx.sbuf(cid + "_vt", [128, 1])
    if "max" in stages:
        p.op("vector", lambda e: e.memset(acc1[:], 0.0), writes=[acc1])
    for tb in (range(NT) if "max" in stages else ()):
        t = xt[tb % 2]
        p.dma("sync", lambda e, tb=tb, t=t: e.dma_start(
            out=t[:], in_=x_src.ap()[tb * 128:(tb + 1) * 128, :]),
            reads=[xsrc_r(tb)], writes=[t])
        p.op("vector", lambda e, t=t: e.bn_stats(bn6[:, 0:6], t[:, 0:512]),
             reads=[t], writes=[bn6])
        p.op("vector", lambda e, t=t: e.bn_stats(bn6[:, 6:12], t[:, 512:1024]),
             reads=[t], writes=[bn6])
        p.op("vector", lambda e: e.bn_aggr(bn2[:], bn6[:]),
             reads=[bn6], writes=[bn2])
        p.op("vector", lambda e, tb=tb: e.tensor_copy(
            stat_m[:, tb:tb + 1], bn2[:, 0:1]), reads=[bn2], writes=[stat_m])
        p.op("scalar", lambda e: e.activation(vtmp[:], bn2[:, 1:2], AF.Ln,
                                              bias=H["eps_b"][:], scale=1.0),
             reads=[bn2, H["eps_b"]], writes=[vtmp])
        p.op("scalar", lambda e, tb=tb: e.activation(
            stat_r[:, tb:tb + 1], vtmp[:], AF.Exp, bias=0.0, scale=-0.5),
            reads=[vtmp], writes=[stat_r])
        p.op("vector", lambda e, t=t: e.tensor_reduce(
            rmx[:], t[:], AX.X, ALU.max), reads=[t], writes=[rmx])
        p.op("vector", lambda e, t=t: e.tensor_reduce(
            rmn[:], t[:], AX.X, ALU.max, negate=True),
            reads=[t], writes=[rmn])
        p.op("vector", lambda e, tb=tb: e.tensor_tensor(
            rmx[:], rmx[:], stat_m[:, tb:tb + 1], ALU.subtract),
            reads=[rmx, stat_m], writes=[rmx])
        p.op("vector", lambda e, tb=tb: e.tensor_tensor(
            rmn[:], rmn[:], stat_m[:, tb:tb + 1], ALU.add),
            reads=[rmn, stat_m], writes=[rmn])
        p.op("vector", lambda e: e.tensor_tensor(
            rmx[:], rmx[:], rmn[:], ALU.max), reads=[rmx, rmn], writes=[rmx])
        p.op("vector", lambda e, tb=tb: e.tensor_tensor(
            rmx[:], rmx[:], stat_r[:, tb:tb + 1], ALU.mult),
            reads=[rmx, stat_r], writes=[rmx])
        p.op("vector", lambda e: e.tensor_tensor(
            acc1[:, 0:1], acc1[:, 0:1], rmx[:], ALU.max),
            reads=[acc1, rmx], writes=[acc1])

    if "max" in stages and not fused:
        export_mx(H, reduce_acc(H, cid + "g1", acc1, 1))
    if "A" in stages:
        if fused:
            gm1 = global_max(H, cid + "g1", acc1, 1, pbc)
        else:
            gm1 = consume_gmax(H, cid + "g1", "g1_in")
        s1, r1, _ = quant_scales(H, cid + "q1", gm1, 0, pbc)

    amax2 = ctx.sbuf(cid + "_am2", [128, 2])
    rtmp = ctx.sbuf(cid + "_rt", [128, 1])
    ctx.push()
    doA = "A" in stages
    wup_s = [ctx.sbuf(cid + "_wu", [128, IF], BF16) for _ in range(8)]
    wgate_s = [ctx.sbuf(cid + "_wg", [128, IF], BF16) for _ in range(8)]
    for kb in (range(8) if doA else ()):
        p.dma("sync", lambda e, kb=kb: e.dma_start(
            out=wup_s[kb][:], in_=dram["wupT"].ap()[kb * 128:(kb + 1) * 128, :]),
            reads=[f"wupT_{mi}_{kb}" for mi in range(NIT)], writes=[wup_s[kb]])
        p.dma("sync", lambda e, kb=kb: e.dma_start(
            out=wgate_s[kb][:],
            in_=dram["wgateT"].ap()[kb * 128:(kb + 1) * 128, :]),
            reads=[f"wgateT_{mi}_{kb}" for mi in range(NIT)],
            writes=[wgate_s[kb]])

    xqT = [ctx.sbuf(cid + "_xqT", [128, 512], BF16) for _ in range(8)]
    ya = ctx.sbuf(cid + "_Ay", [128, D])
    ia = ctx.sbuf(cid + "_Ai", [128, D], I32)
    ga = ctx.sbuf(cid + "_Ag", [128, D])
    ixf = ctx.sbuf(cid + "_ixf", [128, D])
    ixb = ctx.sbuf(cid + "_ixb", [128, D], BF16)
    kvec = ctx.sbuf(cid + "_kv", [128, 1])
    bvec = ctx.sbuf(cid + "_bv", [128, 1])
    ug = [ctx.sbuf(cid + "_ug", [128, 512]) for _ in range(4)]
    if doA:
        p.op("vector", lambda e: e.memset(amax2[:], 0.0), writes=[amax2])

    for ch in (range(NCH) if doA else ()):
        for tb4 in range(4):
            tb = ch * 4 + tb4
            t = xt[tb % 2]
            p.dma("sync", lambda e, tb=tb, t=t: e.dma_start(
                out=t[:], in_=x_src.ap()[tb * 128:(tb + 1) * 128, :]),
                reads=[xsrc_r(tb)], writes=[t])
            p.op("vector", lambda e, t=t: e.bn_stats(bn6[:, 0:6], t[:, 0:512]),
                 reads=[t], writes=[bn6])
            p.op("vector", lambda e, t=t: e.bn_stats(bn6[:, 6:12],
                                                     t[:, 512:1024]),
                 reads=[t], writes=[bn6])
            p.op("vector", lambda e: e.bn_aggr(bn2[:], bn6[:]),
                 reads=[bn6], writes=[bn2])
            p.op("scalar", lambda e: e.activation(
                vtmp[:], bn2[:, 1:2], AF.Ln, bias=H["eps_b"][:], scale=1.0),
                reads=[bn2, H["eps_b"]], writes=[vtmp])
            p.op("scalar", lambda e: e.activation(
                rmx[:], vtmp[:], AF.Exp, bias=0.0, scale=-0.5),
                reads=[vtmp], writes=[rmx])
            p.op("vector", lambda e: e.tensor_tensor(
                kvec[:], rmx[:], r1[:], ALU.mult),
                reads=[rmx, r1], writes=[kvec])
            p.op("vector", lambda e: e.tensor_tensor(
                bvec[:], bn2[:, 0:1], kvec[:], ALU.mult),
                reads=[bn2, kvec], writes=[bvec])
            p.op("vector", lambda e: e.tensor_scalar(
                bvec[:], bvec[:], -1.0, 0.5, ALU.mult, ALU.add),
                reads=[bvec], writes=[bvec])

            def pre(e, yy, t=t):
                return e.tensor_scalar(yy[:], t[:], kvec[:], bvec[:],
                                       ALU.mult, ALU.add)

            round_half_up(H, ya, ia, ixf, ga, pre, [t, kvec, bvec])
            p.op("vector", lambda e: e.tensor_copy(ixb[:], ixf[:]),
                 reads=[ixf], writes=[ixb])
            for db in range(8):
                tpx = tpp[db % 2]
                p.op("tensor", lambda e, db=db, tpx=tpx: e.transpose(
                    tpx[:], ixb[:, db * 128:(db + 1) * 128], ident_b[:]),
                    reads=[ixb, ident_b], writes=[tpx])
                p.op("scalar", lambda e, db=db, tb4=tb4, tpx=tpx: e.activation(
                    xqT[db][:, tb4 * 128:(tb4 + 1) * 128], tpx[:], AF.Copy,
                    bias=0.0, scale=1.0),
                    reads=[tpx], writes=[xqT[db]])
        for wi, (ws, spd) in enumerate(((wup_s, sp_up), (wgate_s, sp_gate))):
            for mi in range(NIT):
                pmm = mmp[mi % 2]
                for kb in range(8):
                    p.op("tensor", lambda e, wi=wi, mi=mi, kb=kb, pmm=pmm,
                         ws=ws: e.matmul(
                             pmm[:], ws[kb][:, mi * 128:(mi + 1) * 128],
                             xqT[kb][:], start=(kb == 0), stop=(kb == 7)),
                         reads=[ws[kb], xqT[kb]], writes=[pmm])
                ut = ug[(wi * 2 + mi) % 4]
                p.op("scalar", lambda e, pmm=pmm, ut=ut: e.activation(
                    ut[:], pmm[:], AF.Copy, bias=0.0, scale=s1[:]),
                    reads=[pmm, s1], writes=[ut])
                p.op("vector", lambda e, ut=ut: e.tensor_reduce(
                    rtmp[:], ut[:], AX.X, ALU.max, apply_absolute_value=True),
                    reads=[ut], writes=[rtmp])
                p.op("vector", lambda e, wi=wi: e.tensor_tensor(
                    amax2[:, wi:wi + 1], amax2[:, wi:wi + 1], rtmp[:],
                    ALU.max), reads=[amax2, rtmp], writes=[amax2])
                p.dma("sync", lambda e, spd=spd, mi=mi, ch=ch, ut=ut:
                      e.dma_start(out=spd.ap()[mi * 128:(mi + 1) * 128,
                                               ch * 512:(ch + 1) * 512],
                                  in_=ut[:]),
                      reads=[ut], writes=[f"{spd.name}_{ch}_{mi}"])

    ctx.pop()
    if doA and not fused:
        export_mx(H, reduce_acc(H, cid + "g23", amax2, 2))
    doB = "B" in stages
    doC = "C" in stages
    doD = "D" in stages
    if fused:
        gm23 = global_max(H, cid + "g23", amax2, 2, pbc)
    elif doB or doC or doD:
        gm23 = consume_gmax(H, cid + "g23", "g23_in")
    if fused or doC or doD:
        s2, r2, s2v = quant_scales(H, cid + "q2", gm23, 0, pbc)
    if fused or doB or doC or doD:
        s3, r3, s3v = quant_scales(H, cid + "q3", gm23, 1, pbc)

    gtl = [ctx.sbuf(cid + "_Bg", [128, 512]) for _ in range(2)]
    yb = ctx.sbuf(cid + "_By", [128, 512])
    ib = ctx.sbuf(cid + "_Bi", [128, 512], I32)
    gb = ctx.sbuf(cid + "_Bgg", [128, 512])
    i3f = ctx.sbuf(cid + "_i3f", [128, 512])
    sig = ctx.sbuf(cid + "_sig", [128, 512])
    jf = ctx.sbuf(cid + "_jf", [128, 512])
    ppt = ctx.sbuf(cid + "_pp", [128, 512])
    amax4 = ctx.sbuf(cid + "_am4", [128, 2])
    if doB:
        p.op("vector", lambda e: e.memset(amax4[:], 0.0), writes=[amax4])

    for ch in (range(NCH) if doB else ()):
        for mi in range(NIT):
            gt = gtl[mi % 2]
            p.dma("sync", lambda e, mi=mi, ch=ch, gt=gt: e.dma_start(
                out=gt[:], in_=sp_gate.ap()[mi * 128:(mi + 1) * 128,
                                            ch * 512:(ch + 1) * 512]),
                reads=[f"sp_gate_{ch}_{mi}"], writes=[gt])

            def preb(e, yy, gt=gt):
                return e.tensor_scalar(yy[:], gt[:], r3[:], 0.5,
                                       ALU.mult, ALU.add)

            round_half_up(H, yb, ib, i3f, gb, preb, [gt, r3])
            p.op("scalar", lambda e: e.activation(
                sig[:], i3f[:], AF.Sigmoid, bias=0.0, scale=s3[:]),
                reads=[i3f, s3], writes=[sig])

            def prej(e, yy):
                return e.tensor_scalar(yy[:], sig[:], 65536.0, 0.5,
                                       ALU.mult, ALU.add)

            round_half_up(H, yb, ib, jf, gb, prej, [sig])
            p.op("vector", lambda e: e.tensor_scalar(
                jf[:], jf[:], 65535.0, None, ALU.min), reads=[jf], writes=[jf])
            p.op("vector", lambda e: e.tensor_tensor(
                ppt[:], i3f[:], jf[:], ALU.mult), reads=[i3f, jf], writes=[ppt])
            p.op("vector", lambda e: e.tensor_reduce(
                rtmp[:], ppt[:], AX.X, ALU.max, apply_absolute_value=True),
                reads=[ppt], writes=[rtmp])
            p.op("vector", lambda e: e.tensor_tensor(
                amax4[:, 0:1], amax4[:, 0:1], rtmp[:], ALU.max),
                reads=[amax4, rtmp], writes=[amax4])
            p.dma("sync", lambda e, mi=mi, ch=ch: e.dma_start(
                out=sp_P.ap()[mi * 128:(mi + 1) * 128,
                              ch * 512:(ch + 1) * 512], in_=ppt[:]),
                reads=[ppt], writes=[f"sp_P_{ch}_{mi}"])

    if doB and not fused:
        export_mx(H, reduce_acc(H, cid + "g4", amax4, 1))
    if fused:
        gm4 = global_max(H, cid + "g4", amax4, 1, pbc)
    elif doC or doD:
        gm4 = consume_gmax(H, cid + "g4", "g4_in")
    if fused or doC or doD:
        s3f = ctx.sbuf(cid + "_s3f", [1, 1])
        s4v = ctx.sbuf(cid + "_s4v", [1, 1])
        c4v = ctx.sbuf(cid + "_c4v", [1, 1])
        p.op("vector", lambda e: e.tensor_scalar(
            s3f[:], s3v[:], 1.0 / 65536.0, None, ALU.mult),
            reads=[s3v], writes=[s3f])
        p.op("vector", lambda e: e.tensor_tensor(
            s4v[:], gm4[0:1, 0:1], s3f[:], ALU.mult),
            reads=[gm4, s3f], writes=[s4v])
        p.op("vector", lambda e: e.tensor_scalar(
            s4v[:], s4v[:], 1.0 / QMAX, 1e-12, ALU.mult, ALU.max),
            reads=[s4v], writes=[s4v])
        p.op("vector", lambda e: e.reciprocal(c4v[:], s4v[:]),
             reads=[s4v], writes=[c4v])
        p.op("vector", lambda e: e.tensor_tensor(
            c4v[:], c4v[:], s3f[:], ALU.mult), reads=[c4v, s3f], writes=[c4v])
        c4 = ctx.sbuf(cid + "_c4b", [128, 1])
        scalar_bcast(H, cid + "c4", c4v[0:1, 0:1], c4v, pbc, c4)

    upl = [ctx.sbuf(cid + "_Cu", [128, 512]) for _ in range(2)]
    ptl = [ctx.sbuf(cid + "_Cp", [128, 512]) for _ in range(2)]
    i4f = ctx.sbuf(cid + "_i4f", [128, 512])
    iuf = ctx.sbuf(cid + "_iuf", [128, 512])
    oo = ctx.sbuf(cid + "_oo", [128, 512])
    amax5 = ctx.sbuf(cid + "_am5", [128, 2])
    if doC:
        p.op("vector", lambda e: e.memset(amax5[:], 0.0), writes=[amax5])

    for ch in (range(NCH) if doC else ()):
        for mi in range(NIT):
            ut = upl[mi % 2]
            pt = ptl[mi % 2]
            p.dma("sync", lambda e, mi=mi, ch=ch, ut=ut: e.dma_start(
                out=ut[:], in_=sp_up.ap()[mi * 128:(mi + 1) * 128,
                                          ch * 512:(ch + 1) * 512]),
                reads=[f"sp_up_{ch}_{mi}"], writes=[ut])
            p.dma("scalar", lambda e, mi=mi, ch=ch, pt=pt: e.dma_start(
                out=pt[:], in_=sp_P.ap()[mi * 128:(mi + 1) * 128,
                                         ch * 512:(ch + 1) * 512]),
                reads=[f"sp_P_{ch}_{mi}"], writes=[pt])

            def prec4(e, yy, pt=pt):
                return e.tensor_scalar(yy[:], pt[:], c4[:], 0.5,
                                       ALU.mult, ALU.add)

            round_half_up(H, yb, ib, i4f, gb, prec4, [pt, c4])

            def prec2(e, yy, ut=ut):
                return e.tensor_scalar(yy[:], ut[:], r2[:], 0.5,
                                       ALU.mult, ALU.add)

            round_half_up(H, yb, ib, iuf, gb, prec2, [ut, r2])
            p.op("vector", lambda e: e.tensor_tensor(
                oo[:], i4f[:], iuf[:], ALU.mult), reads=[i4f, iuf], writes=[oo])
            p.op("vector", lambda e: e.tensor_reduce(
                rtmp[:], oo[:], AX.X, ALU.max, apply_absolute_value=True),
                reads=[oo], writes=[rtmp])
            p.op("vector", lambda e: e.tensor_tensor(
                amax5[:, 0:1], amax5[:, 0:1], rtmp[:], ALU.max),
                reads=[amax5, rtmp], writes=[amax5])
            p.dma("sync", lambda e, mi=mi, ch=ch: e.dma_start(
                out=sp_o.ap()[mi * 128:(mi + 1) * 128,
                              ch * 512:(ch + 1) * 512], in_=oo[:]),
                reads=[oo], writes=[f"sp_o_{ch}_{mi}"])

    if doC and not fused:
        export_mx(H, reduce_acc(H, cid + "g5", amax5, 1))
    if fused:
        gm5 = global_max(H, cid + "g5", amax5, 1, pbc)
    elif doD:
        gm5 = consume_gmax(H, cid + "g5", "g5_in")
    if fused or doD:
        s42 = ctx.sbuf(cid + "_s42", [1, 1])
        s5v = ctx.sbuf(cid + "_s5v", [1, 1])
        c5v = ctx.sbuf(cid + "_c5v", [1, 1])
        sc5v = ctx.sbuf(cid + "_sc5v", [1, 1])
        p.op("vector", lambda e: e.tensor_tensor(
            s42[:], s4v[:], s2v[:], ALU.mult), reads=[s4v, s2v], writes=[s42])
        p.op("vector", lambda e: e.tensor_tensor(
            s5v[:], gm5[0:1, 0:1], s42[:], ALU.mult),
            reads=[gm5, s42], writes=[s5v])
        p.op("vector", lambda e: e.tensor_scalar(
            s5v[:], s5v[:], 1.0 / QMAX, 1e-12, ALU.mult, ALU.max),
            reads=[s5v], writes=[s5v])
        p.op("vector", lambda e: e.reciprocal(c5v[:], s5v[:]),
             reads=[s5v], writes=[c5v])
        p.op("vector", lambda e: e.tensor_tensor(
            c5v[:], c5v[:], s42[:], ALU.mult), reads=[c5v, s42], writes=[c5v])
        p.op("vector", lambda e: e.tensor_scalar(
            sc5v[:], s5v[:], 0.5, None, ALU.mult), reads=[s5v], writes=[sc5v])
        c5 = ctx.sbuf(cid + "_c5b", [128, 1])
        sc5 = ctx.sbuf(cid + "_sc5b", [128, 1])
        scalar_bcast(H, cid + "c5", c5v[0:1, 0:1], c5v, pbc, c5)
        scalar_bcast(H, cid + "sc5", sc5v[0:1, 0:1], sc5v, pbc, sc5)

    ctx.push()
    wd_s = [ctx.sbuf(cid + "_wd", [128, D], BF16) for _ in range(NIT)]
    for mi in (range(NIT) if doD else ()):
        p.dma("sync", lambda e, mi=mi: e.dma_start(
            out=wd_s[mi][:], in_=dram["wdownT"].ap()[mi * 128:(mi + 1) * 128, :]),
            reads=[f"wdownT_{it}_{mi}" for it in range(D // 128)],
            writes=[wd_s[mi]])
    otl = [ctx.sbuf(cid + "_Do", [128, 512]) for _ in range(2)]
    i5b = [ctx.sbuf(cid + "_i5b", [128, 512], BF16) for _ in range(NIT)]
    i5f = ctx.sbuf(cid + "_i5f", [128, 512])
    xr = [ctx.sbuf(cid + "_Dx", [128, D]) for _ in range(2)]
    xo = [ctx.sbuf(cid + "_Dxo", [128, D]) for _ in range(2)]

    for ch in (range(NCH) if doD else ()):
        for mi in range(NIT):
            ot = otl[mi % 2]
            p.dma("sync", lambda e, mi=mi, ch=ch, ot=ot: e.dma_start(
                out=ot[:], in_=sp_o.ap()[mi * 128:(mi + 1) * 128,
                                         ch * 512:(ch + 1) * 512]),
                reads=[f"sp_o_{ch}_{mi}"], writes=[ot])

            def prec5(e, yy, ot=ot):
                return e.tensor_scalar(yy[:], ot[:], c5[:], 0.5,
                                       ALU.mult, ALU.add)

            round_half_up(H, yb, ib, i5f, gb, prec5, [ot, c5])
            p.op("vector", lambda e, mi=mi: e.tensor_copy(i5b[mi][:], i5f[:]),
                 reads=[i5f], writes=[i5b[mi]])
        for tb4 in range(4):
            tb = ch * 4 + tb4
            xrt = xr[tb % 2]
            xot = xo[tb % 2]
            p.dma("sync", lambda e, tb=tb, xrt=xrt: e.dma_start(
                out=xrt[:], in_=x_src.ap()[tb * 128:(tb + 1) * 128, :]),
                reads=[xsrc_r(tb)], writes=[xrt])
            for dh in range(2):
                pd = dps[dh]
                for mi in range(NIT):
                    p.op("tensor", lambda e, mi=mi, tb4=tb4, dh=dh, pd=pd:
                         e.matmul(pd[:],
                                  i5b[mi][:, tb4 * 128:(tb4 + 1) * 128],
                                  wd_s[mi][:, dh * 512:(dh + 1) * 512],
                                  start=(mi == 0), stop=(mi == NIT - 1)),
                         reads=[i5b[mi], wd_s[mi]], writes=[pd])
                p.op("vector", lambda e, dh=dh, xrt=xrt, xot=xot, pd=pd:
                     e.scalar_tensor_tensor(
                         xot[:, dh * 512:(dh + 1) * 512], pd[:], sc5[:],
                         xrt[:, dh * 512:(dh + 1) * 512], ALU.mult, ALU.add),
                     reads=[pd, sc5, xrt], writes=[xot])
            p.dma("sync", lambda e, tb=tb, xot=xot: e.dma_start(
                out=x_dst.ap()[tb * 128:(tb + 1) * 128, :], in_=xot[:]),
                reads=[xot], writes=[f"xdst_{x_dst.name}_{tb}"])
    ctx.pop()



def mixer_copy(H, x_src, x_dst):
    ctx, p = H["ctx"], H["p"]
    T = H["T"]
    t = ctx.sbuf("mbcp", [128, D])
    for tb in range(T // 128):
        p.dma("sync", lambda e, tb=tb: e.dma_start(
            out=t[:], in_=x_src.ap()[tb * 128:(tb + 1) * 128, :]),
            reads=[f"xdst_{x_src.name}_{tb}"], writes=[t])
        p.dma("sync", lambda e, tb=tb: e.dma_start(
            out=x_dst.ap()[tb * 128:(tb + 1) * 128, :], in_=t[:]),
            reads=[t], writes=[f"xdst_{x_dst.name}_{tb}"])


def mamba(H, x_src, x_dst):
    ctx, p, dram = H["ctx"], H["p"], H["nc"] and H["dram"]
    p = H["p"]
    ctx = H["ctx"]
    T, NT = H["T"], H["NT"]
    NSC = T // 512
    ident_b = H["ident_b"]

    def xsrc_r(tb):
        return f"xdst_{x_src.name}_{tb}"

    ps = [ctx.psum("mb_ps", [128, 512]) for _ in range(6)]
    p96 = ctx.psum("mb_p96", [96, 512])
    tpp0 = ctx.psum("mb_tp", [128, 256], BF16)
    tpp = [tpp0[:, 0:128], tpp0[:, 128:256]]

    winb_d = dram["winb"]
    woutb_d = dram["woutb"]
    wx_b = ctx.sbuf("mb_wxb", [128, 16 * 96], BF16)
    wdt_b = ctx.sbuf("mb_wdtb", [64, DI], BF16)
    ctx.push()
    wtmp = ctx.sbuf("mb_wtmp", [128, 2 * DI])
    wtmpb = ctx.sbuf("mb_wtmpb", [128, 2 * DI], BF16)
    for kb in range(8):
        p.dma("sync", lambda e, kb=kb: e.dma_start(
            out=wtmp[:], in_=dram["W_in"].ap()[kb * 128:(kb + 1) * 128, :]),
            writes=[wtmp])
        p.op("vector", lambda e: e.tensor_copy(wtmpb[:], wtmp[:]),
             reads=[wtmp], writes=[wtmpb])
        p.dma("sync", lambda e, kb=kb: e.dma_start(
            out=winb_d.ap()[kb * 128:(kb + 1) * 128, :], in_=wtmpb[:]),
            reads=[wtmpb], writes=[f"winb_{kb}"])
    for kt in range(16):
        p.dma("sync", lambda e, kt=kt: e.dma_start(
            out=wtmp[:, 0:D], in_=dram["W_out"].ap()[kt * 128:(kt + 1) * 128, :]),
            writes=[wtmp])
        p.op("vector", lambda e: e.tensor_copy(wtmpb[:, 0:D], wtmp[:, 0:D]),
             reads=[wtmp], writes=[wtmpb])
        p.dma("sync", lambda e, kt=kt: e.dma_start(
            out=woutb_d.ap()[kt * 128:(kt + 1) * 128, :], in_=wtmpb[:, 0:D]),
            reads=[wtmpb], writes=[f"woutb_{kt}"])

    for kt in range(16):
        p.dma("sync", lambda e, kt=kt: e.dma_start(
            out=wtmp[:, 0:96], in_=dram["W_x"].ap()[kt * 128:(kt + 1) * 128, :]),
            writes=[wtmp])
        p.op("vector", lambda e, kt=kt: e.tensor_copy(
            wx_b[:, kt * 96:(kt + 1) * 96], wtmp[:, 0:96]),
            reads=[wtmp], writes=[wx_b])
    p.dma("sync", lambda e: e.dma_start(
        out=wtmp[0:64, 0:DI], in_=dram["W_dt"].ap()), writes=[wtmp])
    p.op("vector", lambda e: e.tensor_copy(wdt_b[:], wtmp[0:64, 0:DI]),
         reads=[wtmp], writes=[wdt_b])
    ctx.pop()
    p.barrier()
    win_c = [ctx.sbuf("mb_winc", [128, 8 * 128], BF16) for _ in range(2)]
    wout_s = [ctx.sbuf("mb_wouts", [128, D], BF16) for _ in range(2)]


    convw = ctx.sbuf("mb_convw", [128, 64])
    convb2 = ctx.sbuf("mb_convb2", [128, 16])
    bdt = ctx.sbuf("mb_bdt", [128, 16])
    dvec = ctx.sbuf("mb_dvec", [128, 16])
    a_t = ctx.sbuf("mb_at", [128, 256])
    p.dma("sync", lambda e: e.dma_start(out=convw[:], in_=dram["conv_wt"].ap()),
          writes=[convw])
    p.dma("sync", lambda e: e.dma_start(out=convb2[:], in_=dram["conv_b"].ap()),
          writes=[convb2])
    p.dma("sync", lambda e: e.dma_start(out=bdt[:], in_=dram["b_dt"].ap()),
          writes=[bdt])
    p.dma("sync", lambda e: e.dma_start(out=dvec[:], in_=dram["Dvec"].ap()),
          writes=[dvec])
    p.dma("sync", lambda e: e.dma_start(out=a_t[:], in_=dram["A_log_t"].ap()),
          writes=[a_t])
    p.op("scalar", lambda e: e.activation(a_t[:], a_t[:], AF.Exp,
                                          bias=0.0, scale=1.0),
         reads=[a_t], writes=[a_t])
    p.op("vector", lambda e: e.tensor_scalar(
        a_t[:], a_t[:], -1.0, None, ALU.mult), reads=[a_t], writes=[a_t])
    sel16 = ctx.sbuf("mb_sel", [16, 16 * 128], BF16)
    p.dma("sync", lambda e: e.dma_start(out=sel16[:], in_=dram["sel16"].ap()),
          writes=[sel16])

    h_state = ctx.sbuf("mb_h", [128, 256])
    xi_tail = ctx.sbuf("mb_tail", [128, 48])
    p.op("vector", lambda e: e.memset(h_state[:], 0.0),
         writes=[h_state] + [f"hst_{c}" for c in range(256)])
    p.op("vector", lambda e: e.memset(xi_tail[:], 0.0), writes=[xi_tail])

    xc2 = [ctx.sbuf("mb_x2", [128, D])]
    lnT = [ctx.sbuf("mb_lnT", [128, 512], BF16) for _ in range(8)]
    lnb = ctx.sbuf("mb_lnb", [128, D], BF16)
    mloc = ctx.sbuf("mb_m", [128, 1])
    rloc = ctx.sbuf("mb_r", [128, 1])
    bn6 = ctx.sbuf("mb_bn6", [128, 12])
    bn2 = ctx.sbuf("mb_bn2", [128, 2])
    vt = ctx.sbuf("mb_vt", [128, 1])
    xi = [ctx.sbuf("mb_xi", [128, 516]) for _ in range(2)]
    xc_f = [ctx.sbuf("mb_xc", [128, 512]) for _ in range(16)]
    xcb = [ctx.sbuf("mb_xcb", [128, 512], BF16) for _ in range(2)]
    del_f = [ctx.sbuf("mb_del", [128, 512]) for _ in range(16)]
    cvt = [ctx.sbuf("mb_cv", [128, 512]) for _ in range(2)]
    cv2 = [ctx.sbuf("mb_cv2", [128, 512]) for _ in range(2)]
    ztile = [ctx.sbuf("mb_zt", [128, 512])]
    xdbl = ctx.sbuf("mb_xdbl", [96, 512])
    dtb = ctx.sbuf("mb_dtb", [64, 512], BF16)
    bc32 = ctx.sbuf("mb_bc32", [32, 512])
    bhf32 = ctx.sbuf("mb_bhf32", [32, 512])
    bch32 = ctx.sbuf("mb_bch", [32, 512], BF16)
    bcl32 = ctx.sbuf("mb_bcl", [32, 512], BF16)
    bCh = ctx.sbuf("mb_bCh", [16, 512], BF16)
    bCl = ctx.sbuf("mb_bCl", [16, 512], BF16)
    u_t = [ctx.sbuf("mb_u", [128, 512])]
    da_s = [ctx.sbuf("mb_da", [128, 512]) for _ in range(2)]
    h_t = [ctx.sbuf("mb_ht", [128, 512]) for _ in range(2)]
    dbu = [ctx.sbuf("mb_dbu", [128, 512]) for _ in range(2)]
    pb_t = [ctx.sbuf("mb_pb", [128, 512]) for _ in range(2)]
    pbb_t = [ctx.sbuf("mb_pbb", [128, 512], BF16) for _ in range(2)]
    bcB = [ctx.sbuf("mb_bcB", [128, 512], BF16) for _ in range(16)]
    bcC = [ctx.sbuf("mb_bcC", [128, 512]) for _ in range(16)]
    w_t = [ctx.sbuf("mb_wt2", [128, 512])]
    sz_t = [ctx.sbuf("mb_sz", [128, 512])]
    omix = [ctx.sbuf("mb_om", [128, 512], BF16) for _ in range(2)]
    xo_t = [ctx.sbuf("mb_xo", [128, 512]) for _ in range(2)]

    for sc in range(NSC):
        t0 = sc * 512
        for tb4 in range(4):
            tb = sc * 4 + tb4
            xt2 = xc2[0]
            p.dma("sync", lambda e, tb=tb, xt2=xt2: e.dma_start(
                out=xt2[:], in_=x_src.ap()[tb * 128:(tb + 1) * 128, :]),
                reads=[xsrc_r(tb)], writes=[xt2])
            p.op("vector", lambda e, xt2=xt2: e.bn_stats(bn6[:, 0:6],
                                                         xt2[:, 0:512]),
                 reads=[xt2], writes=[bn6])
            p.op("vector", lambda e, xt2=xt2: e.bn_stats(bn6[:, 6:12],
                                                         xt2[:, 512:1024]),
                 reads=[xt2], writes=[bn6])
            p.op("vector", lambda e: e.bn_aggr(bn2[:], bn6[:]),
                 reads=[bn6], writes=[bn2])
            p.op("scalar", lambda e: e.activation(vt[:], bn2[:, 1:2], AF.Ln,
                                                  bias=H["eps_b"][:], scale=1.0),
                 reads=[bn2, H["eps_b"]], writes=[vt])
            p.op("scalar", lambda e: e.activation(rloc[:], vt[:], AF.Exp,
                                                  bias=0.0, scale=-0.5),
                 reads=[vt], writes=[rloc])
            p.op("vector", lambda e: e.tensor_tensor(
                mloc[:], bn2[:, 0:1], rloc[:], ALU.mult),
                reads=[bn2, rloc], writes=[mloc])
            p.op("vector", lambda e: e.tensor_scalar(
                mloc[:], mloc[:], -1.0, None, ALU.mult),
                reads=[mloc], writes=[mloc])
            p.op("vector", lambda e, xt2=xt2: e.tensor_scalar(
                lnb[:], xt2[:], rloc[:], mloc[:], ALU.mult, ALU.add),
                reads=[xt2, rloc, mloc], writes=[lnb])
            for db in range(8):
                tpx = tpp[db % 2]
                p.op("tensor", lambda e, db=db, tpx=tpx: e.transpose(
                    tpx[:], lnb[:, db * 128:(db + 1) * 128], ident_b[:]),
                    reads=[lnb, ident_b], writes=[tpx])
                p.op("scalar", lambda e, db=db, tb4=tb4, tpx=tpx: e.activation(
                    lnT[db][:, tb4 * 128:(tb4 + 1) * 128], tpx[:], AF.Copy,
                    bias=0.0, scale=1.0),
                    reads=[tpx], writes=[lnT[db]])
        for mi in range(32):
            pmm = ps[mi % 2]
            wc = win_c[mi % 2]
            p.dma("sync", lambda e, mi=mi, wc=wc: e.dma_start(
                out=wc[:],
                in_=bass.AP(winb_d, mi * 128,
                            [[2 * DI, 128], [128 * 2 * DI, 8], [1, 128]])),
                reads=[f"winb_{kb}" for kb in range(8)], writes=[wc])
            for kb in range(8):
                p.op("tensor", lambda e, mi=mi, kb=kb, pmm=pmm, wc=wc:
                     e.matmul(pmm[:], wc[:, kb * 128:(kb + 1) * 128],
                              lnT[kb][:], start=(kb == 0), stop=(kb == 7)),
                     reads=[wc, lnT[kb]], writes=[pmm])
            if mi < 16:
                dtile = mi
                xit = xi[dtile % 2]
                p.op("scalar", lambda e, pmm=pmm, xit=xit: e.activation(
                    xit[:, 3:515], pmm[:], AF.Copy, bias=0.0, scale=1.0),
                    reads=[pmm], writes=[xit])
                p.op("vector", lambda e, dtile=dtile, xit=xit: e.tensor_copy(
                    xit[:, 0:3], xi_tail[:, dtile * 3:dtile * 3 + 3]),
                    reads=[xi_tail], writes=[xit])
                p.op("vector", lambda e, dtile=dtile, xit=xit: e.tensor_copy(
                    xi_tail[:, dtile * 3:dtile * 3 + 3], xit[:, 512:515]),
                    reads=[xit], writes=[xi_tail])
                cva = cvt[dtile % 2]
                cvb = cv2[dtile % 2]
                p.op("vector", lambda e, dtile=dtile, xit=xit, cva=cva:
                     e.tensor_scalar(
                    cva[:], xit[:, 0:512], convw[:, dtile * 4:dtile * 4 + 1],
                    None, ALU.mult),
                    reads=[xit, convw], writes=[cva])
                for k in range(1, 4):
                    p.op("vector", lambda e, dtile=dtile, k=k, xit=xit,
                         cva=cva: e.scalar_tensor_tensor(
                             cva[:], xit[:, k:k + 512],
                             convw[:, dtile * 4 + k:dtile * 4 + k + 1],
                             cva[:], ALU.mult, ALU.add),
                         reads=[xit, convw, cva], writes=[cva])
                p.op("vector", lambda e, dtile=dtile, cva=cva: e.tensor_scalar(
                    cva[:], cva[:], convb2[:, dtile:dtile + 1], None, ALU.add),
                    reads=[cva, convb2], writes=[cva])
                p.op("scalar", lambda e, cva=cva, cvb=cvb: e.activation(
                    cvb[:], cva[:], AF.Tanh, bias=0.0, scale=0.5),
                     reads=[cva], writes=[cvb])
                p.op("vector", lambda e, cvb=cvb: e.tensor_scalar(
                    cvb[:], cvb[:], 0.5, 0.5, ALU.mult, ALU.add),
                    reads=[cvb], writes=[cvb])
                p.op("vector", lambda e, dtile=dtile, cva=cva, cvb=cvb:
                     e.tensor_tensor(
                    xc_f[dtile][:], cva[:], cvb[:], ALU.mult),
                    reads=[cva, cvb], writes=[xc_f[dtile]])
                xcr = xcb[dtile % 2]
                p.op("vector", lambda e, dtile=dtile, xcr=xcr: e.tensor_copy(
                    xcr[:], xc_f[dtile][:]),
                    reads=[xc_f[dtile]], writes=[xcr])
                p.op("tensor", lambda e, dtile=dtile, xcr=xcr: e.matmul(
                    p96[:], wx_b[:, dtile * 96:(dtile + 1) * 96], xcr[:],
                    start=(dtile == 0), stop=(dtile == 15)),
                    reads=[wx_b, xcr], writes=[p96])
            else:
                dtile = mi - 16
                zt = ztile[0]
                p.op("scalar", lambda e, pmm=pmm, zt=zt: e.activation(
                    zt[:], pmm[:], AF.Copy, bias=0.0, scale=1.0),
                    reads=[pmm], writes=[zt])
                p.dma("sync", lambda e, dtile=dtile, sc=sc, zt=zt: e.dma_start(
                    out=dram["sp_z"].ap()[dtile * 128:(dtile + 1) * 128,
                                          t0:t0 + 512], in_=zt[:]),
                    reads=[zt], writes=[f"sp_z_{sc}_{dtile}"])
        p.op("scalar", lambda e: e.activation(xdbl[:], p96[:], AF.Copy,
                                              bias=0.0, scale=1.0),
             reads=[p96], writes=[xdbl])
        p.op("vector", lambda e: e.tensor_copy(dtb[:], xdbl[0:64, :]),
             reads=[xdbl], writes=[dtb])
        for mt in range(16):
            pmd = ps[2 + (mt % 2)]
            p.op("tensor", lambda e, mt=mt, pmd=pmd: e.matmul(
                pmd[:], wdt_b[:, mt * 128:(mt + 1) * 128], dtb[:],
                start=True, stop=True),
                reads=[wdt_b, dtb], writes=[pmd])
            cva = cvt[mt % 2]
            cvb = cv2[mt % 2]
            p.op("vector", lambda e, mt=mt, pmd=pmd, cva=cva: e.tensor_scalar(
                cva[:], pmd[:], bdt[:, mt:mt + 1], None, ALU.add),
                reads=[pmd, bdt], writes=[cva])
            p.op("scalar", lambda e, cva=cva, cvb=cvb: e.activation(
                cvb[:], cva[:], AF.Abs, bias=0.0, scale=1.0),
                 reads=[cva], writes=[cvb])
            p.op("scalar", lambda e, cvb=cvb: e.activation(
                cvb[:], cvb[:], AF.Exp, bias=0.0, scale=-1.0),
                 reads=[cvb], writes=[cvb])
            p.op("scalar", lambda e, cvb=cvb: e.activation(
                cvb[:], cvb[:], AF.Ln, bias=1.0, scale=1.0),
                 reads=[cvb], writes=[cvb])
            p.op("vector", lambda e, cva=cva: e.tensor_scalar(
                cva[:], cva[:], 0.0, None, ALU.max), reads=[cva], writes=[cva])
            p.op("vector", lambda e, mt=mt, cva=cva, cvb=cvb: e.tensor_tensor(
                del_f[mt][:], cva[:], cvb[:], ALU.add),
                reads=[cva, cvb], writes=[del_f[mt]])
        p.op("scalar", lambda e: e.activation(bc32[:], p96[64:96, :],
                                              AF.Copy, bias=0.0, scale=1.0),
             reads=[p96], writes=[bc32])
        p.op("vector", lambda e: e.tensor_copy(bch32[:], bc32[:]),
             reads=[bc32], writes=[bch32])
        p.op("vector", lambda e: e.tensor_copy(bhf32[:], bch32[:]),
             reads=[bch32], writes=[bhf32])
        p.op("vector", lambda e: e.tensor_tensor(
            bhf32[:], bc32[:], bhf32[:], ALU.subtract), reads=[bc32, bhf32],
            writes=[bhf32])
        p.op("vector", lambda e: e.tensor_copy(bcl32[:], bhf32[:]),
             reads=[bhf32], writes=[bcl32])
        p.dma("scalar", lambda e: e.dma_start(out=bCh[:], in_=bch32[16:32, :]),
              reads=[bch32], writes=[bCh])
        p.dma("scalar", lambda e: e.dma_start(out=bCl[:], in_=bcl32[16:32, :]),
              reads=[bcl32], writes=[bCl])
        for n in range(16):
            pBC = ps[4 + (n % 2)]
            p.op("tensor", lambda e, n=n, pBC=pBC: e.matmul(
                pBC[:], sel16[:, n * 128:(n + 1) * 128], bch32[0:16, :],
                start=True, stop=False),
                reads=[sel16, bch32], writes=[pBC])
            p.op("tensor", lambda e, n=n, pBC=pBC: e.matmul(
                pBC[:], sel16[:, n * 128:(n + 1) * 128], bcl32[0:16, :],
                start=False, stop=True),
                reads=[sel16, bcl32], writes=[pBC])
            p.op("scalar", lambda e, n=n, pBC=pBC: e.activation(
                bcB[n][:], pBC[:], AF.Copy, bias=0.0, scale=1.0),
                reads=[pBC], writes=[bcB[n]])
        for n in range(16):
            pBC = ps[4 + (n % 2)]
            p.op("tensor", lambda e, n=n, pBC=pBC: e.matmul(
                pBC[:], sel16[:, n * 128:(n + 1) * 128], bCh[:],
                start=True, stop=False),
                reads=[sel16, bCh], writes=[pBC])
            p.op("tensor", lambda e, n=n, pBC=pBC: e.matmul(
                pBC[:], sel16[:, n * 128:(n + 1) * 128], bCl[:],
                start=False, stop=True),
                reads=[sel16, bCl], writes=[pBC])
            p.op("scalar", lambda e, n=n, pBC=pBC: e.activation(
                bcC[n][:], pBC[:], AF.Copy, bias=0.0, scale=1.0),
                reads=[pBC], writes=[bcC[n]])
        for dtile in range(16):
            py = ps[dtile % 2]
            ut = u_t[0]
            p.op("vector", lambda e, dtile=dtile, ut=ut: e.tensor_tensor(
                ut[:], del_f[dtile][:], xc_f[dtile][:], ALU.mult),
                reads=[del_f[dtile], xc_f[dtile]], writes=[ut])
            if dtile == 0:
                p.op("scalar", lambda e, dtile=dtile:
                     e.activation(da_s[0][:], del_f[dtile][:], AF.Exp,
                                  bias=0.0, scale=a_t[:, dtile * 16:dtile * 16 + 1]),
                     reads=[del_f[dtile], a_t], writes=[da_s[0]])
            for n in range(16):
                col = dtile * 16 + n
                pA = da_s[n % 2]
                db = dbu[n % 2]
                ht = h_t[n % 2]
                pb = pb_t[n % 2]
                pbb = pbb_t[n % 2]
                hcol = f"hst_{col}"
                ncol = col + 1
                if ncol < 256:
                    ndt, nn = ncol // 16, ncol % 16
                    p.op("scalar", lambda e, ndt=ndt, ncol=ncol, nn=nn:
                         e.activation(da_s[nn % 2][:], del_f[ndt][:], AF.Exp,
                                      bias=0.0, scale=a_t[:, ncol:ncol + 1]),
                         reads=[del_f[ndt], a_t], writes=[da_s[nn % 2]])
                p.op("gpsimd", lambda e, n=n, db=db, ut=ut: e.tensor_tensor(
                    db[:], ut[:], bcB[n][:], ALU.mult),
                    reads=[ut, bcB[n]], writes=[db])
                p.op("vector", lambda e, col=col, pA=pA, db=db, ht=ht:
                     e.tensor_tensor_scan(
                    ht[:], pA[:], db[:], h_state[:, col:col + 1],
                    ALU.mult, ALU.add),
                    reads=[pA, db, hcol], writes=[ht])
                p.op("scalar", lambda e, col=col, ht=ht: e.activation(
                    h_state[:, col:col + 1], ht[:, 511:512], AF.Copy,
                    bias=0.0, scale=1.0),
                    reads=[ht], writes=[hcol])
                p.op("vector", lambda e, n=n, pb=pb, ht=ht: e.tensor_tensor(
                    pb[:], ht[:], bcC[n][:], ALU.mult),
                    reads=[ht, bcC[n]], writes=[pb])
                p.op("scalar", lambda e, pb=pb, pbb=pbb: e.activation(
                    pbb[:], pb[:], AF.Copy, bias=0.0, scale=1.0),
                    reads=[pb], writes=[pbb])
                p.op("tensor", lambda e, n=n, py=py, pbb=pbb: e.matmul(
                    py[:], ident_b[:], pbb[:],
                    start=(n == 0), stop=(n == 15)),
                    reads=[ident_b, pbb], writes=[py])
            wt2 = w_t[0]
            szt = sz_t[0]
            p.op("vector", lambda e, dtile=dtile, py=py, wt2=wt2:
                 e.scalar_tensor_tensor(
                     wt2[:], xc_f[dtile][:], dvec[:, dtile:dtile + 1], py[:],
                     ALU.mult, ALU.add),
                 reads=[xc_f[dtile], dvec, py], writes=[wt2])
            zt = ztile[0]
            p.dma("sync", lambda e, dtile=dtile, sc=sc, zt=zt: e.dma_start(
                out=zt[:], in_=dram["sp_z"].ap()[dtile * 128:(dtile + 1) * 128,
                                                 t0:t0 + 512]),
                reads=[f"sp_z_{sc}_{dtile}"], writes=[zt])
            p.op("scalar", lambda e, zt=zt, szt=szt: e.activation(
                szt[:], zt[:], AF.Tanh, bias=0.0, scale=0.5),
                reads=[zt], writes=[szt])
            p.op("vector", lambda e, szt=szt: e.tensor_scalar(
                szt[:], szt[:], 0.5, 0.5, ALU.mult, ALU.add),
                reads=[szt], writes=[szt])
            p.op("vector", lambda e, zt=zt, szt=szt: e.tensor_tensor(
                szt[:], szt[:], zt[:], ALU.mult),
                reads=[szt, zt], writes=[szt])
            omt = omix[dtile % 2]
            p.op("vector", lambda e, omt=omt, wt2=wt2, szt=szt: e.tensor_tensor(
                omt[:], wt2[:], szt[:], ALU.mult),
                reads=[wt2, szt], writes=[omt])
            p.dma("sync", lambda e, dtile=dtile, sc=sc, omt=omt: e.dma_start(
                out=dram["omixd"].ap()[dtile * 128:(dtile + 1) * 128,
                                       t0:t0 + 512], in_=omt[:]),
                reads=[omt], writes=[f"omixd_{sc}_{dtile}"])
        for half in range(2):
            for kt in range(16):
                wo = wout_s[kt % 2]
                omt = omix[kt % 2]
                p.dma("scalar", lambda e, kt=kt, wo=wo: e.dma_start(
                    out=wo[:], in_=woutb_d.ap()[kt * 128:(kt + 1) * 128, :]),
                    reads=[f"woutb_{kt}"], writes=[wo])
                p.dma("sync", lambda e, kt=kt, sc=sc, omt=omt: e.dma_start(
                    out=omt[:], in_=dram["omixd"].ap()[kt * 128:(kt + 1) * 128,
                                                       t0:t0 + 512]),
                    reads=[f"omixd_{sc}_{kt}"], writes=[omt])
                for t2 in range(2):
                    tb4 = half * 2 + t2
                    for dh in range(2):
                        po = ps[2 + t2 * 2 + dh]
                        p.op("tensor", lambda e, kt=kt, tb4=tb4, dh=dh, po=po,
                             wo=wo, omt=omt: e.matmul(
                                 po[:], omt[:, tb4 * 128:(tb4 + 1) * 128],
                                 wo[:, dh * 512:(dh + 1) * 512],
                                 start=(kt == 0), stop=(kt == 15)),
                             reads=[omt, wo], writes=[po])
            for t2 in range(2):
                tb4 = half * 2 + t2
                tb = sc * 4 + tb4
                x2r = xc2[0]
                p.dma("sync", lambda e, tb=tb, x2r=x2r: e.dma_start(
                    out=x2r[:], in_=x_src.ap()[tb * 128:(tb + 1) * 128, :]),
                    reads=[xsrc_r(tb)], writes=[x2r])
                for dh in range(2):
                    po = ps[2 + t2 * 2 + dh]
                    xot = xo_t[dh]
                    p.op("vector", lambda e, dh=dh, x2r=x2r, xot=xot, po=po:
                         e.scalar_tensor_tensor(
                             xot[:], po[:], 1.0,
                             x2r[:, dh * 512:(dh + 1) * 512],
                             ALU.mult, ALU.add),
                         reads=[po, x2r], writes=[xot])
                    p.dma("sync", lambda e, tb=tb, dh=dh, xot=xot: e.dma_start(
                        out=x_dst.ap()[tb * 128:(tb + 1) * 128,
                                       dh * 512:(dh + 1) * 512], in_=xot[:]),
                        reads=[xot], writes=[f"xdst_{x_dst.name}_{tb}"])


# version 04
# speedup vs baseline: 1.5937x; 1.4370x over previous
import numpy as np
import ml_dtypes

import concourse.bass as bass
import concourse.mybir as mybir
from concourse.bass_utils import run_bass_kernel_spmd

F32 = mybir.dt.float32
BF16 = mybir.dt.bfloat16
I32 = mybir.dt.int32
AF = mybir.ActivationFunctionType
ALU = mybir.AluOpType
AX = mybir.AxisListType

ENGINES = ("sync", "vector", "scalar", "tensor", "gpsimd")
NCORES = 8

D = 1024
DI = 2048
NS = 16
DTR = 64
IF = 3072
QMAX = 32767.0
SEM_MAX = 48000
NSEM = 3


class Prog:

    def __init__(self, nc):
        self.nc = nc
        self.ops = []
        self.last_writer = {}
        self.readers = {}
        self.deps = []
        self.barrier_deps = set()
        self._last_compute = {}
        self._last_dmas = {}
        self.fence_tile = None
        self._fence_n = 0

    def barrier(self):
        deps = set()
        for e, i in self._last_compute.items():
            deps.add(i)
        for e, lst in self._last_dmas.items():
            deps.update(lst)
        self.barrier_deps = deps
        if self.fence_tile is not None and deps:
            self._fence_n += 1
            ft = self.fence_tile
            i = self.op("vector", lambda e, ft=ft: e.memset(ft[:], 0.0),
                        reads=[], writes=[f"fence_tok_{self._fence_n}"])
            self.barrier_deps = {i}

    def _name(self, b):
        if isinstance(b, str):
            return b
        return b.tensor.name if isinstance(b, bass.AP) else b.name

    def _add(self, engine, fn, reads, writes, inc):
        i = len(self.ops)
        dep = set()
        reads = [self._name(b) for b in reads]
        writes = [self._name(b) for b in writes]
        for b in reads:
            w = self.last_writer.get(b)
            if w is not None:
                dep.add(w)
        for b in writes:
            w = self.last_writer.get(b)
            if w is not None:
                dep.add(w)
            dep.update(self.readers.get(b, ()))
        dep |= self.barrier_deps
        dep.discard(i)
        if inc == 16:
            self._last_dmas.setdefault(engine, [])
            self._last_dmas[engine].append(i)
            if len(self._last_dmas[engine]) > 8:
                self._last_dmas[engine].pop(0)
        else:
            self._last_compute[engine] = i
        self.ops.append(dict(engine=engine, fn=fn, inc=inc))
        self.deps.append(dep)
        for b in reads:
            self.readers.setdefault(b, []).append(i)
        for b in writes:
            self.last_writer[b] = i
            self.readers[b] = []
        return i

    def op(self, engine, fn, reads=(), writes=()):
        return self._add(engine, fn, reads, writes, 1)

    def dma(self, engine, fn, reads=(), writes=(), stream=None):
        i = self._add(engine, fn, reads, writes, 16)
        if stream is not None:
            self.ops[i]["stream"] = stream
        return i

    def emit(self, block, semlists):
        n = len(self.ops)
        slot_of = [0] * n
        val_of = [0] * n
        nslot = {e: len(semlists[e]) for e in ENGINES}
        slot_ct = {e: [0] * nslot[e] for e in ENGINES}
        dma_rr = {e: 0 for e in ENGINES}
        engct = {e: 0 for e in ENGINES}
        groups = {e: {"ld": [k for k in range(nslot[e]) if k < (nslot[e] + 1) // 2],
                      "st": [k for k in range(nslot[e]) if k >= (nslot[e] + 1) // 2]}
                  for e in ENGINES}
        gctr = {}
        for i, o in enumerate(self.ops):
            e = o["engine"]
            if o["inc"] == 16:
                st = o.get("stream")
                if st in ("ld", "st") and groups[e][st]:
                    lst = groups[e][st]
                    k = gctr.get((e, st), 0)
                    gctr[(e, st)] = k + 1
                    s = lst[k % len(lst)]
                else:
                    s = dma_rr[e] % nslot[e]
                dma_rr[e] += 1
            else:
                s = (engct[e] // SEM_MAX) % nslot[e]
            engct[e] += o["inc"]
            slot_ct[e][s] += o["inc"]
            slot_of[i] = s
            val_of[i] = slot_ct[e][s]
        slot_ops = {}
        for i, o in enumerate(self.ops):
            slot_ops.setdefault((o["engine"], slot_of[i]), []).append(i)

        import bisect

        def cum_before(key, i):
            lst = slot_ops.get(key)
            if not lst:
                return 0
            k = bisect.bisect_left(lst, i)
            return val_of[lst[k - 1]] if k else 0

        for ename in ENGINES:
            idxs = [i for i, o in enumerate(self.ops) if o["engine"] == ename]
            if not idxs:
                continue

            def make(ename, idxs):
                def section(eng):
                    water = {}
                    for i in idxs:
                        o = self.ops[i]
                        need = {}
                        if o["inc"] == 16:
                            key = (ename, slot_of[i])
                            v = cum_before(key, i)
                            if v:
                                need[key] = v
                        for j in self.deps[i]:
                            je = self.ops[j]["engine"]
                            if (je == ename == "tensor"
                                    and self.ops[j]["inc"] == 1):
                                continue
                            key = (je, slot_of[j])
                            if self.ops[j]["inc"] == 16:
                                v = cum_before(key, i)
                            else:
                                v = val_of[j]
                            if need.get(key, 0) < v:
                                need[key] = v
                        for (je, sl), v in sorted(need.items()):
                            if water.get((je, sl), 0) >= v:
                                continue
                            eng.wait_ge(semlists[je][sl], v)
                            water[(je, sl)] = v
                        o["fn"](eng).then_inc(
                            semlists[ename][slot_of[i]], o["inc"])
                return section

            getattr(block, ename)(make(ename, idxs))
        totals = {e: [0] * nslot[e] for e in ENGINES}
        for i, o in enumerate(self.ops):
            e = o["engine"]
            totals[e][slot_of[i]] = max(totals[e][slot_of[i]], val_of[i])
        return totals


class Ctx:
    def __init__(self, nc):
        self.nc = nc
        self.stack = [[]]
        self.uid = 0

    def push(self):
        self.stack.append([])

    def pop(self):
        for g in reversed(self.stack.pop()):
            g.__exit__(None, None, None)

    def _alloc(self, kind, name, shape, dt):
        self.uid += 1
        nm = f"{name}_{self.uid}"
        g = (self.nc.sbuf_tensor(nm, shape, dt) if kind == "s"
             else self.nc.psum_tensor(nm, shape, dt))
        t = g.__enter__()
        self.stack[-1].append(g)
        return t

    def sbuf(self, name, shape, dt=F32):
        return self._alloc("s", name, shape, dt)

    def psum(self, name, shape, dt=F32):
        return self._alloc("p", name, shape, dt)



def kernel(L0_attn_x, L0_attn_W_in, L0_attn_conv_w, L0_attn_conv_b,
           L0_attn_W_x, L0_attn_W_dt, L0_attn_b_dt, L0_attn_A_log,
           L0_attn_D, L0_attn_W_out, L0_ffn_w_gate, L0_ffn_w_up,
           L0_ffn_w_down):
    x = np.ascontiguousarray(np.asarray(L0_attn_x, dtype=np.float32))
    B, T, Dm = x.shape

    def cols(v):
        return np.ascontiguousarray(
            np.asarray(v, np.float32).reshape(DI // 128, 128).T)

    conv_w = np.asarray(L0_attn_conv_w, np.float32).reshape(16, 128, 4)
    conv_w_t = np.ascontiguousarray(conv_w.transpose(1, 0, 2).reshape(128, 64))
    a_log_t = np.ascontiguousarray(
        np.asarray(L0_attn_A_log, np.float32).reshape(16, 128, NS)
        .transpose(1, 0, 2).reshape(128, 16 * NS))

    common = {
        "W_in": np.ascontiguousarray(np.asarray(L0_attn_W_in, np.float32)),
        "W_x": np.ascontiguousarray(np.asarray(L0_attn_W_x, np.float32)),
        "W_dt": np.ascontiguousarray(np.asarray(L0_attn_W_dt, np.float32)),
        "W_out": np.ascontiguousarray(np.asarray(L0_attn_W_out, np.float32)),
        "w_gate": np.ascontiguousarray(np.asarray(L0_ffn_w_gate, np.float32)),
        "w_up": np.ascontiguousarray(np.asarray(L0_ffn_w_up, np.float32)),
        "w_down": np.ascontiguousarray(np.asarray(L0_ffn_w_down, np.float32)),
        "conv_wt": conv_w_t,
        "conv_b": cols(L0_attn_conv_b),
        "b_dt": cols(L0_attn_b_dt),
        "Dvec": cols(L0_attn_D),
        "A_log_t": a_log_t,
        "ident": np.eye(128, dtype=ml_dtypes.bfloat16),
        "ones_row": np.ones((1, 128), dtype=np.float32),
        "sel16": _sel16(),
    }

    if FUSED:
        nc = build_program(T, ncores=B)
        in_maps = []
        for c in range(B):
            m = dict(common)
            m["x_in"] = np.ascontiguousarray(x[c])
            in_maps.append(m)
        res = run_bass_kernel_spmd(nc, in_maps, core_ids=list(range(B)))
        out = np.stack([np.asarray(r["x_out"]) for r in res.results], axis=0)
        return out.astype(np.float32)
    return _kernel_staged(x, common, T, B)


_PROG_CACHE = {}


def _get_prog(T, B, stage):
    key = (T, B, stage)
    if key not in _PROG_CACHE:
        _PROG_CACHE[key] = build_program(T, ncores=B, stage=stage)
    return _PROG_CACHE[key]


def _kernel_staged(x, common, T, B):
    cores = list(range(B))

    def run(stage, per_core):
        nc = _get_prog(T, B, stage)
        return run_bass_kernel_spmd(nc, per_core, core_ids=cores).results

    base = {k: common[k] for k in ("ident", "ones_row")}

    def gather_mx(res):
        return np.ascontiguousarray(
            np.concatenate([np.asarray(r["mxout"]) for r in res], axis=0))

    def ffn(x_host):
        r = run("max", [dict(base, x_in=x_host[c]) for c in cores])
        g1 = gather_mx(r)
        r = run("A", [dict(base, x_in=x_host[c], g1_in=g1,
                           w_up=common["w_up"], w_gate=common["w_gate"])
                      for c in cores])
        g23 = gather_mx(r)
        sp_up = [np.asarray(r[c]["sp_up"]) for c in cores]
        sp_gate = [np.asarray(r[c]["sp_gate"]) for c in cores]
        r = run("B", [dict(base, sp_gate=sp_gate[c], g23_in=g23)
                      for c in cores])
        g4 = gather_mx(r)
        sp_P = [np.asarray(r[c]["sp_P"]) for c in cores]
        r = run("C", [dict(base, sp_up=sp_up[c], sp_P=sp_P[c], g23_in=g23,
                           g4_in=g4) for c in cores])
        g5 = gather_mx(r)
        sp_o = [np.asarray(r[c]["sp_o"]) for c in cores]
        r = run("D", [dict(base, x_in=x_host[c], sp_o=sp_o[c],
                           w_down=common["w_down"], g23_in=g23, g4_in=g4,
                           g5_in=g5) for c in cores])
        return [np.asarray(r[c]["x_out"]) for c in cores]

    x_host = [np.ascontiguousarray(x[c]) for c in range(B)]
    x2 = ffn(x_host)
    mkeys = ("W_in", "W_x", "W_dt", "W_out", "conv_wt", "conv_b", "b_dt",
             "Dvec", "A_log_t", "sel16")
    r = run("mamba", [dict(base, x_in=x2[c],
                           **{k: common[k] for k in mkeys}) for c in cores])
    x3 = [np.asarray(r[c]["x_out"]) for c in cores]
    out = ffn(x3)
    return np.stack(out, axis=0).astype(np.float32)


FUSED = False


def _sel16():
    sel = np.zeros((16, 16 * 128), dtype=ml_dtypes.bfloat16)
    for n in range(16):
        sel[n, n * 128:(n + 1) * 128] = 1
    return sel



STAGE_IO = {
    "max":   ({"x_in"}, {"mxout"}),
    "A":     ({"x_in", "g1_in", "w_up", "w_gate"},
              {"sp_up", "sp_gate", "mxout"}),
    "B":     ({"sp_gate", "g23_in"}, {"sp_P", "mxout"}),
    "C":     ({"sp_up", "sp_P", "g23_in", "g4_in"}, {"sp_o", "mxout"}),
    "D":     ({"x_in", "sp_o", "w_down", "g23_in", "g4_in", "g5_in"},
              {"x_out"}),
    "mamba": ({"x_in", "W_in", "W_x", "W_dt", "W_out", "conv_wt", "conv_b",
               "b_dt", "Dvec", "A_log_t", "sel16"}, {"x_out"}),
}


def build_program(T, mixer="mamba", stop_after=None, ncores=8, trunc=None,
                  stage=None):
    nc = bass.Bass("TRN2", target_bir_lowering=False)
    ctx = Ctx(nc)
    p = Prog(nc)
    p.fence_tile = ctx.sbuf("fencet", [1, 1])
    NT = T // 128
    NCH = T // 512

    dram = {}
    for nm, shape, dt, kind in [
        ("x_in", [T, D], F32, "ExternalInput"),
        ("x_out", [T, D], F32, "ExternalOutput"),
        ("W_in", [D, 2 * DI], F32, "ExternalInput"),
        ("W_x", [DI, DTR + 2 * NS], F32, "ExternalInput"),
        ("W_dt", [DTR, DI], F32, "ExternalInput"),
        ("W_out", [DI, D], F32, "ExternalInput"),
        ("w_gate", [IF, D], F32, "ExternalInput"),
        ("w_up", [IF, D], F32, "ExternalInput"),
        ("w_down", [D, IF], F32, "ExternalInput"),
        ("conv_wt", [128, 64], F32, "ExternalInput"),
        ("conv_b", [128, 16], F32, "ExternalInput"),
        ("b_dt", [128, 16], F32, "ExternalInput"),
        ("Dvec", [128, 16], F32, "ExternalInput"),
        ("A_log_t", [128, 16 * NS], F32, "ExternalInput"),
        ("ident", [128, 128], BF16, "ExternalInput"),
        ("ones_row", [1, 128], F32, "ExternalInput"),
        ("sel16", [16, 16 * 128], BF16, "ExternalInput"),
        ("xres", [T, D], F32, "Internal"),
        ("sp_up", [IF, T], F32, "Internal"),
        ("sp_gate", [IF, T], F32, "Internal"),
        ("sp_P", [IF, T], F32, "Internal"),
        ("sp_o", [IF, T], F32, "Internal"),
        ("sp_z", [DI, T], F32, "Internal"),
        ("wupT", [D, IF], BF16, "Internal"),
        ("wgateT", [D, IF], BF16, "Internal"),
        ("wdownT", [IF, D], BF16, "Internal"),
        ("winb", [D, 2 * DI], BF16, "Internal"),
        ("omixd", [DI, T], BF16, "Internal"),
        ("woutb", [DI, D], BF16, "Internal"),
        ("cc_in", [1, 2], F32, "Internal"),
        ("cc_out", [ncores, 2], F32, "Internal"),
        ("mxout", [1, 2], F32, "Internal"),
        ("g1_in", [ncores, 2], F32, "Internal"),
        ("g23_in", [ncores, 2], F32, "Internal"),
        ("g4_in", [ncores, 2], F32, "Internal"),
        ("g5_in", [ncores, 2], F32, "Internal"),
    ]:
        if stage is not None:
            ins_, outs_ = STAGE_IO[stage]
            base = {"ident", "ones_row"}
            if nm in ins_ or nm in base:
                kind = "ExternalInput"
            elif nm in outs_:
                kind = "ExternalOutput"
            else:
                kind = "Internal"
        dram[nm] = nc.dram_tensor(nm, shape, dt, kind=kind)

    ident_b = ctx.sbuf("identb", [128, 128], BF16)
    ones_f = ctx.sbuf("onesf", [1, 128], F32)
    p.dma("sync", lambda e: e.dma_start(out=ident_b[:], in_=dram["ident"].ap()),
          writes=[ident_b])
    p.dma("sync", lambda e: e.dma_start(out=ones_f[:], in_=dram["ones_row"].ap()),
          writes=[ones_f])
    stat_m = ctx.sbuf("stat_m", [128, NT])
    stat_r = ctx.sbuf("stat_r", [128, NT])
    eps_b = ctx.sbuf("eps_b", [128, 1])
    p.op("vector", lambda e: e.memset(eps_b[:], 1e-5), writes=[eps_b])

    H = dict(ctx=ctx, p=p, nc=nc, dram=dram, ident_b=ident_b, ones_f=ones_f,
             stat_m=stat_m, stat_r=stat_r, eps_b=eps_b, T=T, NT=NT, NCH=NCH,
             ncores=ncores, stage=stage)

    if stage is not None:
        if stage == "max":
            ffn_call(H, 0, x_src=dram["x_in"], x_dst=None, stages={"max"})
        elif stage in ("A", "B", "C", "D"):
            ctx.push()
            prep_ffn_weights(H, which={
                "A": ("wupT", "wgateT"), "B": (), "C": (),
                "D": ("wdownT",)}[stage])
            ctx.pop()
            p.barrier()
            ctx.push()
            ffn_call(H, 0, x_src=dram["x_in"], x_dst=dram["x_out"],
                     stages={stage})
            ctx.pop()
        elif stage == "mamba":
            ctx.push()
            mamba(H, x_src=dram["x_in"], x_dst=dram["x_out"])
            ctx.pop()
        return _finish(nc, ctx, p, trunc)

    ctx.push()
    prep_ffn_weights(H)
    ctx.pop()
    p.barrier()

    ctx.push()
    ffn_call(H, 0, x_src=dram["x_in"], x_dst=dram["xres"])
    ctx.pop()
    p.barrier()

    if stop_after != "ffn1":
        ctx.push()
        if mixer == "mamba":
            mamba(H, x_src=dram["xres"], x_dst=dram["xres"])
        else:
            mixer_copy(H, x_src=dram["xres"], x_dst=dram["xres"])
        ctx.pop()
        p.barrier()

        ctx.push()
        ffn_call(H, 1, x_src=dram["xres"], x_dst=dram["x_out"])
        ctx.pop()
    else:
        p.barrier()
        mixer_copy(H, x_src=dram["xres"], x_dst=dram["x_out"])

    return _finish(nc, ctx, p, trunc)


def _finish(nc, ctx, p, trunc=None):
    if trunc is not None:
        p.ops = p.ops[:trunc]
        p.deps = p.deps[:trunc]
    NSLOTS = {"sync": 6, "scalar": 6, "gpsimd": 4, "vector": 3, "tensor": 3}
    with nc.Block() as block:
        semctx = []
        semlists = {}
        for e in ENGINES:
            semlists[e] = []
            for k in range(NSLOTS[e]):
                g = nc.semaphore(f"s_{e}{k}")
                semlists[e].append(g.__enter__())
                semctx.append(g)
        totals = p.emit(block, semlists)

        @block.gpsimd
        def _(eng):
            for e in ENGINES:
                for k in range(NSLOTS[e]):
                    if totals[e][k]:
                        eng.wait_ge(semlists[e][k], totals[e][k])

        for g in reversed(semctx):
            g.__exit__(None, None, None)

    while len(ctx.stack) > 1:
        ctx.pop()
    for g in reversed(ctx.stack[0]):
        g.__exit__(None, None, None)
    ctx.stack[0] = []
    return nc



def round_half_up(H, y, i32, vf, gg, pre_fn, pre_reads):
    p = H["p"]
    p.op("vector", lambda e: pre_fn(e, y), reads=pre_reads, writes=[y])
    p.op("vector", lambda e: e.tensor_copy(i32[:], y[:]), reads=[y], writes=[i32])
    p.op("vector", lambda e: e.tensor_copy(vf[:], i32[:]), reads=[i32], writes=[vf])
    p.op("vector", lambda e: e.tensor_tensor(gg[:], vf[:], y[:], ALU.is_gt),
         reads=[vf, y], writes=[gg])
    p.op("vector", lambda e: e.tensor_tensor(vf[:], vf[:], gg[:], ALU.subtract),
         reads=[vf, gg], writes=[vf])


def scalar_bcast(H, tag, src_ap, src_buf, pbc, dst):
    p = H["p"]
    p.op("tensor", lambda e: e.matmul(pbc[:, 0:1], H["ones_f"][:], src_ap,
                                      start=True, stop=True),
         reads=[src_buf, H["ones_f"]], writes=[pbc])
    p.op("scalar", lambda e: e.activation(dst[:], pbc[:, 0:1], AF.Copy,
                                          bias=0.0, scale=1.0),
         reads=[pbc], writes=[dst])


def reduce_acc(H, tag, acc, nvals):
    ctx, p = H["ctx"], H["p"]
    row = ctx.sbuf(tag + "_row", [1, 256])
    mx = ctx.sbuf(tag + "_mx", [1, 2])
    for v in range(nvals):
        p.dma("sync", lambda e, v=v: e.dma_start(
            out=row[0:1, v * 128:(v + 1) * 128], in_=acc[:, v:v + 1]),
            reads=[acc], writes=[row])
    p.op("vector", lambda e: e.memset(mx[:], 0.0), writes=[mx])
    for v in range(nvals):
        p.op("vector", lambda e, v=v: e.tensor_reduce(
            mx[0:1, v:v + 1], row[0:1, v * 128:(v + 1) * 128], AX.X, ALU.max),
            reads=[row], writes=[mx])
    return mx


def export_mx(H, mx):
    p = H["p"]
    p.dma("sync", lambda e: e.dma_start(out=H["dram"]["mxout"].ap(), in_=mx[:]),
          reads=[mx], writes=["mxout"])


def consume_gmax(H, tag, name):
    ctx, p, dram = H["ctx"], H["p"], H["dram"]
    nr = H["ncores"]
    gt = ctx.sbuf(tag + "_g", [1, 16])
    p.dma("sync", lambda e: e.dma_start(
        out=gt[0:1, 0:2 * nr], in_=dram[name].ap()),
        reads=[name], writes=[gt])
    gm = ctx.sbuf(tag + "_gm", [1, 2])
    p.op("vector", lambda e: e.memset(gm[:], 0.0), writes=[gm])
    for v in range(2):
        p.op("vector", lambda e, v=v: e.tensor_reduce(
            gm[0:1, v:v + 1],
            bass.AP(gt, v, [[16, 1], [2, nr]]),
            AX.X, ALU.max),
            reads=[gt], writes=[gm])
    return gm


def global_max(H, tag, acc, nvals, pbc):
    ctx, p, dram = H["ctx"], H["p"], H["dram"]
    mx = reduce_acc(H, tag, acc, nvals)
    p.dma("sync", lambda e: e.dma_start(out=dram["cc_in"].ap(), in_=mx[:]),
          reads=[mx], writes=["cc_in"])
    p.dma("gpsimd", lambda e: e.collective_compute(
        "AllGather", ALU.bypass, [list(range(H["ncores"]))],
        [dram["cc_in"].ap()], [dram["cc_out"].ap()]),
        reads=["cc_in"], writes=["cc_out"])
    gt = ctx.sbuf(tag + "_g", [1, 16])
    nr0 = H["ncores"]
    p.dma("sync", lambda e: e.dma_start(
        out=gt[0:1, 0:2 * nr0], in_=dram["cc_out"].ap()[0:nr0, :]),
        reads=["cc_out"], writes=[gt])
    gm = ctx.sbuf(tag + "_gm", [1, 2])
    p.op("vector", lambda e: e.memset(gm[:], 0.0), writes=[gm])
    nr = H["ncores"]
    for v in range(nvals):
        p.op("vector", lambda e, v=v: e.tensor_reduce(
            gm[0:1, v:v + 1],
            bass.AP(gt, v, [[16, 1], [2, nr]]),
            AX.X, ALU.max),
            reads=[gt], writes=[gm])
    return gm


def quant_scales(H, tag, gm, col, pbc):
    ctx, p = H["ctx"], H["p"]
    sv = ctx.sbuf(tag + "_sv", [1, 1])
    rv = ctx.sbuf(tag + "_rv", [1, 1])
    p.op("vector", lambda e: e.tensor_scalar(
        sv[:], gm[0:1, col:col + 1], 1.0 / QMAX, 1e-12, ALU.mult, ALU.max),
        reads=[gm], writes=[sv])
    p.op("vector", lambda e: e.reciprocal(rv[:], sv[:]), reads=[sv], writes=[rv])
    sbc = ctx.sbuf(tag + "_sb", [128, 1])
    rbc = ctx.sbuf(tag + "_rb", [128, 1])
    scalar_bcast(H, tag + "s", sv[0:1, 0:1], sv, pbc, sbc)
    scalar_bcast(H, tag + "r", rv[0:1, 0:1], rv, pbc, rbc)
    return sbc, rbc, sv



def prep_ffn_weights(H, which=("wupT", "wgateT", "wdownT")):
    ctx, p, dram = H["ctx"], H["p"], H["dram"]
    ident_b = H["ident_b"]
    IMAX = IF
    wt = ctx.sbuf("pw_w", [128, IMAX])
    bm = ctx.sbuf("pw_bm", [128, IMAX // 32])
    rr = ctx.sbuf("pw_r", [128, IMAX // 32])
    y = ctx.sbuf("pw_y", [128, IMAX])
    i32 = ctx.sbuf("pw_i", [128, IMAX], I32)
    gg = ctx.sbuf("pw_g", [128, IMAX])
    q = ctx.sbuf("pw_q", [128, IMAX])
    wb = ctx.sbuf("pw_b", [128, IMAX], BF16)
    tp = [ctx.psum("pw_tp", [128, 128], BF16) for _ in range(2)]
    tps = [ctx.sbuf("pw_tps", [128, 128], BF16) for _ in range(2)]

    for (srcn, dstn, O, I) in (("w_up", "wupT", IF, D),
                               ("w_gate", "wgateT", IF, D),
                               ("w_down", "wdownT", D, IF)):
        if dstn not in which:
            continue
        src, dst = dram[srcn], dram[dstn]
        nb = I // 32
        for it in range(O // 128):
            p.dma("sync", lambda e, it=it, src=src, I=I: e.dma_start(
                out=wt[:, 0:I], in_=src.ap()[it * 128:(it + 1) * 128, :]),
                writes=[wt])
            p.op("vector", lambda e, I=I, nb=nb: e.tensor_reduce(
                bm[:, 0:nb], bass.AP(wt, 0, [[IMAX, 128], [32, nb], [1, 32]]),
                AX.X, ALU.max, apply_absolute_value=True),
                reads=[wt], writes=[bm])
            p.op("vector", lambda e, nb=nb: e.tensor_scalar(
                bm[:, 0:nb], bm[:, 0:nb], 1.0 / 7.0, 1e-12, ALU.mult, ALU.max),
                reads=[bm], writes=[bm])
            p.op("vector", lambda e, nb=nb: e.reciprocal(rr[:, 0:nb], bm[:, 0:nb]),
                 reads=[bm], writes=[rr])
            p.op("vector", lambda e, I=I, nb=nb: e.tensor_tensor(
                y[:, 0:I], wt[:, 0:I],
                bass.AP(rr, 0, [[IMAX // 32, 128], [1, nb], [0, 32]]),
                ALU.mult), reads=[wt, rr], writes=[y])
            p.op("vector", lambda e, I=I: e.tensor_scalar(
                y[:, 0:I], y[:, 0:I], 0.5, None, ALU.add),
                reads=[y], writes=[y])
            p.op("vector", lambda e, I=I: e.tensor_copy(i32[:, 0:I], y[:, 0:I]),
                 reads=[y], writes=[i32])
            p.op("vector", lambda e, I=I: e.tensor_copy(q[:, 0:I], i32[:, 0:I]),
                 reads=[i32], writes=[q])
            p.op("vector", lambda e, I=I: e.tensor_tensor(
                gg[:, 0:I], q[:, 0:I], y[:, 0:I], ALU.is_gt),
                reads=[q, y], writes=[gg])
            p.op("vector", lambda e, I=I: e.tensor_tensor(
                q[:, 0:I], q[:, 0:I], gg[:, 0:I], ALU.subtract),
                reads=[q, gg], writes=[q])
            p.op("vector", lambda e, I=I, nb=nb: e.tensor_tensor(
                wb[:, 0:I], q[:, 0:I],
                bass.AP(bm, 0, [[IMAX // 32, 128], [1, nb], [0, 32]]),
                ALU.mult), reads=[q, bm], writes=[wb])
            for jb in range(I // 128):
                tpx = tp[jb % 2]
                tsx = tps[jb % 2]
                p.op("tensor", lambda e, jb=jb, tpx=tpx: e.transpose(
                    tpx[:], wb[:, jb * 128:(jb + 1) * 128], ident_b[:]),
                    reads=[wb, ident_b], writes=[tpx])
                p.op("scalar", lambda e, tpx=tpx, tsx=tsx: e.activation(
                    tsx[:], tpx[:], AF.Copy, bias=0.0, scale=1.0),
                    reads=[tpx], writes=[tsx])
                p.dma("scalar", lambda e, tsx=tsx, dst=dst, it=it, jb=jb:
                      e.dma_start(out=dst.ap()[jb * 128:(jb + 1) * 128,
                                               it * 128:(it + 1) * 128],
                                  in_=tsx[:]),
                      reads=[tsx], writes=[f"{dstn}_{it}_{jb}"])



def ffn_call(H, call_id, x_src, x_dst, stages=("max", "A", "B", "C", "D")):
    ctx, p, dram = H["ctx"], H["p"], H["dram"]
    T, NT, NCH = H["T"], H["NT"], H["NCH"]
    stat_m, stat_r = H["stat_m"], H["stat_r"]
    ident_b = H["ident_b"]
    fused = H.get("stage") is None
    cid = f"f{call_id}"
    NIT = IF // 128
    sp_up, sp_gate = dram["sp_up"], dram["sp_gate"]
    sp_P, sp_o = dram["sp_P"], dram["sp_o"]

    def xsrc_r(tb):
        return f"xdst_{x_src.name}_{tb}"

    mmp = [ctx.psum(cid + "_mm", [128, 512]) for _ in range(2)]
    dps = [ctx.psum(cid + "_dp", [128, 512]) for _ in range(2)]
    tpp = [ctx.psum(cid + "_tp", [128, 128], BF16) for _ in range(2)]
    pbc = ctx.psum(cid + "_bc", [128, 8])

    xt = [ctx.sbuf(cid + "_mxx", [128, D]) for _ in range(2)]
    bn6 = ctx.sbuf(cid + "_bn6", [128, 12])
    bn2 = ctx.sbuf(cid + "_bn2", [128, 2])
    rmx = ctx.sbuf(cid + "_rmx", [128, 1])
    rmn = ctx.sbuf(cid + "_rmn", [128, 1])
    acc1 = ctx.sbuf(cid + "_acc1", [128, 2])
    vtmp = ctx.sbuf(cid + "_vt", [128, 1])
    if "max" in stages:
        p.op("vector", lambda e: e.memset(acc1[:], 0.0), writes=[acc1])
    for tb in (range(NT) if "max" in stages else ()):
        t = xt[tb % 2]
        p.dma("sync", lambda e, tb=tb, t=t: e.dma_start(
            out=t[:], in_=x_src.ap()[tb * 128:(tb + 1) * 128, :]),
            reads=[xsrc_r(tb)], writes=[t])
        p.op("vector", lambda e, t=t: e.bn_stats(bn6[:, 0:6], t[:, 0:512]),
             reads=[t], writes=[bn6])
        p.op("vector", lambda e, t=t: e.bn_stats(bn6[:, 6:12], t[:, 512:1024]),
             reads=[t], writes=[bn6])
        p.op("vector", lambda e: e.bn_aggr(bn2[:], bn6[:]),
             reads=[bn6], writes=[bn2])
        p.op("vector", lambda e, tb=tb: e.tensor_copy(
            stat_m[:, tb:tb + 1], bn2[:, 0:1]), reads=[bn2], writes=[stat_m])
        p.op("scalar", lambda e: e.activation(vtmp[:], bn2[:, 1:2], AF.Ln,
                                              bias=H["eps_b"][:], scale=1.0),
             reads=[bn2, H["eps_b"]], writes=[vtmp])
        p.op("scalar", lambda e, tb=tb: e.activation(
            stat_r[:, tb:tb + 1], vtmp[:], AF.Exp, bias=0.0, scale=-0.5),
            reads=[vtmp], writes=[stat_r])
        p.op("vector", lambda e, t=t: e.tensor_reduce(
            rmx[:], t[:], AX.X, ALU.max), reads=[t], writes=[rmx])
        p.op("vector", lambda e, t=t: e.tensor_reduce(
            rmn[:], t[:], AX.X, ALU.max, negate=True),
            reads=[t], writes=[rmn])
        p.op("vector", lambda e, tb=tb: e.tensor_tensor(
            rmx[:], rmx[:], stat_m[:, tb:tb + 1], ALU.subtract),
            reads=[rmx, stat_m], writes=[rmx])
        p.op("vector", lambda e, tb=tb: e.tensor_tensor(
            rmn[:], rmn[:], stat_m[:, tb:tb + 1], ALU.add),
            reads=[rmn, stat_m], writes=[rmn])
        p.op("vector", lambda e: e.tensor_tensor(
            rmx[:], rmx[:], rmn[:], ALU.max), reads=[rmx, rmn], writes=[rmx])
        p.op("vector", lambda e, tb=tb: e.tensor_tensor(
            rmx[:], rmx[:], stat_r[:, tb:tb + 1], ALU.mult),
            reads=[rmx, stat_r], writes=[rmx])
        p.op("vector", lambda e: e.tensor_tensor(
            acc1[:, 0:1], acc1[:, 0:1], rmx[:], ALU.max),
            reads=[acc1, rmx], writes=[acc1])

    if "max" in stages and not fused:
        export_mx(H, reduce_acc(H, cid + "g1", acc1, 1))
    if "A" in stages:
        if fused:
            gm1 = global_max(H, cid + "g1", acc1, 1, pbc)
        else:
            gm1 = consume_gmax(H, cid + "g1", "g1_in")
        s1, r1, _ = quant_scales(H, cid + "q1", gm1, 0, pbc)

    amax2 = ctx.sbuf(cid + "_am2", [128, 2])
    rtmp = ctx.sbuf(cid + "_rt", [128, 1])
    ctx.push()
    doA = "A" in stages
    wup_s = [ctx.sbuf(cid + "_wu", [128, IF], BF16) for _ in range(8)]
    wgate_s = [ctx.sbuf(cid + "_wg", [128, IF], BF16) for _ in range(8)]
    for kb in (range(8) if doA else ()):
        p.dma("sync", lambda e, kb=kb: e.dma_start(
            out=wup_s[kb][:], in_=dram["wupT"].ap()[kb * 128:(kb + 1) * 128, :]),
            reads=[f"wupT_{mi}_{kb}" for mi in range(NIT)],
            writes=[wup_s[kb], "wupg_all"], stream="ld")
        p.dma("sync", lambda e, kb=kb: e.dma_start(
            out=wgate_s[kb][:],
            in_=dram["wgateT"].ap()[kb * 128:(kb + 1) * 128, :]),
            reads=[f"wgateT_{mi}_{kb}" for mi in range(NIT)],
            writes=[wgate_s[kb], "wupg_all"], stream="ld")

    xqT2 = [[ctx.sbuf(cid + "_xqT", [128, 512], BF16) for _ in range(8)]
            for _ in range(2)]
    ya = ctx.sbuf(cid + "_Ay", [128, D])
    ia = ctx.sbuf(cid + "_Ai", [128, D], I32)
    ga = ctx.sbuf(cid + "_Ag", [128, D])
    ixf = ctx.sbuf(cid + "_ixf", [128, D])
    ixb2 = [ctx.sbuf(cid + "_ixb", [128, D], BF16) for _ in range(2)]
    kvec = ctx.sbuf(cid + "_kv", [128, 1])
    bvec = ctx.sbuf(cid + "_bv", [128, 1])
    ug = [ctx.sbuf(cid + "_ug", [128, 512]) for _ in range(4)]
    if doA:
        p.op("vector", lambda e: e.memset(amax2[:], 0.0), writes=[amax2])

    for ch in (range(NCH) if doA else ()):
        for tb4 in range(4):
            tb = ch * 4 + tb4
            t = xt[tb % 2]
            p.dma("sync", lambda e, tb=tb, t=t: e.dma_start(
                out=t[:], in_=x_src.ap()[tb * 128:(tb + 1) * 128, :]),
                reads=[xsrc_r(tb)], writes=[t], stream="ld")
            p.op("vector", lambda e, t=t: e.bn_stats(bn6[:, 0:6], t[:, 0:512]),
                 reads=[t], writes=[bn6])
            p.op("vector", lambda e, t=t: e.bn_stats(bn6[:, 6:12],
                                                     t[:, 512:1024]),
                 reads=[t], writes=[bn6])
            p.op("vector", lambda e: e.bn_aggr(bn2[:], bn6[:]),
                 reads=[bn6], writes=[bn2])
            p.op("scalar", lambda e: e.activation(
                vtmp[:], bn2[:, 1:2], AF.Ln, bias=H["eps_b"][:], scale=1.0),
                reads=[bn2, H["eps_b"]], writes=[vtmp])
            p.op("scalar", lambda e: e.activation(
                rmx[:], vtmp[:], AF.Exp, bias=0.0, scale=-0.5),
                reads=[vtmp], writes=[rmx])
            p.op("vector", lambda e: e.tensor_tensor(
                kvec[:], rmx[:], r1[:], ALU.mult),
                reads=[rmx, r1], writes=[kvec])
            p.op("vector", lambda e: e.tensor_tensor(
                bvec[:], bn2[:, 0:1], kvec[:], ALU.mult),
                reads=[bn2, kvec], writes=[bvec])
            p.op("vector", lambda e: e.tensor_scalar(
                bvec[:], bvec[:], -1.0, 0.5, ALU.mult, ALU.add),
                reads=[bvec], writes=[bvec])

            def pre(e, yy, t=t):
                return e.tensor_scalar(yy[:], t[:], kvec[:], bvec[:],
                                       ALU.mult, ALU.add)

            round_half_up(H, ya, ia, ixf, ga, pre, [t, kvec, bvec])
            ixb = ixb2[tb % 2]
            xqT = xqT2[ch % 2]
            p.op("vector", lambda e, ixb=ixb: e.tensor_copy(ixb[:], ixf[:]),
                 reads=[ixf], writes=[ixb])
            for db in range(8):
                tpx = tpp[db % 2]
                p.op("tensor", lambda e, db=db, tpx=tpx, ixb=ixb: e.transpose(
                    tpx[:], ixb[:, db * 128:(db + 1) * 128], ident_b[:]),
                    reads=[ixb, ident_b], writes=[tpx])
                p.op("scalar", lambda e, db=db, tb4=tb4, tpx=tpx, xqT=xqT:
                     e.activation(
                    xqT[db][:, tb4 * 128:(tb4 + 1) * 128], tpx[:], AF.Copy,
                    bias=0.0, scale=1.0),
                    reads=[tpx], writes=[xqT[db], f"xqTall{ch % 2}"])
        xqT = xqT2[ch % 2]
        for wi, (ws, spd) in enumerate(((wup_s, sp_up), (wgate_s, sp_gate))):
            for mi in range(NIT):
                pmm = mmp[mi % 2]
                for kb in range(8):
                    p.op("tensor", lambda e, wi=wi, mi=mi, kb=kb, pmm=pmm,
                         ws=ws, xqT=xqT: e.matmul(
                             pmm[:], ws[kb][:, mi * 128:(mi + 1) * 128],
                             xqT[kb][:], start=(kb == 0), stop=(kb == 7)),
                         reads=["wupg_all", f"xqTall{ch % 2}"], writes=[pmm])
                ut = ug[(wi * 2 + mi) % 4]
                p.op("scalar", lambda e, pmm=pmm, ut=ut: e.activation(
                    ut[:], pmm[:], AF.Copy, bias=0.0, scale=s1[:]),
                    reads=[pmm, s1], writes=[ut])
                p.op("vector", lambda e, ut=ut: e.tensor_reduce(
                    rtmp[:], ut[:], AX.X, ALU.max, apply_absolute_value=True),
                    reads=[ut], writes=[rtmp])
                p.op("vector", lambda e, wi=wi: e.tensor_tensor(
                    amax2[:, wi:wi + 1], amax2[:, wi:wi + 1], rtmp[:],
                    ALU.max), reads=[amax2, rtmp], writes=[amax2])
                p.dma("sync", lambda e, spd=spd, mi=mi, ch=ch, ut=ut:
                      e.dma_start(out=spd.ap()[mi * 128:(mi + 1) * 128,
                                               ch * 512:(ch + 1) * 512],
                                  in_=ut[:]),
                      reads=[ut], writes=[f"{spd.name}_{ch}_{mi}"],
                      stream="st")

    ctx.pop()
    if doA and not fused:
        export_mx(H, reduce_acc(H, cid + "g23", amax2, 2))
    doB = "B" in stages
    doC = "C" in stages
    doD = "D" in stages
    if fused:
        gm23 = global_max(H, cid + "g23", amax2, 2, pbc)
    elif doB or doC or doD:
        gm23 = consume_gmax(H, cid + "g23", "g23_in")
    if fused or doC or doD:
        s2, r2, s2v = quant_scales(H, cid + "q2", gm23, 0, pbc)
    if fused or doB or doC or doD:
        s3, r3, s3v = quant_scales(H, cid + "q3", gm23, 1, pbc)

    WBe = min(2048, T) if not fused else 512
    yb = [ctx.sbuf(cid + "_By", [128, WBe]) for _ in range(2)]
    ib = [ctx.sbuf(cid + "_Bi", [128, WBe], I32) for _ in range(2)]
    gb = [ctx.sbuf(cid + "_Bgg", [128, WBe]) for _ in range(2)]
    rtl = [ctx.sbuf(cid + "_rtl", [128, 1]) for _ in range(2)]
    amax4 = ctx.sbuf(cid + "_am4", [128, 2])
    ctx.push()
    gtl = [ctx.sbuf(cid + "_Bg", [128, WBe]) for _ in range(2)]
    i3f = [ctx.sbuf(cid + "_i3f", [128, WBe]) for _ in range(2)]
    sig = [ctx.sbuf(cid + "_sig", [128, WBe]) for _ in range(2)]
    jf = [ctx.sbuf(cid + "_jf", [128, WBe]) for _ in range(2)]
    ppt = [ctx.sbuf(cid + "_pp", [128, WBe]) for _ in range(2)]
    if doB:
        p.op("vector", lambda e: e.memset(amax4[:], 0.0), writes=[amax4])

    def rhu_split(par, src, src_buf, rbc, rbc_buf, y, i32, g, dst, eng2):
        p.op("vector", lambda e: e.tensor_scalar(
            y[:], src[:], rbc[:], 0.5, ALU.mult, ALU.add),
            reads=[src_buf, rbc_buf], writes=[y])
        p.op("scalar", lambda e: e.activation(i32[:], y[:], AF.Copy,
                                              bias=0.0, scale=1.0),
             reads=[y], writes=[i32])
        p.op("scalar", lambda e: e.activation(dst[:], i32[:], AF.Copy,
                                              bias=0.0, scale=1.0),
             reads=[i32], writes=[dst])
        p.op("vector", lambda e: e.tensor_tensor(g[:], dst[:], y[:],
                                                 ALU.is_gt),
             reads=[dst, y], writes=[g])
        p.op(eng2, lambda e: e.tensor_tensor(dst[:], dst[:], g[:],
                                             ALU.subtract),
             reads=[dst, g], writes=[dst])

    for cg in (range(T // WBe) if doB else ()):
        for mi in range(NIT):
            k = mi % 2
            gt = gtl[k]
            p.dma("sync", lambda e, mi=mi, cg=cg, gt=gt: e.dma_start(
                out=gt[:], in_=sp_gate.ap()[mi * 128:(mi + 1) * 128,
                                            cg * WBe:(cg + 1) * WBe]),
                reads=[f"sp_gate_{cc}_{mi}" for cc in range(
                    cg * (WBe // 512), (cg + 1) * (WBe // 512))],
                writes=[gt], stream="ld")
            rhu_split("B", gt, gt, r3, r3, yb[k], ib[k], gb[k], i3f[k],
                      "gpsimd")
            p.op("scalar", lambda e, k=k: e.activation(
                sig[k][:], i3f[k][:], AF.Sigmoid, bias=0.0, scale=s3[:]),
                reads=[i3f[k], s3], writes=[sig[k]])
            p.op("vector", lambda e, k=k: e.tensor_scalar(
                yb[k][:], sig[k][:], 65536.0, 0.5, ALU.mult, ALU.add),
                reads=[sig[k]], writes=[yb[k]])
            p.op("scalar", lambda e, k=k: e.activation(
                ib[k][:], yb[k][:], AF.Copy, bias=0.0, scale=1.0),
                reads=[yb[k]], writes=[ib[k]])
            p.op("scalar", lambda e, k=k: e.activation(
                jf[k][:], ib[k][:], AF.Copy, bias=0.0, scale=1.0),
                reads=[ib[k]], writes=[jf[k]])
            p.op("vector", lambda e, k=k: e.tensor_tensor(
                gb[k][:], jf[k][:], yb[k][:], ALU.is_gt),
                reads=[jf[k], yb[k]], writes=[gb[k]])
            p.op("gpsimd", lambda e, k=k: e.tensor_tensor(
                jf[k][:], jf[k][:], gb[k][:], ALU.subtract),
                reads=[jf[k], gb[k]], writes=[jf[k]])
            p.op("vector", lambda e, k=k: e.tensor_scalar(
                jf[k][:], jf[k][:], 65535.0, None, ALU.min),
                reads=[jf[k]], writes=[jf[k]])
            p.op("vector", lambda e, k=k: e.tensor_tensor(
                ppt[k][:], i3f[k][:], jf[k][:], ALU.mult),
                reads=[i3f[k], jf[k]], writes=[ppt[k]])
            p.op("vector", lambda e, k=k: e.tensor_reduce(
                rtl[k][:], ppt[k][:], AX.X, ALU.max,
                apply_absolute_value=True),
                reads=[ppt[k]], writes=[rtl[k]])
            p.op("vector", lambda e, k=k: e.tensor_tensor(
                amax4[:, 0:1], amax4[:, 0:1], rtl[k][:], ALU.max),
                reads=[amax4, rtl[k]], writes=[amax4])
            p.dma("sync", lambda e, mi=mi, cg=cg, k=k: e.dma_start(
                out=sp_P.ap()[mi * 128:(mi + 1) * 128,
                              cg * WBe:(cg + 1) * WBe], in_=ppt[k][:]),
                reads=[ppt[k]],
                writes=[f"sp_P_{cc}_{mi}" for cc in range(
                    cg * (WBe // 512), (cg + 1) * (WBe // 512))],
                stream="st")

    ctx.pop()
    p.barrier()
    if doB and not fused:
        export_mx(H, reduce_acc(H, cid + "g4", amax4, 1))
    if fused:
        gm4 = global_max(H, cid + "g4", amax4, 1, pbc)
    elif doC or doD:
        gm4 = consume_gmax(H, cid + "g4", "g4_in")
    if fused or doC or doD:
        s3f = ctx.sbuf(cid + "_s3f", [1, 1])
        s4v = ctx.sbuf(cid + "_s4v", [1, 1])
        c4v = ctx.sbuf(cid + "_c4v", [1, 1])
        p.op("vector", lambda e: e.tensor_scalar(
            s3f[:], s3v[:], 1.0 / 65536.0, None, ALU.mult),
            reads=[s3v], writes=[s3f])
        p.op("vector", lambda e: e.tensor_tensor(
            s4v[:], gm4[0:1, 0:1], s3f[:], ALU.mult),
            reads=[gm4, s3f], writes=[s4v])
        p.op("vector", lambda e: e.tensor_scalar(
            s4v[:], s4v[:], 1.0 / QMAX, 1e-12, ALU.mult, ALU.max),
            reads=[s4v], writes=[s4v])
        p.op("vector", lambda e: e.reciprocal(c4v[:], s4v[:]),
             reads=[s4v], writes=[c4v])
        p.op("vector", lambda e: e.tensor_tensor(
            c4v[:], c4v[:], s3f[:], ALU.mult), reads=[c4v, s3f], writes=[c4v])
        c4 = ctx.sbuf(cid + "_c4b", [128, 1])
        scalar_bcast(H, cid + "c4", c4v[0:1, 0:1], c4v, pbc, c4)

    ctx.push()
    upl = [ctx.sbuf(cid + "_Cu", [128, WBe]) for _ in range(2)]
    ptl = [ctx.sbuf(cid + "_Cp", [128, WBe]) for _ in range(2)]
    i4f = [ctx.sbuf(cid + "_i4f", [128, WBe]) for _ in range(2)]
    iuf = [ctx.sbuf(cid + "_iuf", [128, WBe]) for _ in range(2)]
    oo = [ctx.sbuf(cid + "_oo", [128, WBe]) for _ in range(2)]
    amax5 = ctx.sbuf(cid + "_am5", [128, 2])
    if doC:
        p.op("vector", lambda e: e.memset(amax5[:], 0.0), writes=[amax5])

    for cg in (range(T // WBe) if doC else ()):
        for mi in range(NIT):
            k = mi % 2
            ut = upl[k]
            pt = ptl[k]
            rg = list(range(cg * (WBe // 512), (cg + 1) * (WBe // 512)))
            p.dma("sync", lambda e, mi=mi, cg=cg, ut=ut: e.dma_start(
                out=ut[:], in_=sp_up.ap()[mi * 128:(mi + 1) * 128,
                                          cg * WBe:(cg + 1) * WBe]),
                reads=[f"sp_up_{cc}_{mi}" for cc in rg], writes=[ut],
                stream="ld")
            p.dma("scalar", lambda e, mi=mi, cg=cg, pt=pt: e.dma_start(
                out=pt[:], in_=sp_P.ap()[mi * 128:(mi + 1) * 128,
                                         cg * WBe:(cg + 1) * WBe]),
                reads=[f"sp_P_{cc}_{mi}" for cc in rg], writes=[pt])
            rhu_split("C4", pt, pt, c4, c4, yb[k], ib[k], gb[k], i4f[k],
                      "vector")
            rhu_split("C2", ut, ut, r2, r2, yb[k], ib[k], gb[k], iuf[k],
                      "gpsimd")
            p.op("vector", lambda e, k=k: e.tensor_tensor(
                oo[k][:], i4f[k][:], iuf[k][:], ALU.mult),
                reads=[i4f[k], iuf[k]], writes=[oo[k]])
            p.op("vector", lambda e, k=k: e.tensor_reduce(
                rtl[k][:], oo[k][:], AX.X, ALU.max, apply_absolute_value=True),
                reads=[oo[k]], writes=[rtl[k]])
            p.op("vector", lambda e, k=k: e.tensor_tensor(
                amax5[:, 0:1], amax5[:, 0:1], rtl[k][:], ALU.max),
                reads=[amax5, rtl[k]], writes=[amax5])
            p.dma("sync", lambda e, mi=mi, cg=cg, k=k: e.dma_start(
                out=sp_o.ap()[mi * 128:(mi + 1) * 128,
                              cg * WBe:(cg + 1) * WBe], in_=oo[k][:]),
                reads=[oo[k]], writes=[f"sp_o_{cc}_{mi}" for cc in rg],
                stream="st")

    ctx.pop()
    p.barrier()
    if doC and not fused:
        export_mx(H, reduce_acc(H, cid + "g5", amax5, 1))
    if fused:
        gm5 = global_max(H, cid + "g5", amax5, 1, pbc)
    elif doD:
        gm5 = consume_gmax(H, cid + "g5", "g5_in")
    if fused or doD:
        s42 = ctx.sbuf(cid + "_s42", [1, 1])
        s5v = ctx.sbuf(cid + "_s5v", [1, 1])
        c5v = ctx.sbuf(cid + "_c5v", [1, 1])
        sc5v = ctx.sbuf(cid + "_sc5v", [1, 1])
        p.op("vector", lambda e: e.tensor_tensor(
            s42[:], s4v[:], s2v[:], ALU.mult), reads=[s4v, s2v], writes=[s42])
        p.op("vector", lambda e: e.tensor_tensor(
            s5v[:], gm5[0:1, 0:1], s42[:], ALU.mult),
            reads=[gm5, s42], writes=[s5v])
        p.op("vector", lambda e: e.tensor_scalar(
            s5v[:], s5v[:], 1.0 / QMAX, 1e-12, ALU.mult, ALU.max),
            reads=[s5v], writes=[s5v])
        p.op("vector", lambda e: e.reciprocal(c5v[:], s5v[:]),
             reads=[s5v], writes=[c5v])
        p.op("vector", lambda e: e.tensor_tensor(
            c5v[:], c5v[:], s42[:], ALU.mult), reads=[c5v, s42], writes=[c5v])
        p.op("vector", lambda e: e.tensor_scalar(
            sc5v[:], s5v[:], 0.5, None, ALU.mult), reads=[s5v], writes=[sc5v])
        c5 = ctx.sbuf(cid + "_c5b", [128, 1])
        sc5 = ctx.sbuf(cid + "_sc5b", [128, 1])
        scalar_bcast(H, cid + "c5", c5v[0:1, 0:1], c5v, pbc, c5)
        scalar_bcast(H, cid + "sc5", sc5v[0:1, 0:1], sc5v, pbc, sc5)

    ctx.push()
    wd_s = [ctx.sbuf(cid + "_wd", [128, D], BF16) for _ in range(NIT)]
    for mi in (range(NIT) if doD else ()):
        p.dma("sync", lambda e, mi=mi: e.dma_start(
            out=wd_s[mi][:], in_=dram["wdownT"].ap()[mi * 128:(mi + 1) * 128, :]),
            reads=[f"wdownT_{it}_{mi}" for it in range(D // 128)],
            writes=[wd_s[mi], "wd_all"], stream="ld")
    otl = [ctx.sbuf(cid + "_Do", [128, 512]) for _ in range(2)]
    i5b = [ctx.sbuf(cid + "_i5b", [128, 512], BF16) for _ in range(NIT)]
    i5f = [ctx.sbuf(cid + "_i5f", [128, 512]) for _ in range(2)]
    xr = [ctx.sbuf(cid + "_Dx", [128, D]) for _ in range(2)]
    xo = [ctx.sbuf(cid + "_Dxo", [128, D]) for _ in range(2)]

    for ch in (range(NCH) if doD else ()):
        for mi in range(NIT):
            k = mi % 2
            ot = otl[k]
            p.dma("sync", lambda e, mi=mi, ch=ch, ot=ot: e.dma_start(
                out=ot[:], in_=sp_o.ap()[mi * 128:(mi + 1) * 128,
                                         ch * 512:(ch + 1) * 512]),
                reads=[f"sp_o_{ch}_{mi}"], writes=[ot], stream="ld")
            def prec5(e, yy, ot=ot):
                return e.tensor_scalar(yy[:], ot[:], c5[:], 0.5,
                                       ALU.mult, ALU.add)

            round_half_up(H, yb[k][:, 0:512], ib[k][:, 0:512], i5f[k],
                          gb[k][:, 0:512], prec5, [ot, c5])
            p.op("scalar", lambda e, mi=mi, k=k: e.activation(
                i5b[mi][:], i5f[k][:], AF.Copy, bias=0.0, scale=1.0),
                reads=[i5f[k]], writes=[i5b[mi], "i5b_all"])
        if True:
          chs = 0
          for tb4 in range(4):
            tb = ch * 4 + tb4
            xrt = xr[tb % 2]
            xot = xo[tb % 2]
            p.dma("sync", lambda e, tb=tb, xrt=xrt: e.dma_start(
                out=xrt[:], in_=x_src.ap()[tb * 128:(tb + 1) * 128, :]),
                reads=[xsrc_r(tb)], writes=[xrt], stream="ld")
            for dh in range(2):
                pd = dps[dh]
                for mi in range(NIT):
                    p.op("tensor", lambda e, mi=mi, tb4=tb4, dh=dh, pd=pd,
                         chs=chs: e.matmul(
                             pd[:],
                             i5b[mi][:, tb4 * 128:(tb4 + 1) * 128],
                             wd_s[mi][:, dh * 512:(dh + 1) * 512],
                             start=(mi == 0), stop=(mi == NIT - 1)),
                         reads=["i5b_all", "wd_all"], writes=[pd])
                p.op("vector", lambda e, dh=dh, xrt=xrt, xot=xot, pd=pd:
                     e.scalar_tensor_tensor(
                         xot[:, dh * 512:(dh + 1) * 512], pd[:], sc5[:],
                         xrt[:, dh * 512:(dh + 1) * 512], ALU.mult, ALU.add),
                     reads=[pd, sc5, xrt], writes=[xot])
            p.dma("sync", lambda e, tb=tb, xot=xot: e.dma_start(
                out=x_dst.ap()[tb * 128:(tb + 1) * 128, :], in_=xot[:]),
                reads=[xot], writes=[f"xdst_{x_dst.name}_{tb}"],
                stream="st")
    ctx.pop()



def mixer_copy(H, x_src, x_dst):
    ctx, p = H["ctx"], H["p"]
    T = H["T"]
    t = ctx.sbuf("mbcp", [128, D])
    for tb in range(T // 128):
        p.dma("sync", lambda e, tb=tb: e.dma_start(
            out=t[:], in_=x_src.ap()[tb * 128:(tb + 1) * 128, :]),
            reads=[f"xdst_{x_src.name}_{tb}"], writes=[t])
        p.dma("sync", lambda e, tb=tb: e.dma_start(
            out=x_dst.ap()[tb * 128:(tb + 1) * 128, :], in_=t[:]),
            reads=[t], writes=[f"xdst_{x_dst.name}_{tb}"])


def mamba(H, x_src, x_dst):
    ctx, p, dram = H["ctx"], H["p"], H["nc"] and H["dram"]
    p = H["p"]
    ctx = H["ctx"]
    T, NT = H["T"], H["NT"]
    NSC = T // 512
    ident_b = H["ident_b"]

    def xsrc_r(tb):
        return f"xdst_{x_src.name}_{tb}"

    ps = [ctx.psum("mb_ps", [128, 512]) for _ in range(6)]
    p96 = ctx.psum("mb_p96", [96, 512])
    tpp0 = ctx.psum("mb_tp", [128, 256], BF16)
    tpp = [tpp0[:, 0:128], tpp0[:, 128:256]]

    winb_d = dram["winb"]
    woutb_d = dram["woutb"]
    wx_b = ctx.sbuf("mb_wxb", [128, 16 * 96], BF16)
    wdt_b = ctx.sbuf("mb_wdtb", [64, DI], BF16)
    ctx.push()
    wtmp = ctx.sbuf("mb_wtmp", [128, 2 * DI])
    wtmpb = ctx.sbuf("mb_wtmpb", [128, 2 * DI], BF16)
    for kb in range(8):
        p.dma("sync", lambda e, kb=kb: e.dma_start(
            out=wtmp[:], in_=dram["W_in"].ap()[kb * 128:(kb + 1) * 128, :]),
            writes=[wtmp])
        p.op("vector", lambda e: e.tensor_copy(wtmpb[:], wtmp[:]),
             reads=[wtmp], writes=[wtmpb])
        p.dma("sync", lambda e, kb=kb: e.dma_start(
            out=winb_d.ap()[kb * 128:(kb + 1) * 128, :], in_=wtmpb[:]),
            reads=[wtmpb], writes=[f"winb_{kb}"])
    for kt in range(16):
        p.dma("sync", lambda e, kt=kt: e.dma_start(
            out=wtmp[:, 0:D], in_=dram["W_out"].ap()[kt * 128:(kt + 1) * 128, :]),
            writes=[wtmp])
        p.op("vector", lambda e: e.tensor_copy(wtmpb[:, 0:D], wtmp[:, 0:D]),
             reads=[wtmp], writes=[wtmpb])
        p.dma("sync", lambda e, kt=kt: e.dma_start(
            out=woutb_d.ap()[kt * 128:(kt + 1) * 128, :], in_=wtmpb[:, 0:D]),
            reads=[wtmpb], writes=[f"woutb_{kt}"])

    for kt in range(16):
        p.dma("sync", lambda e, kt=kt: e.dma_start(
            out=wtmp[:, 0:96], in_=dram["W_x"].ap()[kt * 128:(kt + 1) * 128, :]),
            writes=[wtmp])
        p.op("vector", lambda e, kt=kt: e.tensor_copy(
            wx_b[:, kt * 96:(kt + 1) * 96], wtmp[:, 0:96]),
            reads=[wtmp], writes=[wx_b])
    p.dma("sync", lambda e: e.dma_start(
        out=wtmp[0:64, 0:DI], in_=dram["W_dt"].ap()), writes=[wtmp])
    p.op("vector", lambda e: e.tensor_copy(wdt_b[:], wtmp[0:64, 0:DI]),
         reads=[wtmp], writes=[wdt_b])
    ctx.pop()
    p.barrier()
    win_c = [ctx.sbuf("mb_winc", [128, 8 * 128], BF16) for _ in range(2)]
    wout_s = [ctx.sbuf("mb_wouts", [128, D], BF16) for _ in range(2)]


    convw = ctx.sbuf("mb_convw", [128, 64])
    convb2 = ctx.sbuf("mb_convb2", [128, 16])
    bdt = ctx.sbuf("mb_bdt", [128, 16])
    dvec = ctx.sbuf("mb_dvec", [128, 16])
    a_t = ctx.sbuf("mb_at", [128, 256])
    p.dma("sync", lambda e: e.dma_start(out=convw[:], in_=dram["conv_wt"].ap()),
          writes=[convw])
    p.dma("sync", lambda e: e.dma_start(out=convb2[:], in_=dram["conv_b"].ap()),
          writes=[convb2])
    p.dma("sync", lambda e: e.dma_start(out=bdt[:], in_=dram["b_dt"].ap()),
          writes=[bdt])
    p.dma("sync", lambda e: e.dma_start(out=dvec[:], in_=dram["Dvec"].ap()),
          writes=[dvec])
    p.dma("sync", lambda e: e.dma_start(out=a_t[:], in_=dram["A_log_t"].ap()),
          writes=[a_t])
    p.op("scalar", lambda e: e.activation(a_t[:], a_t[:], AF.Exp,
                                          bias=0.0, scale=1.0),
         reads=[a_t], writes=[a_t])
    p.op("vector", lambda e: e.tensor_scalar(
        a_t[:], a_t[:], -1.0, None, ALU.mult), reads=[a_t], writes=[a_t])
    sel16 = ctx.sbuf("mb_sel", [16, 16 * 128], BF16)
    p.dma("sync", lambda e: e.dma_start(out=sel16[:], in_=dram["sel16"].ap()),
          writes=[sel16])

    h_state = ctx.sbuf("mb_h", [128, 256])
    xi_tail = ctx.sbuf("mb_tail", [128, 48])
    p.op("vector", lambda e: e.memset(h_state[:], 0.0),
         writes=[h_state] + [f"hst_{c}" for c in range(256)])
    p.op("vector", lambda e: e.memset(xi_tail[:], 0.0), writes=[xi_tail])

    xc2 = [ctx.sbuf("mb_x2", [128, D])]
    lnT = [ctx.sbuf("mb_lnT", [128, 512], BF16) for _ in range(8)]
    lnb = ctx.sbuf("mb_lnb", [128, D], BF16)
    mloc = ctx.sbuf("mb_m", [128, 1])
    rloc = ctx.sbuf("mb_r", [128, 1])
    bn6 = ctx.sbuf("mb_bn6", [128, 12])
    bn2 = ctx.sbuf("mb_bn2", [128, 2])
    vt = ctx.sbuf("mb_vt", [128, 1])
    xi = [ctx.sbuf("mb_xi", [128, 516]) for _ in range(2)]
    xc_f = [ctx.sbuf("mb_xc", [128, 512]) for _ in range(16)]
    xcb = [ctx.sbuf("mb_xcb", [128, 512], BF16) for _ in range(2)]
    del_f = [ctx.sbuf("mb_del", [128, 512]) for _ in range(16)]
    cvt = [ctx.sbuf("mb_cv", [128, 512]) for _ in range(2)]
    cv2 = [ctx.sbuf("mb_cv2", [128, 512]) for _ in range(2)]
    ztile = [ctx.sbuf("mb_zt", [128, 512])]
    xdbl = ctx.sbuf("mb_xdbl", [96, 512])
    dtb = ctx.sbuf("mb_dtb", [64, 512], BF16)
    bc32 = ctx.sbuf("mb_bc32", [32, 512])
    bhf32 = ctx.sbuf("mb_bhf32", [32, 512])
    bch32 = ctx.sbuf("mb_bch", [32, 512], BF16)
    bcl32 = ctx.sbuf("mb_bcl", [32, 512], BF16)
    bCh = ctx.sbuf("mb_bCh", [16, 512], BF16)
    bCl = ctx.sbuf("mb_bCl", [16, 512], BF16)
    u_t = [ctx.sbuf("mb_u", [128, 512])]
    da_s = [ctx.sbuf("mb_da", [128, 512]) for _ in range(2)]
    h_t = [ctx.sbuf("mb_ht", [128, 512]) for _ in range(2)]
    dbu = [ctx.sbuf("mb_dbu", [128, 512]) for _ in range(2)]
    pb_t = [ctx.sbuf("mb_pb", [128, 512]) for _ in range(2)]
    pbb_t = [ctx.sbuf("mb_pbb", [128, 512], BF16) for _ in range(2)]
    bcB = [ctx.sbuf("mb_bcB", [128, 512], BF16) for _ in range(16)]
    bcC = [ctx.sbuf("mb_bcC", [128, 512]) for _ in range(16)]
    w_t = [ctx.sbuf("mb_wt2", [128, 512])]
    sz_t = [ctx.sbuf("mb_sz", [128, 512])]
    omix = [ctx.sbuf("mb_om", [128, 512], BF16) for _ in range(2)]
    xo_t = [ctx.sbuf("mb_xo", [128, 512]) for _ in range(2)]

    for sc in range(NSC):
        t0 = sc * 512
        for tb4 in range(4):
            tb = sc * 4 + tb4
            xt2 = xc2[0]
            p.dma("sync", lambda e, tb=tb, xt2=xt2: e.dma_start(
                out=xt2[:], in_=x_src.ap()[tb * 128:(tb + 1) * 128, :]),
                reads=[xsrc_r(tb)], writes=[xt2], stream="ld")
            p.op("vector", lambda e, xt2=xt2: e.bn_stats(bn6[:, 0:6],
                                                         xt2[:, 0:512]),
                 reads=[xt2], writes=[bn6])
            p.op("vector", lambda e, xt2=xt2: e.bn_stats(bn6[:, 6:12],
                                                         xt2[:, 512:1024]),
                 reads=[xt2], writes=[bn6])
            p.op("vector", lambda e: e.bn_aggr(bn2[:], bn6[:]),
                 reads=[bn6], writes=[bn2])
            p.op("scalar", lambda e: e.activation(vt[:], bn2[:, 1:2], AF.Ln,
                                                  bias=H["eps_b"][:], scale=1.0),
                 reads=[bn2, H["eps_b"]], writes=[vt])
            p.op("scalar", lambda e: e.activation(rloc[:], vt[:], AF.Exp,
                                                  bias=0.0, scale=-0.5),
                 reads=[vt], writes=[rloc])
            p.op("vector", lambda e: e.tensor_tensor(
                mloc[:], bn2[:, 0:1], rloc[:], ALU.mult),
                reads=[bn2, rloc], writes=[mloc])
            p.op("vector", lambda e: e.tensor_scalar(
                mloc[:], mloc[:], -1.0, None, ALU.mult),
                reads=[mloc], writes=[mloc])
            p.op("vector", lambda e, xt2=xt2: e.tensor_scalar(
                lnb[:], xt2[:], rloc[:], mloc[:], ALU.mult, ALU.add),
                reads=[xt2, rloc, mloc], writes=[lnb])
            for db in range(8):
                tpx = tpp[db % 2]
                p.op("tensor", lambda e, db=db, tpx=tpx: e.transpose(
                    tpx[:], lnb[:, db * 128:(db + 1) * 128], ident_b[:]),
                    reads=[lnb, ident_b], writes=[tpx])
                p.op("scalar", lambda e, db=db, tb4=tb4, tpx=tpx: e.activation(
                    lnT[db][:, tb4 * 128:(tb4 + 1) * 128], tpx[:], AF.Copy,
                    bias=0.0, scale=1.0),
                    reads=[tpx], writes=[lnT[db], "lnT_all"])
        for mi in range(32):
            pmm = ps[mi % 2]
            wc = win_c[mi % 2]
            p.dma("sync", lambda e, mi=mi, wc=wc: e.dma_start(
                out=wc[:],
                in_=bass.AP(winb_d, mi * 128,
                            [[2 * DI, 128], [128 * 2 * DI, 8], [1, 128]])),
                reads=[f"winb_{kb}" for kb in range(8)], writes=[wc],
                stream="ld")
            for kb in range(8):
                p.op("tensor", lambda e, mi=mi, kb=kb, pmm=pmm, wc=wc:
                     e.matmul(pmm[:], wc[:, kb * 128:(kb + 1) * 128],
                              lnT[kb][:], start=(kb == 0), stop=(kb == 7)),
                     reads=[wc, "lnT_all"], writes=[pmm])
            if mi < 16:
                dtile = mi
                xit = xi[dtile % 2]
                p.op("scalar", lambda e, pmm=pmm, xit=xit: e.activation(
                    xit[:, 3:515], pmm[:], AF.Copy, bias=0.0, scale=1.0),
                    reads=[pmm], writes=[xit])
                p.op("vector", lambda e, dtile=dtile, xit=xit: e.tensor_copy(
                    xit[:, 0:3], xi_tail[:, dtile * 3:dtile * 3 + 3]),
                    reads=[xi_tail], writes=[xit])
                p.op("vector", lambda e, dtile=dtile, xit=xit: e.tensor_copy(
                    xi_tail[:, dtile * 3:dtile * 3 + 3], xit[:, 512:515]),
                    reads=[xit], writes=[xi_tail])
                cva = cvt[dtile % 2]
                cvb = cv2[dtile % 2]
                p.op("vector", lambda e, dtile=dtile, xit=xit, cva=cva:
                     e.tensor_scalar(
                    cva[:], xit[:, 0:512], convw[:, dtile * 4:dtile * 4 + 1],
                    None, ALU.mult),
                    reads=[xit, convw], writes=[cva])
                for k in range(1, 4):
                    p.op("vector", lambda e, dtile=dtile, k=k, xit=xit,
                         cva=cva: e.scalar_tensor_tensor(
                             cva[:], xit[:, k:k + 512],
                             convw[:, dtile * 4 + k:dtile * 4 + k + 1],
                             cva[:], ALU.mult, ALU.add),
                         reads=[xit, convw, cva], writes=[cva])
                p.op("vector", lambda e, dtile=dtile, cva=cva: e.tensor_scalar(
                    cva[:], cva[:], convb2[:, dtile:dtile + 1], None, ALU.add),
                    reads=[cva, convb2], writes=[cva])
                p.op("scalar", lambda e, cva=cva, cvb=cvb: e.activation(
                    cvb[:], cva[:], AF.Tanh, bias=0.0, scale=0.5),
                     reads=[cva], writes=[cvb])
                p.op("vector", lambda e, cvb=cvb: e.tensor_scalar(
                    cvb[:], cvb[:], 0.5, 0.5, ALU.mult, ALU.add),
                    reads=[cvb], writes=[cvb])
                p.op("vector", lambda e, dtile=dtile, cva=cva, cvb=cvb:
                     e.tensor_tensor(
                    xc_f[dtile][:], cva[:], cvb[:], ALU.mult),
                    reads=[cva, cvb], writes=[xc_f[dtile]])
                xcr = xcb[dtile % 2]
                p.op("vector", lambda e, dtile=dtile, xcr=xcr: e.tensor_copy(
                    xcr[:], xc_f[dtile][:]),
                    reads=[xc_f[dtile]], writes=[xcr])
                p.op("tensor", lambda e, dtile=dtile, xcr=xcr: e.matmul(
                    p96[:], wx_b[:, dtile * 96:(dtile + 1) * 96], xcr[:],
                    start=(dtile == 0), stop=(dtile == 15)),
                    reads=[wx_b, xcr], writes=[p96])
            else:
                dtile = mi - 16
                zt = ztile[0]
                p.op("scalar", lambda e, pmm=pmm, zt=zt: e.activation(
                    zt[:], pmm[:], AF.Copy, bias=0.0, scale=1.0),
                    reads=[pmm], writes=[zt])
                p.dma("sync", lambda e, dtile=dtile, sc=sc, zt=zt: e.dma_start(
                    out=dram["sp_z"].ap()[dtile * 128:(dtile + 1) * 128,
                                          t0:t0 + 512], in_=zt[:]),
                    reads=[zt], writes=[f"sp_z_{sc}_{dtile}"], stream="st")
        p.op("scalar", lambda e: e.activation(xdbl[:], p96[:], AF.Copy,
                                              bias=0.0, scale=1.0),
             reads=[p96], writes=[xdbl])
        p.op("vector", lambda e: e.tensor_copy(dtb[:], xdbl[0:64, :]),
             reads=[xdbl], writes=[dtb])
        for mt in range(16):
            pmd = ps[2 + (mt % 2)]
            p.op("tensor", lambda e, mt=mt, pmd=pmd: e.matmul(
                pmd[:], wdt_b[:, mt * 128:(mt + 1) * 128], dtb[:],
                start=True, stop=True),
                reads=[wdt_b, dtb], writes=[pmd])
            cva = cvt[mt % 2]
            cvb = cv2[mt % 2]
            p.op("vector", lambda e, mt=mt, pmd=pmd, cva=cva: e.tensor_scalar(
                cva[:], pmd[:], bdt[:, mt:mt + 1], None, ALU.add),
                reads=[pmd, bdt], writes=[cva])
            p.op("scalar", lambda e, cva=cva, cvb=cvb: e.activation(
                cvb[:], cva[:], AF.Abs, bias=0.0, scale=1.0),
                 reads=[cva], writes=[cvb])
            p.op("scalar", lambda e, cvb=cvb: e.activation(
                cvb[:], cvb[:], AF.Exp, bias=0.0, scale=-1.0),
                 reads=[cvb], writes=[cvb])
            p.op("scalar", lambda e, cvb=cvb: e.activation(
                cvb[:], cvb[:], AF.Ln, bias=1.0, scale=1.0),
                 reads=[cvb], writes=[cvb])
            p.op("vector", lambda e, cva=cva: e.tensor_scalar(
                cva[:], cva[:], 0.0, None, ALU.max), reads=[cva], writes=[cva])
            p.op("vector", lambda e, mt=mt, cva=cva, cvb=cvb: e.tensor_tensor(
                del_f[mt][:], cva[:], cvb[:], ALU.add),
                reads=[cva, cvb], writes=[del_f[mt]])
        p.op("scalar", lambda e: e.activation(bc32[:], p96[64:96, :],
                                              AF.Copy, bias=0.0, scale=1.0),
             reads=[p96], writes=[bc32])
        p.op("vector", lambda e: e.tensor_copy(bch32[:], bc32[:]),
             reads=[bc32], writes=[bch32])
        p.op("vector", lambda e: e.tensor_copy(bhf32[:], bch32[:]),
             reads=[bch32], writes=[bhf32])
        p.op("vector", lambda e: e.tensor_tensor(
            bhf32[:], bc32[:], bhf32[:], ALU.subtract), reads=[bc32, bhf32],
            writes=[bhf32])
        p.op("vector", lambda e: e.tensor_copy(bcl32[:], bhf32[:]),
             reads=[bhf32], writes=[bcl32])
        p.dma("scalar", lambda e: e.dma_start(out=bCh[:], in_=bch32[16:32, :]),
              reads=[bch32], writes=[bCh])
        p.dma("scalar", lambda e: e.dma_start(out=bCl[:], in_=bcl32[16:32, :]),
              reads=[bcl32], writes=[bCl])
        for n in range(16):
            pBC = ps[4 + (n % 2)]
            p.op("tensor", lambda e, n=n, pBC=pBC: e.matmul(
                pBC[:], sel16[:, n * 128:(n + 1) * 128], bch32[0:16, :],
                start=True, stop=False),
                reads=[sel16, bch32], writes=[pBC])
            p.op("tensor", lambda e, n=n, pBC=pBC: e.matmul(
                pBC[:], sel16[:, n * 128:(n + 1) * 128], bcl32[0:16, :],
                start=False, stop=True),
                reads=[sel16, bcl32], writes=[pBC])
            p.op("scalar", lambda e, n=n, pBC=pBC: e.activation(
                bcB[n][:], pBC[:], AF.Copy, bias=0.0, scale=1.0),
                reads=[pBC], writes=[bcB[n]])
        for n in range(16):
            pBC = ps[4 + (n % 2)]
            p.op("tensor", lambda e, n=n, pBC=pBC: e.matmul(
                pBC[:], sel16[:, n * 128:(n + 1) * 128], bCh[:],
                start=True, stop=False),
                reads=[sel16, bCh], writes=[pBC])
            p.op("tensor", lambda e, n=n, pBC=pBC: e.matmul(
                pBC[:], sel16[:, n * 128:(n + 1) * 128], bCl[:],
                start=False, stop=True),
                reads=[sel16, bCl], writes=[pBC])
            p.op("scalar", lambda e, n=n, pBC=pBC: e.activation(
                bcC[n][:], pBC[:], AF.Copy, bias=0.0, scale=1.0),
                reads=[pBC], writes=[bcC[n]])
        for dtile in range(16):
            py = ps[dtile % 2]
            ut = u_t[0]
            p.op("vector", lambda e, dtile=dtile, ut=ut: e.tensor_tensor(
                ut[:], del_f[dtile][:], xc_f[dtile][:], ALU.mult),
                reads=[del_f[dtile], xc_f[dtile]], writes=[ut])
            if dtile == 0:
                p.op("scalar", lambda e, dtile=dtile:
                     e.activation(da_s[0][:], del_f[dtile][:], AF.Exp,
                                  bias=0.0, scale=a_t[:, dtile * 16:dtile * 16 + 1]),
                     reads=[del_f[dtile], a_t], writes=[da_s[0]])
            for n in range(16):
                col = dtile * 16 + n
                pA = da_s[n % 2]
                db = dbu[n % 2]
                ht = h_t[n % 2]
                pb = pb_t[n % 2]
                pbb = pbb_t[n % 2]
                hcol = f"hst_{col}"
                ncol = col + 1
                if ncol < 256:
                    ndt, nn = ncol // 16, ncol % 16
                    p.op("scalar", lambda e, ndt=ndt, ncol=ncol, nn=nn:
                         e.activation(da_s[nn % 2][:], del_f[ndt][:], AF.Exp,
                                      bias=0.0, scale=a_t[:, ncol:ncol + 1]),
                         reads=[del_f[ndt], a_t], writes=[da_s[nn % 2]])
                p.op("gpsimd", lambda e, n=n, db=db, ut=ut: e.tensor_tensor(
                    db[:], ut[:], bcB[n][:], ALU.mult),
                    reads=[ut, bcB[n]], writes=[db])
                p.op("vector", lambda e, col=col, pA=pA, db=db, ht=ht:
                     e.tensor_tensor_scan(
                    ht[:], pA[:], db[:], h_state[:, col:col + 1],
                    ALU.mult, ALU.add),
                    reads=[pA, db, hcol], writes=[ht])
                p.op("scalar", lambda e, col=col, ht=ht: e.activation(
                    h_state[:, col:col + 1], ht[:, 511:512], AF.Copy,
                    bias=0.0, scale=1.0),
                    reads=[ht], writes=[hcol])
                p.op("vector", lambda e, n=n, pb=pb, ht=ht: e.tensor_tensor(
                    pb[:], ht[:], bcC[n][:], ALU.mult),
                    reads=[ht, bcC[n]], writes=[pb])
                p.op("scalar", lambda e, pb=pb, pbb=pbb: e.activation(
                    pbb[:], pb[:], AF.Copy, bias=0.0, scale=1.0),
                    reads=[pb], writes=[pbb])
                p.op("tensor", lambda e, n=n, py=py, pbb=pbb: e.matmul(
                    py[:], ident_b[:], pbb[:],
                    start=(n == 0), stop=(n == 15)),
                    reads=[ident_b, pbb], writes=[py])
            wt2 = w_t[0]
            szt = sz_t[0]
            p.op("vector", lambda e, dtile=dtile, py=py, wt2=wt2:
                 e.scalar_tensor_tensor(
                     wt2[:], xc_f[dtile][:], dvec[:, dtile:dtile + 1], py[:],
                     ALU.mult, ALU.add),
                 reads=[xc_f[dtile], dvec, py], writes=[wt2])
            zt = ztile[0]
            p.dma("sync", lambda e, dtile=dtile, sc=sc, zt=zt: e.dma_start(
                out=zt[:], in_=dram["sp_z"].ap()[dtile * 128:(dtile + 1) * 128,
                                                 t0:t0 + 512]),
                reads=[f"sp_z_{sc}_{dtile}"], writes=[zt], stream="ld")
            p.op("scalar", lambda e, zt=zt, szt=szt: e.activation(
                szt[:], zt[:], AF.Tanh, bias=0.0, scale=0.5),
                reads=[zt], writes=[szt])
            p.op("vector", lambda e, szt=szt: e.tensor_scalar(
                szt[:], szt[:], 0.5, 0.5, ALU.mult, ALU.add),
                reads=[szt], writes=[szt])
            p.op("vector", lambda e, zt=zt, szt=szt: e.tensor_tensor(
                szt[:], szt[:], zt[:], ALU.mult),
                reads=[szt, zt], writes=[szt])
            omt = omix[dtile % 2]
            p.op("vector", lambda e, omt=omt, wt2=wt2, szt=szt: e.tensor_tensor(
                omt[:], wt2[:], szt[:], ALU.mult),
                reads=[wt2, szt], writes=[omt])
            p.dma("sync", lambda e, dtile=dtile, sc=sc, omt=omt: e.dma_start(
                out=dram["omixd"].ap()[dtile * 128:(dtile + 1) * 128,
                                       t0:t0 + 512], in_=omt[:]),
                reads=[omt], writes=[f"omixd_{sc}_{dtile}"], stream="st")
        for half in range(2):
            for kt in range(16):
                wo = wout_s[kt % 2]
                omt = omix[kt % 2]
                p.dma("scalar", lambda e, kt=kt, wo=wo: e.dma_start(
                    out=wo[:], in_=woutb_d.ap()[kt * 128:(kt + 1) * 128, :]),
                    reads=[f"woutb_{kt}"], writes=[wo], stream="ld")
                p.dma("sync", lambda e, kt=kt, sc=sc, omt=omt: e.dma_start(
                    out=omt[:], in_=dram["omixd"].ap()[kt * 128:(kt + 1) * 128,
                                                       t0:t0 + 512]),
                    reads=[f"omixd_{sc}_{kt}"], writes=[omt], stream="ld")
                for t2 in range(2):
                    tb4 = half * 2 + t2
                    for dh in range(2):
                        po = ps[2 + t2 * 2 + dh]
                        p.op("tensor", lambda e, kt=kt, tb4=tb4, dh=dh, po=po,
                             wo=wo, omt=omt: e.matmul(
                                 po[:], omt[:, tb4 * 128:(tb4 + 1) * 128],
                                 wo[:, dh * 512:(dh + 1) * 512],
                                 start=(kt == 0), stop=(kt == 15)),
                             reads=[omt, wo], writes=[po])
            for t2 in range(2):
                tb4 = half * 2 + t2
                tb = sc * 4 + tb4
                x2r = xc2[0]
                p.dma("sync", lambda e, tb=tb, x2r=x2r: e.dma_start(
                    out=x2r[:], in_=x_src.ap()[tb * 128:(tb + 1) * 128, :]),
                    reads=[xsrc_r(tb)], writes=[x2r], stream="ld")
                for dh in range(2):
                    po = ps[2 + t2 * 2 + dh]
                    xot = xo_t[dh]
                    p.op("vector", lambda e, dh=dh, x2r=x2r, xot=xot, po=po:
                         e.scalar_tensor_tensor(
                             xot[:], po[:], 1.0,
                             x2r[:, dh * 512:(dh + 1) * 512],
                             ALU.mult, ALU.add),
                         reads=[po, x2r], writes=[xot])
                    p.dma("sync", lambda e, tb=tb, dh=dh, xot=xot: e.dma_start(
                        out=x_dst.ap()[tb * 128:(tb + 1) * 128,
                                       dh * 512:(dh + 1) * 512], in_=xot[:]),
                        reads=[xot], writes=[f"xdst_{x_dst.name}_{tb}"],
                        stream="st")


# version 05 (final)
# speedup vs baseline: 1.6055x; 1.0074x over previous
import numpy as np
import ml_dtypes

import concourse.bass as bass
import concourse.mybir as mybir
from concourse.bass_utils import run_bass_kernel_spmd

F32 = mybir.dt.float32
BF16 = mybir.dt.bfloat16
I32 = mybir.dt.int32
AF = mybir.ActivationFunctionType
ALU = mybir.AluOpType
AX = mybir.AxisListType

ENGINES = ("sync", "vector", "scalar", "tensor", "gpsimd")
NCORES = 8

D = 1024
DI = 2048
NS = 16
DTR = 64
IF = 3072
QMAX = 32767.0
SEM_MAX = 48000
NSEM = 3


class Prog:

    def __init__(self, nc):
        self.nc = nc
        self.ops = []
        self.last_writer = {}
        self.readers = {}
        self.deps = []
        self.barrier_deps = set()
        self._last_compute = {}
        self._last_dmas = {}
        self.fence_tile = None
        self._fence_n = 0

    def barrier(self):
        deps = set()
        for e, i in self._last_compute.items():
            deps.add(i)
        for e, lst in self._last_dmas.items():
            deps.update(lst)
        self.barrier_deps = deps
        if self.fence_tile is not None and deps:
            self._fence_n += 1
            ft = self.fence_tile
            i = self.op("vector", lambda e, ft=ft: e.memset(ft[:], 0.0),
                        reads=[], writes=[f"fence_tok_{self._fence_n}"])
            self.barrier_deps = {i}

    def _name(self, b):
        if isinstance(b, str):
            return b
        return b.tensor.name if isinstance(b, bass.AP) else b.name

    def _add(self, engine, fn, reads, writes, inc):
        i = len(self.ops)
        dep = set()
        reads = [self._name(b) for b in reads]
        writes = [self._name(b) for b in writes]
        for b in reads:
            w = self.last_writer.get(b)
            if w is not None:
                dep.add(w)
        for b in writes:
            w = self.last_writer.get(b)
            if w is not None:
                dep.add(w)
            dep.update(self.readers.get(b, ()))
        dep |= self.barrier_deps
        dep.discard(i)
        if inc == 16:
            self._last_dmas.setdefault(engine, [])
            self._last_dmas[engine].append(i)
            if len(self._last_dmas[engine]) > 8:
                self._last_dmas[engine].pop(0)
        else:
            self._last_compute[engine] = i
        self.ops.append(dict(engine=engine, fn=fn, inc=inc))
        self.deps.append(dep)
        for b in reads:
            self.readers.setdefault(b, []).append(i)
        for b in writes:
            self.last_writer[b] = i
            self.readers[b] = []
        return i

    def op(self, engine, fn, reads=(), writes=()):
        return self._add(engine, fn, reads, writes, 1)

    def dma(self, engine, fn, reads=(), writes=(), stream=None):
        i = self._add(engine, fn, reads, writes, 16)
        if stream is not None:
            self.ops[i]["stream"] = stream
        return i

    def emit(self, block, semlists):
        n = len(self.ops)
        slot_of = [0] * n
        val_of = [0] * n
        nslot = {e: len(semlists[e]) for e in ENGINES}
        slot_ct = {e: [0] * nslot[e] for e in ENGINES}
        dma_rr = {e: 0 for e in ENGINES}
        engct = {e: 0 for e in ENGINES}
        groups = {e: {"ld": [k for k in range(nslot[e]) if k < (nslot[e] + 1) // 2],
                      "st": [k for k in range(nslot[e]) if k >= (nslot[e] + 1) // 2]}
                  for e in ENGINES}
        gctr = {}
        for i, o in enumerate(self.ops):
            e = o["engine"]
            if o["inc"] == 16:
                st = o.get("stream")
                if st in ("ld", "st") and groups[e][st]:
                    lst = groups[e][st]
                    k = gctr.get((e, st), 0)
                    gctr[(e, st)] = k + 1
                    s = lst[k % len(lst)]
                else:
                    s = dma_rr[e] % nslot[e]
                dma_rr[e] += 1
            else:
                s = (engct[e] // SEM_MAX) % nslot[e]
            engct[e] += o["inc"]
            slot_ct[e][s] += o["inc"]
            slot_of[i] = s
            val_of[i] = slot_ct[e][s]
        slot_ops = {}
        for i, o in enumerate(self.ops):
            slot_ops.setdefault((o["engine"], slot_of[i]), []).append(i)

        import bisect

        def cum_before(key, i):
            lst = slot_ops.get(key)
            if not lst:
                return 0
            k = bisect.bisect_left(lst, i)
            return val_of[lst[k - 1]] if k else 0

        for ename in ENGINES:
            idxs = [i for i, o in enumerate(self.ops) if o["engine"] == ename]
            if not idxs:
                continue

            def make(ename, idxs):
                def section(eng):
                    water = {}
                    for i in idxs:
                        o = self.ops[i]
                        need = {}
                        if o["inc"] == 16:
                            key = (ename, slot_of[i])
                            v = cum_before(key, i)
                            if v:
                                need[key] = v
                        for j in self.deps[i]:
                            je = self.ops[j]["engine"]
                            if (je == ename == "tensor"
                                    and self.ops[j]["inc"] == 1):
                                continue
                            key = (je, slot_of[j])
                            if self.ops[j]["inc"] == 16:
                                v = cum_before(key, i)
                            else:
                                v = val_of[j]
                            if need.get(key, 0) < v:
                                need[key] = v
                        for (je, sl), v in sorted(need.items()):
                            if water.get((je, sl), 0) >= v:
                                continue
                            eng.wait_ge(semlists[je][sl], v)
                            water[(je, sl)] = v
                        o["fn"](eng).then_inc(
                            semlists[ename][slot_of[i]], o["inc"])
                return section

            getattr(block, ename)(make(ename, idxs))
        totals = {e: [0] * nslot[e] for e in ENGINES}
        for i, o in enumerate(self.ops):
            e = o["engine"]
            totals[e][slot_of[i]] = max(totals[e][slot_of[i]], val_of[i])
        return totals


class Ctx:
    def __init__(self, nc):
        self.nc = nc
        self.stack = [[]]
        self.uid = 0

    def push(self):
        self.stack.append([])

    def pop(self):
        for g in reversed(self.stack.pop()):
            g.__exit__(None, None, None)

    def _alloc(self, kind, name, shape, dt):
        self.uid += 1
        nm = f"{name}_{self.uid}"
        g = (self.nc.sbuf_tensor(nm, shape, dt) if kind == "s"
             else self.nc.psum_tensor(nm, shape, dt))
        t = g.__enter__()
        self.stack[-1].append(g)
        return t

    def sbuf(self, name, shape, dt=F32):
        return self._alloc("s", name, shape, dt)

    def psum(self, name, shape, dt=F32):
        return self._alloc("p", name, shape, dt)



def kernel(L0_attn_x, L0_attn_W_in, L0_attn_conv_w, L0_attn_conv_b,
           L0_attn_W_x, L0_attn_W_dt, L0_attn_b_dt, L0_attn_A_log,
           L0_attn_D, L0_attn_W_out, L0_ffn_w_gate, L0_ffn_w_up,
           L0_ffn_w_down):
    x = np.ascontiguousarray(np.asarray(L0_attn_x, dtype=np.float32))
    B, T, Dm = x.shape

    def cols(v):
        return np.ascontiguousarray(
            np.asarray(v, np.float32).reshape(DI // 128, 128).T)

    conv_w = np.asarray(L0_attn_conv_w, np.float32).reshape(16, 128, 4)
    conv_w_t = np.ascontiguousarray(conv_w.transpose(1, 0, 2).reshape(128, 64))
    a_log_t = np.ascontiguousarray(
        np.asarray(L0_attn_A_log, np.float32).reshape(16, 128, NS)
        .transpose(1, 0, 2).reshape(128, 16 * NS))

    common = {
        "W_in": np.ascontiguousarray(np.asarray(L0_attn_W_in, np.float32)),
        "W_x": np.ascontiguousarray(np.asarray(L0_attn_W_x, np.float32)),
        "W_dt": np.ascontiguousarray(np.asarray(L0_attn_W_dt, np.float32)),
        "W_out": np.ascontiguousarray(np.asarray(L0_attn_W_out, np.float32)),
        "w_gate": np.ascontiguousarray(np.asarray(L0_ffn_w_gate, np.float32)),
        "w_up": np.ascontiguousarray(np.asarray(L0_ffn_w_up, np.float32)),
        "w_down": np.ascontiguousarray(np.asarray(L0_ffn_w_down, np.float32)),
        "conv_wt": conv_w_t,
        "conv_b": cols(L0_attn_conv_b),
        "b_dt": cols(L0_attn_b_dt),
        "Dvec": cols(L0_attn_D),
        "A_log_t": a_log_t,
        "ident": np.eye(128, dtype=ml_dtypes.bfloat16),
        "ones_row": np.ones((1, 128), dtype=np.float32),
        "sel16": _sel16(),
    }

    if FUSED:
        nc = build_program(T, ncores=B)
        in_maps = []
        for c in range(B):
            m = dict(common)
            m["x_in"] = np.ascontiguousarray(x[c])
            in_maps.append(m)
        res = run_bass_kernel_spmd(nc, in_maps, core_ids=list(range(B)))
        out = np.stack([np.asarray(r["x_out"]) for r in res.results], axis=0)
        return out.astype(np.float32)
    return _kernel_staged(x, common, T, B)


_PROG_CACHE = {}


def _get_prog(T, B, stage):
    key = (T, B, stage)
    if key not in _PROG_CACHE:
        _PROG_CACHE[key] = build_program(T, ncores=B, stage=stage)
    return _PROG_CACHE[key]


def _kernel_staged(x, common, T, B):
    cores = list(range(B))

    def run(stage, per_core):
        nc = _get_prog(T, B, stage)
        return run_bass_kernel_spmd(nc, per_core, core_ids=cores).results

    base = {k: common[k] for k in ("ident", "ones_row")}

    def gather_mx(res):
        return np.ascontiguousarray(
            np.concatenate([np.asarray(r["mxout"]) for r in res], axis=0))

    def front(x_host, g1):
        r = run("A", [dict(base, x_in=x_host[c], g1_in=g1,
                           w_up=common["w_up"], w_gate=common["w_gate"])
                      for c in cores])
        g23 = gather_mx(r)
        sp_up = [np.asarray(r[c]["sp_up"]) for c in cores]
        sp_gate = [np.asarray(r[c]["sp_gate"]) for c in cores]
        r = run("B", [dict(base, sp_gate=sp_gate[c], g23_in=g23)
                      for c in cores])
        g4 = gather_mx(r)
        sp_P = [np.asarray(r[c]["sp_P"]) for c in cores]
        r = run("C", [dict(base, sp_up=sp_up[c], sp_P=sp_P[c], g23_in=g23,
                           g4_in=g4) for c in cores])
        g5 = gather_mx(r)
        sp_o = [np.asarray(r[c]["sp_o"]) for c in cores]
        return g23, g4, g5, sp_o

    x_host = [np.ascontiguousarray(x[c]) for c in range(B)]
    r = run("max", [dict(base, x_in=x_host[c]) for c in cores])
    g1 = gather_mx(r)
    g23, g4, g5, sp_o = front(x_host, g1)
    mkeys = ("W_in", "W_x", "W_dt", "W_out", "conv_wt", "conv_b", "b_dt",
             "Dvec", "A_log_t", "sel16")
    r = run("Dmm", [dict(base, x_in=x_host[c], sp_o=sp_o[c],
                         w_down=common["w_down"], g23_in=g23, g4_in=g4,
                         g5_in=g5, **{k: common[k] for k in mkeys})
                    for c in cores])
    x2 = [np.asarray(r[c]["x_out"]) for c in cores]
    g1b = gather_mx(r)
    g23b, g4b, g5b, sp_ob = front(x2, g1b)
    r = run("D", [dict(base, x_in=x2[c], sp_o=sp_ob[c],
                       w_down=common["w_down"], g23_in=g23b, g4_in=g4b,
                       g5_in=g5b) for c in cores])
    return np.stack([np.asarray(r[c]["x_out"]) for c in cores],
                    axis=0).astype(np.float32)


FUSED = False


def _sel16():
    sel = np.zeros((16, 16 * 128), dtype=ml_dtypes.bfloat16)
    for n in range(16):
        sel[n, n * 128:(n + 1) * 128] = 1
    return sel



STAGE_IO = {
    "max":   ({"x_in"}, {"mxout"}),
    "A":     ({"x_in", "g1_in", "w_up", "w_gate"},
              {"sp_up", "sp_gate", "mxout"}),
    "B":     ({"sp_gate", "g23_in"}, {"sp_P", "mxout"}),
    "C":     ({"sp_up", "sp_P", "g23_in", "g4_in"}, {"sp_o", "mxout"}),
    "D":     ({"x_in", "sp_o", "w_down", "g23_in", "g4_in", "g5_in"},
              {"x_out"}),
    "mamba": ({"x_in", "W_in", "W_x", "W_dt", "W_out", "conv_wt", "conv_b",
               "b_dt", "Dvec", "A_log_t", "sel16"}, {"x_out"}),
    "Dmm":   ({"x_in", "sp_o", "w_down", "g23_in", "g4_in", "g5_in",
               "W_in", "W_x", "W_dt", "W_out", "conv_wt", "conv_b",
               "b_dt", "Dvec", "A_log_t", "sel16"}, {"x_out", "mxout"}),
}


def build_program(T, mixer="mamba", stop_after=None, ncores=8, trunc=None,
                  stage=None):
    nc = bass.Bass("TRN2", target_bir_lowering=False)
    ctx = Ctx(nc)
    p = Prog(nc)
    p.fence_tile = ctx.sbuf("fencet", [1, 1])
    NT = T // 128
    NCH = T // 512

    dram = {}
    for nm, shape, dt, kind in [
        ("x_in", [T, D], F32, "ExternalInput"),
        ("x_out", [T, D], F32, "ExternalOutput"),
        ("W_in", [D, 2 * DI], F32, "ExternalInput"),
        ("W_x", [DI, DTR + 2 * NS], F32, "ExternalInput"),
        ("W_dt", [DTR, DI], F32, "ExternalInput"),
        ("W_out", [DI, D], F32, "ExternalInput"),
        ("w_gate", [IF, D], F32, "ExternalInput"),
        ("w_up", [IF, D], F32, "ExternalInput"),
        ("w_down", [D, IF], F32, "ExternalInput"),
        ("conv_wt", [128, 64], F32, "ExternalInput"),
        ("conv_b", [128, 16], F32, "ExternalInput"),
        ("b_dt", [128, 16], F32, "ExternalInput"),
        ("Dvec", [128, 16], F32, "ExternalInput"),
        ("A_log_t", [128, 16 * NS], F32, "ExternalInput"),
        ("ident", [128, 128], BF16, "ExternalInput"),
        ("ones_row", [1, 128], F32, "ExternalInput"),
        ("sel16", [16, 16 * 128], BF16, "ExternalInput"),
        ("xres", [T, D], F32, "Internal"),
        ("sp_up", [IF, T], F32, "Internal"),
        ("sp_gate", [IF, T], F32, "Internal"),
        ("sp_P", [IF, T], F32, "Internal"),
        ("sp_o", [IF, T], F32, "Internal"),
        ("sp_z", [DI, T], F32, "Internal"),
        ("wupT", [D, IF], BF16, "Internal"),
        ("wgateT", [D, IF], BF16, "Internal"),
        ("wdownT", [IF, D], BF16, "Internal"),
        ("winb", [D, 2 * DI], BF16, "Internal"),
        ("omixd", [DI, T], BF16, "Internal"),
        ("woutb", [DI, D], BF16, "Internal"),
        ("cc_in", [1, 2], F32, "Internal"),
        ("cc_out", [ncores, 2], F32, "Internal"),
        ("mxout", [1, 2], F32, "Internal"),
        ("g1_in", [ncores, 2], F32, "Internal"),
        ("g23_in", [ncores, 2], F32, "Internal"),
        ("g4_in", [ncores, 2], F32, "Internal"),
        ("g5_in", [ncores, 2], F32, "Internal"),
    ]:
        if stage is not None:
            ins_, outs_ = STAGE_IO[stage]
            base = {"ident", "ones_row"}
            if nm in ins_ or nm in base:
                kind = "ExternalInput"
            elif nm in outs_:
                kind = "ExternalOutput"
            else:
                kind = "Internal"
        dram[nm] = nc.dram_tensor(nm, shape, dt, kind=kind)

    ident_b = ctx.sbuf("identb", [128, 128], BF16)
    ones_f = ctx.sbuf("onesf", [1, 128], F32)
    p.dma("sync", lambda e: e.dma_start(out=ident_b[:], in_=dram["ident"].ap()),
          writes=[ident_b])
    p.dma("sync", lambda e: e.dma_start(out=ones_f[:], in_=dram["ones_row"].ap()),
          writes=[ones_f])
    stat_m = ctx.sbuf("stat_m", [128, NT])
    stat_r = ctx.sbuf("stat_r", [128, NT])
    eps_b = ctx.sbuf("eps_b", [128, 1])
    p.op("vector", lambda e: e.memset(eps_b[:], 1e-5), writes=[eps_b])

    H = dict(ctx=ctx, p=p, nc=nc, dram=dram, ident_b=ident_b, ones_f=ones_f,
             stat_m=stat_m, stat_r=stat_r, eps_b=eps_b, T=T, NT=NT, NCH=NCH,
             ncores=ncores, stage=stage)

    if stage is not None:
        if stage == "max":
            ffn_call(H, 0, x_src=dram["x_in"], x_dst=None, stages={"max"})
        elif stage in ("A", "B", "C", "D"):
            ctx.push()
            prep_ffn_weights(H, which={
                "A": ("wupT", "wgateT"), "B": (), "C": (),
                "D": ("wdownT",)}[stage])
            ctx.pop()
            p.barrier()
            ctx.push()
            ffn_call(H, 0, x_src=dram["x_in"], x_dst=dram["x_out"],
                     stages={stage})
            ctx.pop()
        elif stage == "mamba":
            ctx.push()
            mamba(H, x_src=dram["x_in"], x_dst=dram["x_out"])
            ctx.pop()
        elif stage == "Dmm":
            ctx.push()
            prep_ffn_weights(H, which=("wdownT",))
            ctx.pop()
            p.barrier()
            ctx.push()
            ffn_call(H, 0, x_src=dram["x_in"], x_dst=dram["xres"],
                     stages={"D"})
            ctx.pop()
            p.barrier()
            ctx.push()
            mamba(H, x_src=dram["xres"], x_dst=dram["x_out"])
            ctx.pop()
            p.barrier()
            ctx.push()
            ffn_call(H, 1, x_src=dram["x_out"], x_dst=None, stages={"max"})
            ctx.pop()
        return _finish(nc, ctx, p, trunc)

    ctx.push()
    prep_ffn_weights(H)
    ctx.pop()
    p.barrier()

    ctx.push()
    ffn_call(H, 0, x_src=dram["x_in"], x_dst=dram["xres"])
    ctx.pop()
    p.barrier()

    if stop_after != "ffn1":
        ctx.push()
        if mixer == "mamba":
            mamba(H, x_src=dram["xres"], x_dst=dram["xres"])
        else:
            mixer_copy(H, x_src=dram["xres"], x_dst=dram["xres"])
        ctx.pop()
        p.barrier()

        ctx.push()
        ffn_call(H, 1, x_src=dram["xres"], x_dst=dram["x_out"])
        ctx.pop()
    else:
        p.barrier()
        mixer_copy(H, x_src=dram["xres"], x_dst=dram["x_out"])

    return _finish(nc, ctx, p, trunc)


def _finish(nc, ctx, p, trunc=None):
    if trunc is not None:
        p.ops = p.ops[:trunc]
        p.deps = p.deps[:trunc]
    NSLOTS = {"sync": 6, "scalar": 6, "gpsimd": 4, "vector": 3, "tensor": 3}
    with nc.Block() as block:
        semctx = []
        semlists = {}
        for e in ENGINES:
            semlists[e] = []
            for k in range(NSLOTS[e]):
                g = nc.semaphore(f"s_{e}{k}")
                semlists[e].append(g.__enter__())
                semctx.append(g)
        totals = p.emit(block, semlists)

        @block.gpsimd
        def _(eng):
            for e in ENGINES:
                for k in range(NSLOTS[e]):
                    if totals[e][k]:
                        eng.wait_ge(semlists[e][k], totals[e][k])

        for g in reversed(semctx):
            g.__exit__(None, None, None)

    while len(ctx.stack) > 1:
        ctx.pop()
    for g in reversed(ctx.stack[0]):
        g.__exit__(None, None, None)
    ctx.stack[0] = []
    return nc



def round_half_up(H, y, i32, vf, gg, pre_fn, pre_reads):
    p = H["p"]
    p.op("vector", lambda e: pre_fn(e, y), reads=pre_reads, writes=[y])
    p.op("vector", lambda e: e.tensor_copy(i32[:], y[:]), reads=[y], writes=[i32])
    p.op("vector", lambda e: e.tensor_copy(vf[:], i32[:]), reads=[i32], writes=[vf])
    p.op("vector", lambda e: e.tensor_tensor(gg[:], vf[:], y[:], ALU.is_gt),
         reads=[vf, y], writes=[gg])
    p.op("vector", lambda e: e.tensor_tensor(vf[:], vf[:], gg[:], ALU.subtract),
         reads=[vf, gg], writes=[vf])


def scalar_bcast(H, tag, src_ap, src_buf, pbc, dst):
    p = H["p"]
    p.op("tensor", lambda e: e.matmul(pbc[:, 0:1], H["ones_f"][:], src_ap,
                                      start=True, stop=True),
         reads=[src_buf, H["ones_f"]], writes=[pbc])
    p.op("scalar", lambda e: e.activation(dst[:], pbc[:, 0:1], AF.Copy,
                                          bias=0.0, scale=1.0),
         reads=[pbc], writes=[dst])


def reduce_acc(H, tag, acc, nvals):
    ctx, p = H["ctx"], H["p"]
    row = ctx.sbuf(tag + "_row", [1, 256])
    mx = ctx.sbuf(tag + "_mx", [1, 2])
    for v in range(nvals):
        p.dma("sync", lambda e, v=v: e.dma_start(
            out=row[0:1, v * 128:(v + 1) * 128], in_=acc[:, v:v + 1]),
            reads=[acc], writes=[row])
    p.op("vector", lambda e: e.memset(mx[:], 0.0), writes=[mx])
    for v in range(nvals):
        p.op("vector", lambda e, v=v: e.tensor_reduce(
            mx[0:1, v:v + 1], row[0:1, v * 128:(v + 1) * 128], AX.X, ALU.max),
            reads=[row], writes=[mx])
    return mx


def export_mx(H, mx):
    p = H["p"]
    p.dma("sync", lambda e: e.dma_start(out=H["dram"]["mxout"].ap(), in_=mx[:]),
          reads=[mx], writes=["mxout"])


def consume_gmax(H, tag, name):
    ctx, p, dram = H["ctx"], H["p"], H["dram"]
    nr = H["ncores"]
    gt = ctx.sbuf(tag + "_g", [1, 16])
    p.dma("sync", lambda e: e.dma_start(
        out=gt[0:1, 0:2 * nr], in_=dram[name].ap()),
        reads=[name], writes=[gt])
    gm = ctx.sbuf(tag + "_gm", [1, 2])
    p.op("vector", lambda e: e.memset(gm[:], 0.0), writes=[gm])
    for v in range(2):
        p.op("vector", lambda e, v=v: e.tensor_reduce(
            gm[0:1, v:v + 1],
            bass.AP(gt, v, [[16, 1], [2, nr]]),
            AX.X, ALU.max),
            reads=[gt], writes=[gm])
    return gm


def global_max(H, tag, acc, nvals, pbc):
    ctx, p, dram = H["ctx"], H["p"], H["dram"]
    mx = reduce_acc(H, tag, acc, nvals)
    p.dma("sync", lambda e: e.dma_start(out=dram["cc_in"].ap(), in_=mx[:]),
          reads=[mx], writes=["cc_in"])
    p.dma("gpsimd", lambda e: e.collective_compute(
        "AllGather", ALU.bypass, [list(range(H["ncores"]))],
        [dram["cc_in"].ap()], [dram["cc_out"].ap()]),
        reads=["cc_in"], writes=["cc_out"])
    gt = ctx.sbuf(tag + "_g", [1, 16])
    nr0 = H["ncores"]
    p.dma("sync", lambda e: e.dma_start(
        out=gt[0:1, 0:2 * nr0], in_=dram["cc_out"].ap()[0:nr0, :]),
        reads=["cc_out"], writes=[gt])
    gm = ctx.sbuf(tag + "_gm", [1, 2])
    p.op("vector", lambda e: e.memset(gm[:], 0.0), writes=[gm])
    nr = H["ncores"]
    for v in range(nvals):
        p.op("vector", lambda e, v=v: e.tensor_reduce(
            gm[0:1, v:v + 1],
            bass.AP(gt, v, [[16, 1], [2, nr]]),
            AX.X, ALU.max),
            reads=[gt], writes=[gm])
    return gm


def quant_scales(H, tag, gm, col, pbc):
    ctx, p = H["ctx"], H["p"]
    sv = ctx.sbuf(tag + "_sv", [1, 1])
    rv = ctx.sbuf(tag + "_rv", [1, 1])
    p.op("vector", lambda e: e.tensor_scalar(
        sv[:], gm[0:1, col:col + 1], 1.0 / QMAX, 1e-12, ALU.mult, ALU.max),
        reads=[gm], writes=[sv])
    p.op("vector", lambda e: e.reciprocal(rv[:], sv[:]), reads=[sv], writes=[rv])
    sbc = ctx.sbuf(tag + "_sb", [128, 1])
    rbc = ctx.sbuf(tag + "_rb", [128, 1])
    scalar_bcast(H, tag + "s", sv[0:1, 0:1], sv, pbc, sbc)
    scalar_bcast(H, tag + "r", rv[0:1, 0:1], rv, pbc, rbc)
    return sbc, rbc, sv



def prep_ffn_weights(H, which=("wupT", "wgateT", "wdownT")):
    ctx, p, dram = H["ctx"], H["p"], H["dram"]
    ident_b = H["ident_b"]
    IMAX = IF
    wt = ctx.sbuf("pw_w", [128, IMAX])
    bm = ctx.sbuf("pw_bm", [128, IMAX // 32])
    rr = ctx.sbuf("pw_r", [128, IMAX // 32])
    y = ctx.sbuf("pw_y", [128, IMAX])
    i32 = ctx.sbuf("pw_i", [128, IMAX], I32)
    gg = ctx.sbuf("pw_g", [128, IMAX])
    q = ctx.sbuf("pw_q", [128, IMAX])
    wb = ctx.sbuf("pw_b", [128, IMAX], BF16)
    tp = [ctx.psum("pw_tp", [128, 128], BF16) for _ in range(2)]
    tps = [ctx.sbuf("pw_tps", [128, 128], BF16) for _ in range(2)]

    for (srcn, dstn, O, I) in (("w_up", "wupT", IF, D),
                               ("w_gate", "wgateT", IF, D),
                               ("w_down", "wdownT", D, IF)):
        if dstn not in which:
            continue
        src, dst = dram[srcn], dram[dstn]
        nb = I // 32
        for it in range(O // 128):
            p.dma("sync", lambda e, it=it, src=src, I=I: e.dma_start(
                out=wt[:, 0:I], in_=src.ap()[it * 128:(it + 1) * 128, :]),
                writes=[wt])
            p.op("vector", lambda e, I=I, nb=nb: e.tensor_reduce(
                bm[:, 0:nb], bass.AP(wt, 0, [[IMAX, 128], [32, nb], [1, 32]]),
                AX.X, ALU.max, apply_absolute_value=True),
                reads=[wt], writes=[bm])
            p.op("vector", lambda e, nb=nb: e.tensor_scalar(
                bm[:, 0:nb], bm[:, 0:nb], 1.0 / 7.0, 1e-12, ALU.mult, ALU.max),
                reads=[bm], writes=[bm])
            p.op("vector", lambda e, nb=nb: e.reciprocal(rr[:, 0:nb], bm[:, 0:nb]),
                 reads=[bm], writes=[rr])
            p.op("vector", lambda e, I=I, nb=nb: e.tensor_tensor(
                y[:, 0:I], wt[:, 0:I],
                bass.AP(rr, 0, [[IMAX // 32, 128], [1, nb], [0, 32]]),
                ALU.mult), reads=[wt, rr], writes=[y])
            p.op("vector", lambda e, I=I: e.tensor_scalar(
                y[:, 0:I], y[:, 0:I], 0.5, None, ALU.add),
                reads=[y], writes=[y])
            p.op("vector", lambda e, I=I: e.tensor_copy(i32[:, 0:I], y[:, 0:I]),
                 reads=[y], writes=[i32])
            p.op("vector", lambda e, I=I: e.tensor_copy(q[:, 0:I], i32[:, 0:I]),
                 reads=[i32], writes=[q])
            p.op("vector", lambda e, I=I: e.tensor_tensor(
                gg[:, 0:I], q[:, 0:I], y[:, 0:I], ALU.is_gt),
                reads=[q, y], writes=[gg])
            p.op("vector", lambda e, I=I: e.tensor_tensor(
                q[:, 0:I], q[:, 0:I], gg[:, 0:I], ALU.subtract),
                reads=[q, gg], writes=[q])
            p.op("vector", lambda e, I=I, nb=nb: e.tensor_tensor(
                wb[:, 0:I], q[:, 0:I],
                bass.AP(bm, 0, [[IMAX // 32, 128], [1, nb], [0, 32]]),
                ALU.mult), reads=[q, bm], writes=[wb])
            for jb in range(I // 128):
                tpx = tp[jb % 2]
                tsx = tps[jb % 2]
                p.op("tensor", lambda e, jb=jb, tpx=tpx: e.transpose(
                    tpx[:], wb[:, jb * 128:(jb + 1) * 128], ident_b[:]),
                    reads=[wb, ident_b], writes=[tpx])
                p.op("scalar", lambda e, tpx=tpx, tsx=tsx: e.activation(
                    tsx[:], tpx[:], AF.Copy, bias=0.0, scale=1.0),
                    reads=[tpx], writes=[tsx])
                p.dma("scalar", lambda e, tsx=tsx, dst=dst, it=it, jb=jb:
                      e.dma_start(out=dst.ap()[jb * 128:(jb + 1) * 128,
                                               it * 128:(it + 1) * 128],
                                  in_=tsx[:]),
                      reads=[tsx], writes=[f"{dstn}_{it}_{jb}"])



def ffn_call(H, call_id, x_src, x_dst, stages=("max", "A", "B", "C", "D")):
    ctx, p, dram = H["ctx"], H["p"], H["dram"]
    T, NT, NCH = H["T"], H["NT"], H["NCH"]
    stat_m, stat_r = H["stat_m"], H["stat_r"]
    ident_b = H["ident_b"]
    fused = H.get("stage") is None
    cid = f"f{call_id}"
    NIT = IF // 128
    sp_up, sp_gate = dram["sp_up"], dram["sp_gate"]
    sp_P, sp_o = dram["sp_P"], dram["sp_o"]

    def xsrc_r(tb):
        return f"xdst_{x_src.name}_{tb}"

    mmp = [ctx.psum(cid + "_mm", [128, 512]) for _ in range(2)]
    dps = [ctx.psum(cid + "_dp", [128, 512]) for _ in range(2)]
    tpp = [ctx.psum(cid + "_tp", [128, 128], BF16) for _ in range(2)]
    pbc = ctx.psum(cid + "_bc", [128, 8])

    xt = [ctx.sbuf(cid + "_mxx", [128, D]) for _ in range(2)]
    bn6 = ctx.sbuf(cid + "_bn6", [128, 12])
    bn2 = ctx.sbuf(cid + "_bn2", [128, 2])
    rmx = ctx.sbuf(cid + "_rmx", [128, 1])
    rmn = ctx.sbuf(cid + "_rmn", [128, 1])
    acc1 = ctx.sbuf(cid + "_acc1", [128, 2])
    vtmp = ctx.sbuf(cid + "_vt", [128, 1])
    if "max" in stages:
        p.op("vector", lambda e: e.memset(acc1[:], 0.0), writes=[acc1])
    for tb in (range(NT) if "max" in stages else ()):
        t = xt[tb % 2]
        p.dma("sync", lambda e, tb=tb, t=t: e.dma_start(
            out=t[:], in_=x_src.ap()[tb * 128:(tb + 1) * 128, :]),
            reads=[xsrc_r(tb)], writes=[t])
        p.op("vector", lambda e, t=t: e.bn_stats(bn6[:, 0:6], t[:, 0:512]),
             reads=[t], writes=[bn6])
        p.op("vector", lambda e, t=t: e.bn_stats(bn6[:, 6:12], t[:, 512:1024]),
             reads=[t], writes=[bn6])
        p.op("vector", lambda e: e.bn_aggr(bn2[:], bn6[:]),
             reads=[bn6], writes=[bn2])
        p.op("vector", lambda e, tb=tb: e.tensor_copy(
            stat_m[:, tb:tb + 1], bn2[:, 0:1]), reads=[bn2], writes=[stat_m])
        p.op("scalar", lambda e: e.activation(vtmp[:], bn2[:, 1:2], AF.Ln,
                                              bias=H["eps_b"][:], scale=1.0),
             reads=[bn2, H["eps_b"]], writes=[vtmp])
        p.op("scalar", lambda e, tb=tb: e.activation(
            stat_r[:, tb:tb + 1], vtmp[:], AF.Exp, bias=0.0, scale=-0.5),
            reads=[vtmp], writes=[stat_r])
        p.op("vector", lambda e, t=t: e.tensor_reduce(
            rmx[:], t[:], AX.X, ALU.max), reads=[t], writes=[rmx])
        p.op("vector", lambda e, t=t: e.tensor_reduce(
            rmn[:], t[:], AX.X, ALU.max, negate=True),
            reads=[t], writes=[rmn])
        p.op("vector", lambda e, tb=tb: e.tensor_tensor(
            rmx[:], rmx[:], stat_m[:, tb:tb + 1], ALU.subtract),
            reads=[rmx, stat_m], writes=[rmx])
        p.op("vector", lambda e, tb=tb: e.tensor_tensor(
            rmn[:], rmn[:], stat_m[:, tb:tb + 1], ALU.add),
            reads=[rmn, stat_m], writes=[rmn])
        p.op("vector", lambda e: e.tensor_tensor(
            rmx[:], rmx[:], rmn[:], ALU.max), reads=[rmx, rmn], writes=[rmx])
        p.op("vector", lambda e, tb=tb: e.tensor_tensor(
            rmx[:], rmx[:], stat_r[:, tb:tb + 1], ALU.mult),
            reads=[rmx, stat_r], writes=[rmx])
        p.op("vector", lambda e: e.tensor_tensor(
            acc1[:, 0:1], acc1[:, 0:1], rmx[:], ALU.max),
            reads=[acc1, rmx], writes=[acc1])

    if "max" in stages and not fused:
        export_mx(H, reduce_acc(H, cid + "g1", acc1, 1))
    if "A" in stages:
        if fused:
            gm1 = global_max(H, cid + "g1", acc1, 1, pbc)
        else:
            gm1 = consume_gmax(H, cid + "g1", "g1_in")
        s1, r1, _ = quant_scales(H, cid + "q1", gm1, 0, pbc)

    amax2 = ctx.sbuf(cid + "_am2", [128, 2])
    rtmp = ctx.sbuf(cid + "_rt", [128, 1])
    ctx.push()
    doA = "A" in stages
    wup_s = [ctx.sbuf(cid + "_wu", [128, IF], BF16) for _ in range(8)]
    wgate_s = [ctx.sbuf(cid + "_wg", [128, IF], BF16) for _ in range(8)]
    for kb in (range(8) if doA else ()):
        p.dma("sync", lambda e, kb=kb: e.dma_start(
            out=wup_s[kb][:], in_=dram["wupT"].ap()[kb * 128:(kb + 1) * 128, :]),
            reads=[f"wupT_{mi}_{kb}" for mi in range(NIT)],
            writes=[wup_s[kb], "wupg_all"], stream="ld")
        p.dma("sync", lambda e, kb=kb: e.dma_start(
            out=wgate_s[kb][:],
            in_=dram["wgateT"].ap()[kb * 128:(kb + 1) * 128, :]),
            reads=[f"wgateT_{mi}_{kb}" for mi in range(NIT)],
            writes=[wgate_s[kb], "wupg_all"], stream="ld")

    xqT2 = [[ctx.sbuf(cid + "_xqT", [128, 512], BF16) for _ in range(8)]
            for _ in range(2)]
    ya = ctx.sbuf(cid + "_Ay", [128, D])
    ia = ctx.sbuf(cid + "_Ai", [128, D], I32)
    ga = ctx.sbuf(cid + "_Ag", [128, D])
    ixf = ctx.sbuf(cid + "_ixf", [128, D])
    ixb2 = [ctx.sbuf(cid + "_ixb", [128, D], BF16) for _ in range(2)]
    kvec = ctx.sbuf(cid + "_kv", [128, 1])
    bvec = ctx.sbuf(cid + "_bv", [128, 1])
    ug = [ctx.sbuf(cid + "_ug", [128, 512]) for _ in range(4)]
    if doA:
        p.op("vector", lambda e: e.memset(amax2[:], 0.0), writes=[amax2])

    for ch in (range(NCH) if doA else ()):
        for tb4 in range(4):
            tb = ch * 4 + tb4
            t = xt[tb % 2]
            p.dma("sync", lambda e, tb=tb, t=t: e.dma_start(
                out=t[:], in_=x_src.ap()[tb * 128:(tb + 1) * 128, :]),
                reads=[xsrc_r(tb)], writes=[t], stream="ld")
            p.op("vector", lambda e, t=t: e.bn_stats(bn6[:, 0:6], t[:, 0:512]),
                 reads=[t], writes=[bn6])
            p.op("vector", lambda e, t=t: e.bn_stats(bn6[:, 6:12],
                                                     t[:, 512:1024]),
                 reads=[t], writes=[bn6])
            p.op("vector", lambda e: e.bn_aggr(bn2[:], bn6[:]),
                 reads=[bn6], writes=[bn2])
            p.op("scalar", lambda e: e.activation(
                vtmp[:], bn2[:, 1:2], AF.Ln, bias=H["eps_b"][:], scale=1.0),
                reads=[bn2, H["eps_b"]], writes=[vtmp])
            p.op("scalar", lambda e: e.activation(
                rmx[:], vtmp[:], AF.Exp, bias=0.0, scale=-0.5),
                reads=[vtmp], writes=[rmx])
            p.op("vector", lambda e: e.tensor_tensor(
                kvec[:], rmx[:], r1[:], ALU.mult),
                reads=[rmx, r1], writes=[kvec])
            p.op("vector", lambda e: e.tensor_tensor(
                bvec[:], bn2[:, 0:1], kvec[:], ALU.mult),
                reads=[bn2, kvec], writes=[bvec])
            p.op("vector", lambda e: e.tensor_scalar(
                bvec[:], bvec[:], -1.0, 0.5, ALU.mult, ALU.add),
                reads=[bvec], writes=[bvec])

            def pre(e, yy, t=t):
                return e.tensor_scalar(yy[:], t[:], kvec[:], bvec[:],
                                       ALU.mult, ALU.add)

            round_half_up(H, ya, ia, ixf, ga, pre, [t, kvec, bvec])
            ixb = ixb2[tb % 2]
            xqT = xqT2[ch % 2]
            p.op("vector", lambda e, ixb=ixb: e.tensor_copy(ixb[:], ixf[:]),
                 reads=[ixf], writes=[ixb])
            for db in range(8):
                tpx = tpp[db % 2]
                p.op("tensor", lambda e, db=db, tpx=tpx, ixb=ixb: e.transpose(
                    tpx[:], ixb[:, db * 128:(db + 1) * 128], ident_b[:]),
                    reads=[ixb, ident_b], writes=[tpx])
                p.op("scalar", lambda e, db=db, tb4=tb4, tpx=tpx, xqT=xqT:
                     e.activation(
                    xqT[db][:, tb4 * 128:(tb4 + 1) * 128], tpx[:], AF.Copy,
                    bias=0.0, scale=1.0),
                    reads=[tpx], writes=[xqT[db], f"xqTall{ch % 2}"])
        xqT = xqT2[ch % 2]
        for wi, (ws, spd) in enumerate(((wup_s, sp_up), (wgate_s, sp_gate))):
            for mi in range(NIT):
                pmm = mmp[mi % 2]
                for kb in range(8):
                    p.op("tensor", lambda e, wi=wi, mi=mi, kb=kb, pmm=pmm,
                         ws=ws, xqT=xqT: e.matmul(
                             pmm[:], ws[kb][:, mi * 128:(mi + 1) * 128],
                             xqT[kb][:], start=(kb == 0), stop=(kb == 7)),
                         reads=["wupg_all", f"xqTall{ch % 2}"], writes=[pmm])
                ut = ug[(wi * 2 + mi) % 4]
                p.op("scalar", lambda e, pmm=pmm, ut=ut: e.activation(
                    ut[:], pmm[:], AF.Copy, bias=0.0, scale=s1[:]),
                    reads=[pmm, s1], writes=[ut])
                p.op("vector", lambda e, ut=ut: e.tensor_reduce(
                    rtmp[:], ut[:], AX.X, ALU.max, apply_absolute_value=True),
                    reads=[ut], writes=[rtmp])
                p.op("vector", lambda e, wi=wi: e.tensor_tensor(
                    amax2[:, wi:wi + 1], amax2[:, wi:wi + 1], rtmp[:],
                    ALU.max), reads=[amax2, rtmp], writes=[amax2])
                p.dma("sync", lambda e, spd=spd, mi=mi, ch=ch, ut=ut:
                      e.dma_start(out=spd.ap()[mi * 128:(mi + 1) * 128,
                                               ch * 512:(ch + 1) * 512],
                                  in_=ut[:]),
                      reads=[ut], writes=[f"{spd.name}_{ch}_{mi}"],
                      stream="st")

    ctx.pop()
    if doA and not fused:
        export_mx(H, reduce_acc(H, cid + "g23", amax2, 2))
    doB = "B" in stages
    doC = "C" in stages
    doD = "D" in stages
    if fused:
        gm23 = global_max(H, cid + "g23", amax2, 2, pbc)
    elif doB or doC or doD:
        gm23 = consume_gmax(H, cid + "g23", "g23_in")
    if fused or doC or doD:
        s2, r2, s2v = quant_scales(H, cid + "q2", gm23, 0, pbc)
    if fused or doB or doC or doD:
        s3, r3, s3v = quant_scales(H, cid + "q3", gm23, 1, pbc)

    WBe = min(2048, T) if not fused else 512
    yb = [ctx.sbuf(cid + "_By", [128, WBe]) for _ in range(2)]
    ib = [ctx.sbuf(cid + "_Bi", [128, WBe], I32) for _ in range(2)]
    gb = [ctx.sbuf(cid + "_Bgg", [128, WBe]) for _ in range(2)]
    rtl = [ctx.sbuf(cid + "_rtl", [128, 1]) for _ in range(2)]
    amax4 = ctx.sbuf(cid + "_am4", [128, 2])
    ctx.push()
    gtl = [ctx.sbuf(cid + "_Bg", [128, WBe]) for _ in range(2)]
    i3f = [ctx.sbuf(cid + "_i3f", [128, WBe]) for _ in range(2)]
    sig = [ctx.sbuf(cid + "_sig", [128, WBe]) for _ in range(2)]
    jf = [ctx.sbuf(cid + "_jf", [128, WBe]) for _ in range(2)]
    ppt = [ctx.sbuf(cid + "_pp", [128, WBe]) for _ in range(2)]
    if doB:
        p.op("vector", lambda e: e.memset(amax4[:], 0.0), writes=[amax4])

    def rhu_split(par, src, src_buf, rbc, rbc_buf, y, i32, g, dst, eng2):
        p.op("vector", lambda e: e.tensor_scalar(
            y[:], src[:], rbc[:], 0.5, ALU.mult, ALU.add),
            reads=[src_buf, rbc_buf], writes=[y])
        p.op("scalar", lambda e: e.activation(i32[:], y[:], AF.Copy,
                                              bias=0.0, scale=1.0),
             reads=[y], writes=[i32])
        p.op("scalar", lambda e: e.activation(dst[:], i32[:], AF.Copy,
                                              bias=0.0, scale=1.0),
             reads=[i32], writes=[dst])
        p.op("vector", lambda e: e.tensor_tensor(g[:], dst[:], y[:],
                                                 ALU.is_gt),
             reads=[dst, y], writes=[g])
        p.op(eng2, lambda e: e.tensor_tensor(dst[:], dst[:], g[:],
                                             ALU.subtract),
             reads=[dst, g], writes=[dst])

    for cg in (range(T // WBe) if doB else ()):
        for mi in range(NIT):
            k = mi % 2
            gt = gtl[k]
            p.dma("sync", lambda e, mi=mi, cg=cg, gt=gt: e.dma_start(
                out=gt[:], in_=sp_gate.ap()[mi * 128:(mi + 1) * 128,
                                            cg * WBe:(cg + 1) * WBe]),
                reads=[f"sp_gate_{cc}_{mi}" for cc in range(
                    cg * (WBe // 512), (cg + 1) * (WBe // 512))],
                writes=[gt], stream="ld")
            rhu_split("B", gt, gt, r3, r3, yb[k], ib[k], gb[k], i3f[k],
                      "gpsimd")
            p.op("scalar", lambda e, k=k: e.activation(
                sig[k][:], i3f[k][:], AF.Sigmoid, bias=0.0, scale=s3[:]),
                reads=[i3f[k], s3], writes=[sig[k]])
            p.op("vector", lambda e, k=k: e.tensor_scalar(
                yb[k][:], sig[k][:], 65536.0, 0.5, ALU.mult, ALU.add),
                reads=[sig[k]], writes=[yb[k]])
            p.op("scalar", lambda e, k=k: e.activation(
                ib[k][:], yb[k][:], AF.Copy, bias=0.0, scale=1.0),
                reads=[yb[k]], writes=[ib[k]])
            p.op("scalar", lambda e, k=k: e.activation(
                jf[k][:], ib[k][:], AF.Copy, bias=0.0, scale=1.0),
                reads=[ib[k]], writes=[jf[k]])
            p.op("vector", lambda e, k=k: e.tensor_tensor(
                gb[k][:], jf[k][:], yb[k][:], ALU.is_gt),
                reads=[jf[k], yb[k]], writes=[gb[k]])
            p.op("gpsimd", lambda e, k=k: e.tensor_tensor(
                jf[k][:], jf[k][:], gb[k][:], ALU.subtract),
                reads=[jf[k], gb[k]], writes=[jf[k]])
            p.op("vector", lambda e, k=k: e.tensor_scalar(
                jf[k][:], jf[k][:], 65535.0, None, ALU.min),
                reads=[jf[k]], writes=[jf[k]])
            p.op("vector", lambda e, k=k: e.tensor_tensor(
                ppt[k][:], i3f[k][:], jf[k][:], ALU.mult),
                reads=[i3f[k], jf[k]], writes=[ppt[k]])
            p.op("vector", lambda e, k=k: e.tensor_reduce(
                rtl[k][:], ppt[k][:], AX.X, ALU.max,
                apply_absolute_value=True),
                reads=[ppt[k]], writes=[rtl[k]])
            p.op("vector", lambda e, k=k: e.tensor_tensor(
                amax4[:, 0:1], amax4[:, 0:1], rtl[k][:], ALU.max),
                reads=[amax4, rtl[k]], writes=[amax4])
            p.dma("sync", lambda e, mi=mi, cg=cg, k=k: e.dma_start(
                out=sp_P.ap()[mi * 128:(mi + 1) * 128,
                              cg * WBe:(cg + 1) * WBe], in_=ppt[k][:]),
                reads=[ppt[k]],
                writes=[f"sp_P_{cc}_{mi}" for cc in range(
                    cg * (WBe // 512), (cg + 1) * (WBe // 512))],
                stream="st")

    ctx.pop()
    p.barrier()
    if doB and not fused:
        export_mx(H, reduce_acc(H, cid + "g4", amax4, 1))
    if fused:
        gm4 = global_max(H, cid + "g4", amax4, 1, pbc)
    elif doC or doD:
        gm4 = consume_gmax(H, cid + "g4", "g4_in")
    if fused or doC or doD:
        s3f = ctx.sbuf(cid + "_s3f", [1, 1])
        s4v = ctx.sbuf(cid + "_s4v", [1, 1])
        c4v = ctx.sbuf(cid + "_c4v", [1, 1])
        p.op("vector", lambda e: e.tensor_scalar(
            s3f[:], s3v[:], 1.0 / 65536.0, None, ALU.mult),
            reads=[s3v], writes=[s3f])
        p.op("vector", lambda e: e.tensor_tensor(
            s4v[:], gm4[0:1, 0:1], s3f[:], ALU.mult),
            reads=[gm4, s3f], writes=[s4v])
        p.op("vector", lambda e: e.tensor_scalar(
            s4v[:], s4v[:], 1.0 / QMAX, 1e-12, ALU.mult, ALU.max),
            reads=[s4v], writes=[s4v])
        p.op("vector", lambda e: e.reciprocal(c4v[:], s4v[:]),
             reads=[s4v], writes=[c4v])
        p.op("vector", lambda e: e.tensor_tensor(
            c4v[:], c4v[:], s3f[:], ALU.mult), reads=[c4v, s3f], writes=[c4v])
        c4 = ctx.sbuf(cid + "_c4b", [128, 1])
        scalar_bcast(H, cid + "c4", c4v[0:1, 0:1], c4v, pbc, c4)

    ctx.push()
    upl = [ctx.sbuf(cid + "_Cu", [128, WBe]) for _ in range(2)]
    ptl = [ctx.sbuf(cid + "_Cp", [128, WBe]) for _ in range(2)]
    i4f = [ctx.sbuf(cid + "_i4f", [128, WBe]) for _ in range(2)]
    iuf = [ctx.sbuf(cid + "_iuf", [128, WBe]) for _ in range(2)]
    oo = [ctx.sbuf(cid + "_oo", [128, WBe]) for _ in range(2)]
    amax5 = ctx.sbuf(cid + "_am5", [128, 2])
    if doC:
        p.op("vector", lambda e: e.memset(amax5[:], 0.0), writes=[amax5])

    for cg in (range(T // WBe) if doC else ()):
        for mi in range(NIT):
            k = mi % 2
            ut = upl[k]
            pt = ptl[k]
            rg = list(range(cg * (WBe // 512), (cg + 1) * (WBe // 512)))
            p.dma("sync", lambda e, mi=mi, cg=cg, ut=ut: e.dma_start(
                out=ut[:], in_=sp_up.ap()[mi * 128:(mi + 1) * 128,
                                          cg * WBe:(cg + 1) * WBe]),
                reads=[f"sp_up_{cc}_{mi}" for cc in rg], writes=[ut],
                stream="ld")
            p.dma("scalar", lambda e, mi=mi, cg=cg, pt=pt: e.dma_start(
                out=pt[:], in_=sp_P.ap()[mi * 128:(mi + 1) * 128,
                                         cg * WBe:(cg + 1) * WBe]),
                reads=[f"sp_P_{cc}_{mi}" for cc in rg], writes=[pt])
            rhu_split("C4", pt, pt, c4, c4, yb[k], ib[k], gb[k], i4f[k],
                      "vector")
            rhu_split("C2", ut, ut, r2, r2, yb[k], ib[k], gb[k], iuf[k],
                      "gpsimd")
            p.op("vector", lambda e, k=k: e.tensor_tensor(
                oo[k][:], i4f[k][:], iuf[k][:], ALU.mult),
                reads=[i4f[k], iuf[k]], writes=[oo[k]])
            p.op("vector", lambda e, k=k: e.tensor_reduce(
                rtl[k][:], oo[k][:], AX.X, ALU.max, apply_absolute_value=True),
                reads=[oo[k]], writes=[rtl[k]])
            p.op("vector", lambda e, k=k: e.tensor_tensor(
                amax5[:, 0:1], amax5[:, 0:1], rtl[k][:], ALU.max),
                reads=[amax5, rtl[k]], writes=[amax5])
            p.dma("sync", lambda e, mi=mi, cg=cg, k=k: e.dma_start(
                out=sp_o.ap()[mi * 128:(mi + 1) * 128,
                              cg * WBe:(cg + 1) * WBe], in_=oo[k][:]),
                reads=[oo[k]], writes=[f"sp_o_{cc}_{mi}" for cc in rg],
                stream="st")

    ctx.pop()
    p.barrier()
    if doC and not fused:
        export_mx(H, reduce_acc(H, cid + "g5", amax5, 1))
    if fused:
        gm5 = global_max(H, cid + "g5", amax5, 1, pbc)
    elif doD:
        gm5 = consume_gmax(H, cid + "g5", "g5_in")
    if fused or doD:
        s42 = ctx.sbuf(cid + "_s42", [1, 1])
        s5v = ctx.sbuf(cid + "_s5v", [1, 1])
        c5v = ctx.sbuf(cid + "_c5v", [1, 1])
        sc5v = ctx.sbuf(cid + "_sc5v", [1, 1])
        p.op("vector", lambda e: e.tensor_tensor(
            s42[:], s4v[:], s2v[:], ALU.mult), reads=[s4v, s2v], writes=[s42])
        p.op("vector", lambda e: e.tensor_tensor(
            s5v[:], gm5[0:1, 0:1], s42[:], ALU.mult),
            reads=[gm5, s42], writes=[s5v])
        p.op("vector", lambda e: e.tensor_scalar(
            s5v[:], s5v[:], 1.0 / QMAX, 1e-12, ALU.mult, ALU.max),
            reads=[s5v], writes=[s5v])
        p.op("vector", lambda e: e.reciprocal(c5v[:], s5v[:]),
             reads=[s5v], writes=[c5v])
        p.op("vector", lambda e: e.tensor_tensor(
            c5v[:], c5v[:], s42[:], ALU.mult), reads=[c5v, s42], writes=[c5v])
        p.op("vector", lambda e: e.tensor_scalar(
            sc5v[:], s5v[:], 0.5, None, ALU.mult), reads=[s5v], writes=[sc5v])
        c5 = ctx.sbuf(cid + "_c5b", [128, 1])
        sc5 = ctx.sbuf(cid + "_sc5b", [128, 1])
        scalar_bcast(H, cid + "c5", c5v[0:1, 0:1], c5v, pbc, c5)
        scalar_bcast(H, cid + "sc5", sc5v[0:1, 0:1], sc5v, pbc, sc5)

    ctx.push()
    wd_s = [ctx.sbuf(cid + "_wd", [128, D], BF16) for _ in range(NIT)]
    for mi in (range(NIT) if doD else ()):
        p.dma("sync", lambda e, mi=mi: e.dma_start(
            out=wd_s[mi][:], in_=dram["wdownT"].ap()[mi * 128:(mi + 1) * 128, :]),
            reads=[f"wdownT_{it}_{mi}" for it in range(D // 128)],
            writes=[wd_s[mi], "wd_all"], stream="ld")
    otl = [ctx.sbuf(cid + "_Do", [128, 512]) for _ in range(2)]
    i5b = [ctx.sbuf(cid + "_i5b", [128, 512], BF16) for _ in range(NIT)]
    i5f = [ctx.sbuf(cid + "_i5f", [128, 512]) for _ in range(2)]
    xr = [ctx.sbuf(cid + "_Dx", [128, D]) for _ in range(2)]
    xo = [ctx.sbuf(cid + "_Dxo", [128, D]) for _ in range(2)]

    for ch in (range(NCH) if doD else ()):
        for mi in range(NIT):
            k = mi % 2
            ot = otl[k]
            p.dma("sync", lambda e, mi=mi, ch=ch, ot=ot: e.dma_start(
                out=ot[:], in_=sp_o.ap()[mi * 128:(mi + 1) * 128,
                                         ch * 512:(ch + 1) * 512]),
                reads=[f"sp_o_{ch}_{mi}"], writes=[ot], stream="ld")
            def prec5(e, yy, ot=ot):
                return e.tensor_scalar(yy[:], ot[:], c5[:], 0.5,
                                       ALU.mult, ALU.add)

            round_half_up(H, yb[k][:, 0:512], ib[k][:, 0:512], i5f[k],
                          gb[k][:, 0:512], prec5, [ot, c5])
            p.op("scalar", lambda e, mi=mi, k=k: e.activation(
                i5b[mi][:], i5f[k][:], AF.Copy, bias=0.0, scale=1.0),
                reads=[i5f[k]], writes=[i5b[mi], "i5b_all"])
        if True:
          chs = 0
          for tb4 in range(4):
            tb = ch * 4 + tb4
            xrt = xr[tb % 2]
            xot = xo[tb % 2]
            p.dma("sync", lambda e, tb=tb, xrt=xrt: e.dma_start(
                out=xrt[:], in_=x_src.ap()[tb * 128:(tb + 1) * 128, :]),
                reads=[xsrc_r(tb)], writes=[xrt], stream="ld")
            for dh in range(2):
                pd = dps[dh]
                for mi in range(NIT):
                    p.op("tensor", lambda e, mi=mi, tb4=tb4, dh=dh, pd=pd,
                         chs=chs: e.matmul(
                             pd[:],
                             i5b[mi][:, tb4 * 128:(tb4 + 1) * 128],
                             wd_s[mi][:, dh * 512:(dh + 1) * 512],
                             start=(mi == 0), stop=(mi == NIT - 1)),
                         reads=["i5b_all", "wd_all"], writes=[pd])
                p.op("vector", lambda e, dh=dh, xrt=xrt, xot=xot, pd=pd:
                     e.scalar_tensor_tensor(
                         xot[:, dh * 512:(dh + 1) * 512], pd[:], sc5[:],
                         xrt[:, dh * 512:(dh + 1) * 512], ALU.mult, ALU.add),
                     reads=[pd, sc5, xrt], writes=[xot])
            p.dma("sync", lambda e, tb=tb, xot=xot: e.dma_start(
                out=x_dst.ap()[tb * 128:(tb + 1) * 128, :], in_=xot[:]),
                reads=[xot], writes=[f"xdst_{x_dst.name}_{tb}"],
                stream="st")
    ctx.pop()



def mixer_copy(H, x_src, x_dst):
    ctx, p = H["ctx"], H["p"]
    T = H["T"]
    t = ctx.sbuf("mbcp", [128, D])
    for tb in range(T // 128):
        p.dma("sync", lambda e, tb=tb: e.dma_start(
            out=t[:], in_=x_src.ap()[tb * 128:(tb + 1) * 128, :]),
            reads=[f"xdst_{x_src.name}_{tb}"], writes=[t])
        p.dma("sync", lambda e, tb=tb: e.dma_start(
            out=x_dst.ap()[tb * 128:(tb + 1) * 128, :], in_=t[:]),
            reads=[t], writes=[f"xdst_{x_dst.name}_{tb}"])


def mamba(H, x_src, x_dst):
    ctx, p, dram = H["ctx"], H["p"], H["nc"] and H["dram"]
    p = H["p"]
    ctx = H["ctx"]
    T, NT = H["T"], H["NT"]
    NSC = T // 512
    ident_b = H["ident_b"]

    def xsrc_r(tb):
        return f"xdst_{x_src.name}_{tb}"

    ps = [ctx.psum("mb_ps", [128, 512]) for _ in range(6)]
    p96 = ctx.psum("mb_p96", [96, 512])
    tpp0 = ctx.psum("mb_tp", [128, 256], BF16)
    tpp = [tpp0[:, 0:128], tpp0[:, 128:256]]

    winb_d = dram["winb"]
    woutb_d = dram["woutb"]
    wx_b = ctx.sbuf("mb_wxb", [128, 16 * 96], BF16)
    wdt_b = ctx.sbuf("mb_wdtb", [64, DI], BF16)
    ctx.push()
    wtmp = ctx.sbuf("mb_wtmp", [128, 2 * DI])
    wtmpb = ctx.sbuf("mb_wtmpb", [128, 2 * DI], BF16)
    for kb in range(8):
        p.dma("sync", lambda e, kb=kb: e.dma_start(
            out=wtmp[:], in_=dram["W_in"].ap()[kb * 128:(kb + 1) * 128, :]),
            writes=[wtmp])
        p.op("vector", lambda e: e.tensor_copy(wtmpb[:], wtmp[:]),
             reads=[wtmp], writes=[wtmpb])
        p.dma("sync", lambda e, kb=kb: e.dma_start(
            out=winb_d.ap()[kb * 128:(kb + 1) * 128, :], in_=wtmpb[:]),
            reads=[wtmpb], writes=[f"winb_{kb}"])
    for kt in range(16):
        p.dma("sync", lambda e, kt=kt: e.dma_start(
            out=wtmp[:, 0:D], in_=dram["W_out"].ap()[kt * 128:(kt + 1) * 128, :]),
            writes=[wtmp])
        p.op("vector", lambda e: e.tensor_copy(wtmpb[:, 0:D], wtmp[:, 0:D]),
             reads=[wtmp], writes=[wtmpb])
        p.dma("sync", lambda e, kt=kt: e.dma_start(
            out=woutb_d.ap()[kt * 128:(kt + 1) * 128, :], in_=wtmpb[:, 0:D]),
            reads=[wtmpb], writes=[f"woutb_{kt}"])

    for kt in range(16):
        p.dma("sync", lambda e, kt=kt: e.dma_start(
            out=wtmp[:, 0:96], in_=dram["W_x"].ap()[kt * 128:(kt + 1) * 128, :]),
            writes=[wtmp])
        p.op("vector", lambda e, kt=kt: e.tensor_copy(
            wx_b[:, kt * 96:(kt + 1) * 96], wtmp[:, 0:96]),
            reads=[wtmp], writes=[wx_b])
    p.dma("sync", lambda e: e.dma_start(
        out=wtmp[0:64, 0:DI], in_=dram["W_dt"].ap()), writes=[wtmp])
    p.op("vector", lambda e: e.tensor_copy(wdt_b[:], wtmp[0:64, 0:DI]),
         reads=[wtmp], writes=[wdt_b])
    ctx.pop()
    p.barrier()
    win_c = [ctx.sbuf("mb_winc", [128, 8 * 128], BF16) for _ in range(2)]
    wout_s = [ctx.sbuf("mb_wouts", [128, D], BF16) for _ in range(2)]


    convw = ctx.sbuf("mb_convw", [128, 64])
    convb2 = ctx.sbuf("mb_convb2", [128, 16])
    bdt = ctx.sbuf("mb_bdt", [128, 16])
    dvec = ctx.sbuf("mb_dvec", [128, 16])
    a_t = ctx.sbuf("mb_at", [128, 256])
    p.dma("sync", lambda e: e.dma_start(out=convw[:], in_=dram["conv_wt"].ap()),
          writes=[convw])
    p.dma("sync", lambda e: e.dma_start(out=convb2[:], in_=dram["conv_b"].ap()),
          writes=[convb2])
    p.dma("sync", lambda e: e.dma_start(out=bdt[:], in_=dram["b_dt"].ap()),
          writes=[bdt])
    p.dma("sync", lambda e: e.dma_start(out=dvec[:], in_=dram["Dvec"].ap()),
          writes=[dvec])
    p.dma("sync", lambda e: e.dma_start(out=a_t[:], in_=dram["A_log_t"].ap()),
          writes=[a_t])
    p.op("scalar", lambda e: e.activation(a_t[:], a_t[:], AF.Exp,
                                          bias=0.0, scale=1.0),
         reads=[a_t], writes=[a_t])
    p.op("vector", lambda e: e.tensor_scalar(
        a_t[:], a_t[:], -1.0, None, ALU.mult), reads=[a_t], writes=[a_t])
    sel16 = ctx.sbuf("mb_sel", [16, 16 * 128], BF16)
    p.dma("sync", lambda e: e.dma_start(out=sel16[:], in_=dram["sel16"].ap()),
          writes=[sel16])

    h_state = ctx.sbuf("mb_h", [128, 256])
    xi_tail = ctx.sbuf("mb_tail", [128, 48])
    p.op("vector", lambda e: e.memset(h_state[:], 0.0),
         writes=[h_state] + [f"hst_{c}" for c in range(256)])
    p.op("vector", lambda e: e.memset(xi_tail[:], 0.0), writes=[xi_tail])

    xc2 = [ctx.sbuf("mb_x2", [128, D])]
    lnT = [ctx.sbuf("mb_lnT", [128, 512], BF16) for _ in range(8)]
    lnb = ctx.sbuf("mb_lnb", [128, D], BF16)
    mloc = ctx.sbuf("mb_m", [128, 1])
    rloc = ctx.sbuf("mb_r", [128, 1])
    bn6 = ctx.sbuf("mb_bn6", [128, 12])
    bn2 = ctx.sbuf("mb_bn2", [128, 2])
    vt = ctx.sbuf("mb_vt", [128, 1])
    xi = [ctx.sbuf("mb_xi", [128, 516]) for _ in range(2)]
    xc_f = [ctx.sbuf("mb_xc", [128, 512]) for _ in range(16)]
    xcb = [ctx.sbuf("mb_xcb", [128, 512], BF16) for _ in range(2)]
    del_f = [ctx.sbuf("mb_del", [128, 512]) for _ in range(16)]
    cvt = [ctx.sbuf("mb_cv", [128, 512]) for _ in range(2)]
    cv2 = [ctx.sbuf("mb_cv2", [128, 512]) for _ in range(2)]
    ztile = [ctx.sbuf("mb_zt", [128, 512])]
    xdbl = ctx.sbuf("mb_xdbl", [96, 512])
    dtb = ctx.sbuf("mb_dtb", [64, 512], BF16)
    bc32 = ctx.sbuf("mb_bc32", [32, 512])
    bhf32 = ctx.sbuf("mb_bhf32", [32, 512])
    bch32 = ctx.sbuf("mb_bch", [32, 512], BF16)
    bcl32 = ctx.sbuf("mb_bcl", [32, 512], BF16)
    bCh = ctx.sbuf("mb_bCh", [16, 512], BF16)
    bCl = ctx.sbuf("mb_bCl", [16, 512], BF16)
    u_t = [ctx.sbuf("mb_u", [128, 512])]
    da_s = [ctx.sbuf("mb_da", [128, 512]) for _ in range(2)]
    h_t = [ctx.sbuf("mb_ht", [128, 512]) for _ in range(2)]
    dbu = [ctx.sbuf("mb_dbu", [128, 512]) for _ in range(2)]
    pb_t = [ctx.sbuf("mb_pb", [128, 512]) for _ in range(2)]
    pbb_t = [ctx.sbuf("mb_pbb", [128, 512], BF16) for _ in range(2)]
    bcB = [ctx.sbuf("mb_bcB", [128, 512], BF16) for _ in range(16)]
    bcC = [ctx.sbuf("mb_bcC", [128, 512]) for _ in range(16)]
    w_t = [ctx.sbuf("mb_wt2", [128, 512])]
    sz_t = [ctx.sbuf("mb_sz", [128, 512])]
    omix = [ctx.sbuf("mb_om", [128, 512], BF16) for _ in range(2)]
    xo_t = [ctx.sbuf("mb_xo", [128, 512]) for _ in range(2)]

    for sc in range(NSC):
        t0 = sc * 512
        for tb4 in range(4):
            tb = sc * 4 + tb4
            xt2 = xc2[0]
            p.dma("sync", lambda e, tb=tb, xt2=xt2: e.dma_start(
                out=xt2[:], in_=x_src.ap()[tb * 128:(tb + 1) * 128, :]),
                reads=[xsrc_r(tb)], writes=[xt2], stream="ld")
            p.op("vector", lambda e, xt2=xt2: e.bn_stats(bn6[:, 0:6],
                                                         xt2[:, 0:512]),
                 reads=[xt2], writes=[bn6])
            p.op("vector", lambda e, xt2=xt2: e.bn_stats(bn6[:, 6:12],
                                                         xt2[:, 512:1024]),
                 reads=[xt2], writes=[bn6])
            p.op("vector", lambda e: e.bn_aggr(bn2[:], bn6[:]),
                 reads=[bn6], writes=[bn2])
            p.op("scalar", lambda e: e.activation(vt[:], bn2[:, 1:2], AF.Ln,
                                                  bias=H["eps_b"][:], scale=1.0),
                 reads=[bn2, H["eps_b"]], writes=[vt])
            p.op("scalar", lambda e: e.activation(rloc[:], vt[:], AF.Exp,
                                                  bias=0.0, scale=-0.5),
                 reads=[vt], writes=[rloc])
            p.op("vector", lambda e: e.tensor_tensor(
                mloc[:], bn2[:, 0:1], rloc[:], ALU.mult),
                reads=[bn2, rloc], writes=[mloc])
            p.op("vector", lambda e: e.tensor_scalar(
                mloc[:], mloc[:], -1.0, None, ALU.mult),
                reads=[mloc], writes=[mloc])
            p.op("vector", lambda e, xt2=xt2: e.tensor_scalar(
                lnb[:], xt2[:], rloc[:], mloc[:], ALU.mult, ALU.add),
                reads=[xt2, rloc, mloc], writes=[lnb])
            for db in range(8):
                tpx = tpp[db % 2]
                p.op("tensor", lambda e, db=db, tpx=tpx: e.transpose(
                    tpx[:], lnb[:, db * 128:(db + 1) * 128], ident_b[:]),
                    reads=[lnb, ident_b], writes=[tpx])
                p.op("scalar", lambda e, db=db, tb4=tb4, tpx=tpx: e.activation(
                    lnT[db][:, tb4 * 128:(tb4 + 1) * 128], tpx[:], AF.Copy,
                    bias=0.0, scale=1.0),
                    reads=[tpx], writes=[lnT[db], "lnT_all"])
        for mi in range(32):
            pmm = ps[mi % 2]
            wc = win_c[mi % 2]
            p.dma("sync", lambda e, mi=mi, wc=wc: e.dma_start(
                out=wc[:],
                in_=bass.AP(winb_d, mi * 128,
                            [[2 * DI, 128], [128 * 2 * DI, 8], [1, 128]])),
                reads=[f"winb_{kb}" for kb in range(8)], writes=[wc],
                stream="ld")
            for kb in range(8):
                p.op("tensor", lambda e, mi=mi, kb=kb, pmm=pmm, wc=wc:
                     e.matmul(pmm[:], wc[:, kb * 128:(kb + 1) * 128],
                              lnT[kb][:], start=(kb == 0), stop=(kb == 7)),
                     reads=[wc, "lnT_all"], writes=[pmm])
            if mi < 16:
                dtile = mi
                xit = xi[dtile % 2]
                p.op("scalar", lambda e, pmm=pmm, xit=xit: e.activation(
                    xit[:, 3:515], pmm[:], AF.Copy, bias=0.0, scale=1.0),
                    reads=[pmm], writes=[xit])
                p.op("vector", lambda e, dtile=dtile, xit=xit: e.tensor_copy(
                    xit[:, 0:3], xi_tail[:, dtile * 3:dtile * 3 + 3]),
                    reads=[xi_tail], writes=[xit])
                p.op("vector", lambda e, dtile=dtile, xit=xit: e.tensor_copy(
                    xi_tail[:, dtile * 3:dtile * 3 + 3], xit[:, 512:515]),
                    reads=[xit], writes=[xi_tail])
                cva = cvt[dtile % 2]
                cvb = cv2[dtile % 2]
                p.op("vector", lambda e, dtile=dtile, xit=xit, cva=cva:
                     e.tensor_scalar(
                    cva[:], xit[:, 0:512], convw[:, dtile * 4:dtile * 4 + 1],
                    None, ALU.mult),
                    reads=[xit, convw], writes=[cva])
                for k in range(1, 4):
                    p.op("vector", lambda e, dtile=dtile, k=k, xit=xit,
                         cva=cva: e.scalar_tensor_tensor(
                             cva[:], xit[:, k:k + 512],
                             convw[:, dtile * 4 + k:dtile * 4 + k + 1],
                             cva[:], ALU.mult, ALU.add),
                         reads=[xit, convw, cva], writes=[cva])
                p.op("vector", lambda e, dtile=dtile, cva=cva: e.tensor_scalar(
                    cva[:], cva[:], convb2[:, dtile:dtile + 1], None, ALU.add),
                    reads=[cva, convb2], writes=[cva])
                p.op("scalar", lambda e, cva=cva, cvb=cvb: e.activation(
                    cvb[:], cva[:], AF.Tanh, bias=0.0, scale=0.5),
                     reads=[cva], writes=[cvb])
                p.op("vector", lambda e, cvb=cvb: e.tensor_scalar(
                    cvb[:], cvb[:], 0.5, 0.5, ALU.mult, ALU.add),
                    reads=[cvb], writes=[cvb])
                p.op("vector", lambda e, dtile=dtile, cva=cva, cvb=cvb:
                     e.tensor_tensor(
                    xc_f[dtile][:], cva[:], cvb[:], ALU.mult),
                    reads=[cva, cvb], writes=[xc_f[dtile]])
                xcr = xcb[dtile % 2]
                p.op("vector", lambda e, dtile=dtile, xcr=xcr: e.tensor_copy(
                    xcr[:], xc_f[dtile][:]),
                    reads=[xc_f[dtile]], writes=[xcr])
                p.op("tensor", lambda e, dtile=dtile, xcr=xcr: e.matmul(
                    p96[:], wx_b[:, dtile * 96:(dtile + 1) * 96], xcr[:],
                    start=(dtile == 0), stop=(dtile == 15)),
                    reads=[wx_b, xcr], writes=[p96])
            else:
                dtile = mi - 16
                zt = ztile[0]
                p.op("scalar", lambda e, pmm=pmm, zt=zt: e.activation(
                    zt[:], pmm[:], AF.Copy, bias=0.0, scale=1.0),
                    reads=[pmm], writes=[zt])
                p.dma("sync", lambda e, dtile=dtile, sc=sc, zt=zt: e.dma_start(
                    out=dram["sp_z"].ap()[dtile * 128:(dtile + 1) * 128,
                                          t0:t0 + 512], in_=zt[:]),
                    reads=[zt], writes=[f"sp_z_{sc}_{dtile}"], stream="st")
        p.op("scalar", lambda e: e.activation(xdbl[:], p96[:], AF.Copy,
                                              bias=0.0, scale=1.0),
             reads=[p96], writes=[xdbl])
        p.op("vector", lambda e: e.tensor_copy(dtb[:], xdbl[0:64, :]),
             reads=[xdbl], writes=[dtb])
        for mt in range(16):
            pmd = ps[2 + (mt % 2)]
            p.op("tensor", lambda e, mt=mt, pmd=pmd: e.matmul(
                pmd[:], wdt_b[:, mt * 128:(mt + 1) * 128], dtb[:],
                start=True, stop=True),
                reads=[wdt_b, dtb], writes=[pmd])
            cva = cvt[mt % 2]
            cvb = cv2[mt % 2]
            p.op("vector", lambda e, mt=mt, pmd=pmd, cva=cva: e.tensor_scalar(
                cva[:], pmd[:], bdt[:, mt:mt + 1], None, ALU.add),
                reads=[pmd, bdt], writes=[cva])
            p.op("scalar", lambda e, cva=cva, cvb=cvb: e.activation(
                cvb[:], cva[:], AF.Abs, bias=0.0, scale=1.0),
                 reads=[cva], writes=[cvb])
            p.op("scalar", lambda e, cvb=cvb: e.activation(
                cvb[:], cvb[:], AF.Exp, bias=0.0, scale=-1.0),
                 reads=[cvb], writes=[cvb])
            p.op("scalar", lambda e, cvb=cvb: e.activation(
                cvb[:], cvb[:], AF.Ln, bias=1.0, scale=1.0),
                 reads=[cvb], writes=[cvb])
            p.op("vector", lambda e, cva=cva: e.tensor_scalar(
                cva[:], cva[:], 0.0, None, ALU.max), reads=[cva], writes=[cva])
            p.op("vector", lambda e, mt=mt, cva=cva, cvb=cvb: e.tensor_tensor(
                del_f[mt][:], cva[:], cvb[:], ALU.add),
                reads=[cva, cvb], writes=[del_f[mt]])
        p.op("scalar", lambda e: e.activation(bc32[:], p96[64:96, :],
                                              AF.Copy, bias=0.0, scale=1.0),
             reads=[p96], writes=[bc32])
        p.op("vector", lambda e: e.tensor_copy(bch32[:], bc32[:]),
             reads=[bc32], writes=[bch32])
        p.op("vector", lambda e: e.tensor_copy(bhf32[:], bch32[:]),
             reads=[bch32], writes=[bhf32])
        p.op("vector", lambda e: e.tensor_tensor(
            bhf32[:], bc32[:], bhf32[:], ALU.subtract), reads=[bc32, bhf32],
            writes=[bhf32])
        p.op("vector", lambda e: e.tensor_copy(bcl32[:], bhf32[:]),
             reads=[bhf32], writes=[bcl32])
        p.dma("scalar", lambda e: e.dma_start(out=bCh[:], in_=bch32[16:32, :]),
              reads=[bch32], writes=[bCh])
        p.dma("scalar", lambda e: e.dma_start(out=bCl[:], in_=bcl32[16:32, :]),
              reads=[bcl32], writes=[bCl])
        for n in range(16):
            pBC = ps[4 + (n % 2)]
            p.op("tensor", lambda e, n=n, pBC=pBC: e.matmul(
                pBC[:], sel16[:, n * 128:(n + 1) * 128], bch32[0:16, :],
                start=True, stop=False),
                reads=[sel16, bch32], writes=[pBC])
            p.op("tensor", lambda e, n=n, pBC=pBC: e.matmul(
                pBC[:], sel16[:, n * 128:(n + 1) * 128], bcl32[0:16, :],
                start=False, stop=True),
                reads=[sel16, bcl32], writes=[pBC])
            p.op("scalar", lambda e, n=n, pBC=pBC: e.activation(
                bcB[n][:], pBC[:], AF.Copy, bias=0.0, scale=1.0),
                reads=[pBC], writes=[bcB[n]])
        for n in range(16):
            pBC = ps[4 + (n % 2)]
            p.op("tensor", lambda e, n=n, pBC=pBC: e.matmul(
                pBC[:], sel16[:, n * 128:(n + 1) * 128], bCh[:],
                start=True, stop=False),
                reads=[sel16, bCh], writes=[pBC])
            p.op("tensor", lambda e, n=n, pBC=pBC: e.matmul(
                pBC[:], sel16[:, n * 128:(n + 1) * 128], bCl[:],
                start=False, stop=True),
                reads=[sel16, bCl], writes=[pBC])
            p.op("scalar", lambda e, n=n, pBC=pBC: e.activation(
                bcC[n][:], pBC[:], AF.Copy, bias=0.0, scale=1.0),
                reads=[pBC], writes=[bcC[n]])
        for dtile in range(16):
            py = ps[dtile % 2]
            ut = u_t[0]
            p.op("vector", lambda e, dtile=dtile, ut=ut: e.tensor_tensor(
                ut[:], del_f[dtile][:], xc_f[dtile][:], ALU.mult),
                reads=[del_f[dtile], xc_f[dtile]], writes=[ut])
            if dtile == 0:
                p.op("scalar", lambda e, dtile=dtile:
                     e.activation(da_s[0][:], del_f[dtile][:], AF.Exp,
                                  bias=0.0, scale=a_t[:, dtile * 16:dtile * 16 + 1]),
                     reads=[del_f[dtile], a_t], writes=[da_s[0]])
            for n in range(16):
                col = dtile * 16 + n
                pA = da_s[n % 2]
                db = dbu[n % 2]
                ht = h_t[n % 2]
                pb = pb_t[n % 2]
                pbb = pbb_t[n % 2]
                hcol = f"hst_{col}"
                ncol = col + 1
                if ncol < 256:
                    ndt, nn = ncol // 16, ncol % 16
                    p.op("scalar", lambda e, ndt=ndt, ncol=ncol, nn=nn:
                         e.activation(da_s[nn % 2][:], del_f[ndt][:], AF.Exp,
                                      bias=0.0, scale=a_t[:, ncol:ncol + 1]),
                         reads=[del_f[ndt], a_t], writes=[da_s[nn % 2]])
                p.op("gpsimd", lambda e, n=n, db=db, ut=ut: e.tensor_tensor(
                    db[:], ut[:], bcB[n][:], ALU.mult),
                    reads=[ut, bcB[n]], writes=[db])
                p.op("vector", lambda e, col=col, pA=pA, db=db, ht=ht:
                     e.tensor_tensor_scan(
                    ht[:], pA[:], db[:], h_state[:, col:col + 1],
                    ALU.mult, ALU.add),
                    reads=[pA, db, hcol], writes=[ht])
                p.op("scalar", lambda e, col=col, ht=ht: e.activation(
                    h_state[:, col:col + 1], ht[:, 511:512], AF.Copy,
                    bias=0.0, scale=1.0),
                    reads=[ht], writes=[hcol])
                p.op("vector", lambda e, n=n, pb=pb, ht=ht: e.tensor_tensor(
                    pb[:], ht[:], bcC[n][:], ALU.mult),
                    reads=[ht, bcC[n]], writes=[pb])
                p.op("scalar", lambda e, pb=pb, pbb=pbb: e.activation(
                    pbb[:], pb[:], AF.Copy, bias=0.0, scale=1.0),
                    reads=[pb], writes=[pbb])
                p.op("tensor", lambda e, n=n, py=py, pbb=pbb: e.matmul(
                    py[:], ident_b[:], pbb[:],
                    start=(n == 0), stop=(n == 15)),
                    reads=[ident_b, pbb], writes=[py])
            wt2 = w_t[0]
            szt = sz_t[0]
            p.op("vector", lambda e, dtile=dtile, py=py, wt2=wt2:
                 e.scalar_tensor_tensor(
                     wt2[:], xc_f[dtile][:], dvec[:, dtile:dtile + 1], py[:],
                     ALU.mult, ALU.add),
                 reads=[xc_f[dtile], dvec, py], writes=[wt2])
            zt = ztile[0]
            p.dma("sync", lambda e, dtile=dtile, sc=sc, zt=zt: e.dma_start(
                out=zt[:], in_=dram["sp_z"].ap()[dtile * 128:(dtile + 1) * 128,
                                                 t0:t0 + 512]),
                reads=[f"sp_z_{sc}_{dtile}"], writes=[zt], stream="ld")
            p.op("scalar", lambda e, zt=zt, szt=szt: e.activation(
                szt[:], zt[:], AF.Tanh, bias=0.0, scale=0.5),
                reads=[zt], writes=[szt])
            p.op("vector", lambda e, szt=szt: e.tensor_scalar(
                szt[:], szt[:], 0.5, 0.5, ALU.mult, ALU.add),
                reads=[szt], writes=[szt])
            p.op("vector", lambda e, zt=zt, szt=szt: e.tensor_tensor(
                szt[:], szt[:], zt[:], ALU.mult),
                reads=[szt, zt], writes=[szt])
            omt = omix[dtile % 2]
            p.op("vector", lambda e, omt=omt, wt2=wt2, szt=szt: e.tensor_tensor(
                omt[:], wt2[:], szt[:], ALU.mult),
                reads=[wt2, szt], writes=[omt])
            p.dma("sync", lambda e, dtile=dtile, sc=sc, omt=omt: e.dma_start(
                out=dram["omixd"].ap()[dtile * 128:(dtile + 1) * 128,
                                       t0:t0 + 512], in_=omt[:]),
                reads=[omt], writes=[f"omixd_{sc}_{dtile}"], stream="st")
        for half in range(2):
            for kt in range(16):
                wo = wout_s[kt % 2]
                omt = omix[kt % 2]
                p.dma("scalar", lambda e, kt=kt, wo=wo: e.dma_start(
                    out=wo[:], in_=woutb_d.ap()[kt * 128:(kt + 1) * 128, :]),
                    reads=[f"woutb_{kt}"], writes=[wo], stream="ld")
                p.dma("sync", lambda e, kt=kt, sc=sc, omt=omt: e.dma_start(
                    out=omt[:], in_=dram["omixd"].ap()[kt * 128:(kt + 1) * 128,
                                                       t0:t0 + 512]),
                    reads=[f"omixd_{sc}_{kt}"], writes=[omt], stream="ld")
                for t2 in range(2):
                    tb4 = half * 2 + t2
                    for dh in range(2):
                        po = ps[2 + t2 * 2 + dh]
                        p.op("tensor", lambda e, kt=kt, tb4=tb4, dh=dh, po=po,
                             wo=wo, omt=omt: e.matmul(
                                 po[:], omt[:, tb4 * 128:(tb4 + 1) * 128],
                                 wo[:, dh * 512:(dh + 1) * 512],
                                 start=(kt == 0), stop=(kt == 15)),
                             reads=[omt, wo], writes=[po])
            for t2 in range(2):
                tb4 = half * 2 + t2
                tb = sc * 4 + tb4
                x2r = xc2[0]
                p.dma("sync", lambda e, tb=tb, x2r=x2r: e.dma_start(
                    out=x2r[:], in_=x_src.ap()[tb * 128:(tb + 1) * 128, :]),
                    reads=[xsrc_r(tb)], writes=[x2r], stream="ld")
                for dh in range(2):
                    po = ps[2 + t2 * 2 + dh]
                    xot = xo_t[dh]
                    p.op("vector", lambda e, dh=dh, x2r=x2r, xot=xot, po=po:
                         e.scalar_tensor_tensor(
                             xot[:], po[:], 1.0,
                             x2r[:, dh * 512:(dh + 1) * 512],
                             ALU.mult, ALU.add),
                         reads=[po, x2r], writes=[xot])
                    p.dma("sync", lambda e, tb=tb, dh=dh, xot=xot: e.dma_start(
                        out=x_dst.ap()[tb * 128:(tb + 1) * 128,
                                       dh * 512:(dh + 1) * 512], in_=xot[:]),
                        reads=[xot], writes=[f"xdst_{x_dst.name}_{tb}"],
                        stream="st")
